# version 34
# speedup vs baseline: 1.0488x; 1.0064x over previous
import math
from contextlib import ExitStack

import numpy as np
import ml_dtypes

import concourse.bass as bass
import concourse.mybir as mybir
from concourse.bass_utils import run_bass_kernel_spmd
from concourse.ap import AP as _AP

F32 = mybir.dt.float32
BF16 = mybir.dt.bfloat16
AF = mybir.ActivationFunctionType
ALU = mybir.AluOpType
AX = mybir.AxisListType.X
BF = ml_dtypes.bfloat16

H, DH, D, B, N = 16, 128, 2048, 2, 8192
NTOK = B * N
TPC = NTOK // 8
CH = 128
NCH = TPC // CH
DC = D // 128
EPS = 1e-5



def _slopes(n_heads):
    start = 2.0 ** (-2.0 ** (-(math.log2(n_heads) - 3)))
    return np.array([start ** (i + 1) for i in range(n_heads)], dtype=np.float64)


LAM64 = np.exp(-_slopes(H))
LAM = LAM64.astype(np.float32)
LAMC = (LAM.astype(np.float64) ** CH)

_pos = np.arange(CH, dtype=np.float64)
DECAY_Q_T = np.broadcast_to(
    (LAM64[:, None] ** (_pos + 1.0))[:, None, :], (H, 128, CH)
).astype(BF)
DECAY_K_COL = (LAM64[None, :] ** (CH - 1.0 - _pos[:, None])).astype(np.float32)
_diff = _pos[None, :] - _pos[:, None]
DMASKT_T = np.where(
    _diff >= 0, LAM64[:, None, None] ** np.maximum(_diff, 0.0)[None], 0.0
).astype(np.float32)

IDENT_B16 = np.eye(128, dtype=BF)
IDENT_F32 = np.eye(128, dtype=np.float32)
ONES_COL_B16 = np.ones((128, 1), dtype=BF)
ONES_ROW_F32 = np.ones((1, 128), dtype=np.float32)

_L1M = np.zeros((4, 128, 512), dtype=np.float64)
for _j in range(4):
    for _i in range(4):
        blk = _L1M[_j][:, _i * 128:(_i + 1) * 128]
        if _i > _j:
            blk[:] = 1.0
        elif _i == _j:
            kk = np.arange(128)[:, None]
            qq = np.arange(128)[None, :]
            blk[:] = (kk <= qq)
L1_DIAG_MASKS = _L1M.astype(BF)

INV_SQRT_DH = 1.0 / math.sqrt(DH)


def _sinit_coeff_table(crank):
    tab = np.zeros((64,), dtype=np.float64)
    for h in range(H):
        for r in range(4):
            if r < crank:
                tab[h * 4 + r] = LAMC[h] ** (16 * (crank - 1 - r))
    return np.broadcast_to(tab[None, :], (128, 64)).astype(np.float32).copy()


class Sem:
    def __init__(self, nc, ctx, name):
        self.h = ctx.enter_context(nc.semaphore(name))
        self.n = 0

    def inc(self, instr, k=1):
        instr.then_inc(self.h, k)
        self.n += k
        return self.n


class Ctx:

    def __init__(self, nc, stack):
        self.nc = nc
        self.stack = stack
        self.ld_g = Sem(nc, stack, "s_ld_g")
        self.ld_s = Sem(nc, stack, "s_ld_s")
        self.mm = Sem(nc, stack, "s_mm")
        self.ev = Sem(nc, stack, "s_ev")
        self.sc = Sem(nc, stack, "s_sc")
        self.gp = Sem(nc, stack, "s_gp")
        self.slot = [Sem(nc, stack, f"s_sl{i}") for i in range(12)]

    def sb(self, name, shape, dtype):
        return self.stack.enter_context(self.nc.sbuf_tensor(name, shape, dtype))


def _bar(nc):
    nc.all_engine_barrier()


def build_launch_A(phases=("A1", "A2", "A3")):
    nc = bass.Bass("TRN2", target_bir_lowering=False, debug=False)
    xT = nc.dram_tensor("xT", [D, TPC], F32, kind="ExternalInput").ap()
    wqkvT = nc.dram_tensor("wqkTt", [32, 128, 16, 128], F32, kind="ExternalInput").ap()
    wvTt = nc.dram_tensor("wvTt", [4, 128, 16, 512], F32, kind="ExternalInput").ap()
    bqkv_pc = nc.dram_tensor("bqkv_pc", [128, 48], F32, kind="ExternalInput").ap()
    bqkv_row = nc.dram_tensor("bqkv_row", [1, 3 * D], F32, kind="ExternalInput").ap()

    qT0 = nc.dram_tensor("qT0", [D, TPC], BF16, kind="ExternalOutput").ap()
    kT0 = nc.dram_tensor("kT0", [D, TPC], BF16, kind="ExternalOutput").ap()
    v0b = nc.dram_tensor("v0h", [H, 128, NCH, 128], BF16, kind="ExternalOutput").ap()
    U0 = nc.dram_tensor("U0", [NCH, H, 128, 128], F32, kind="ExternalOutput").ap()
    T0 = nc.dram_tensor("T0", [H * 128, 128], F32, kind="ExternalOutput").ap()

    ident_t = nc.inline_tensor(IDENT_B16, "c_ident16").ap()
    onesrow_t = nc.inline_tensor(ONES_ROW_F32, "c_onesrow").ap()
    dkcol_t = nc.inline_tensor(DECAY_K_COL, "c_dkcol").ap()

    with ExitStack() as stack:
        ctx = Ctx(nc, stack)
        xb = ctx.sb("xb", [128, DC, TPC], BF16)
        biasc = ctx.sb("biasc", [128, 48], F32)
        biasr = ctx.sb("biasr", [1, 3 * D], F32)
        onesrow = ctx.sb("onesrow", [1, 128], F32)
        identb = ctx.sb("identb", [128, 128], BF16)
        dkcol = ctx.sb("dkcol", [128, H], F32)

        with nc.Block("A0") as blk:
            @blk.gpsimd
            def _(g):
                g.dma_start(out=xb[:], in_=xT.rearrange("(c p) t -> p c t", p=128)).then_inc(ctx.ld_g.h, 16)
                g.dma_start(out=biasc[:], in_=bqkv_pc).then_inc(ctx.ld_g.h, 16)
                g.dma_start(out=biasr[:], in_=bqkv_row).then_inc(ctx.ld_g.h, 16)
                g.dma_start(out=onesrow[:], in_=onesrow_t).then_inc(ctx.ld_g.h, 16)
                g.dma_start(out=identb[:], in_=ident_t).then_inc(ctx.ld_g.h, 16)
                g.dma_start(out=dkcol[:], in_=dkcol_t).then_inc(ctx.ld_g.h, 16)
                ctx.ld_g.n += 96
                g.wait_ge(ctx.ld_g.h, ctx.ld_g.n)
        _bar(nc)

        if "A1" in phases:
            _emit_qk_fm(nc, ctx, xb, wqkvT, biasc, qT0, kT0)
            _bar(nc)

        if "A2" in phases:
            _emit_v_tm(nc, ctx, xb, wvTt, biasr, onesrow, v0b)
            _bar(nc)

        if "A3" in phases:
            _emit_UT(nc, ctx, kT0, v0b, dkcol, identb, U0, T0)
            _bar(nc)
    return nc


def _emit_qk_fm(nc, ctx, xb, wqkvT, biasc, qT0, kT0):
    NB_T = TPC // 512
    NCB = 32
    with ExitStack() as st:
        wbuf = [st.enter_context(nc.sbuf_tensor(f"a1w{i}", [128, DC, 128], BF16)) for i in range(2)]
        wstg = [st.enter_context(nc.sbuf_tensor(f"a1wf{i}", [128, DC, 128], F32)) for i in range(2)]
        stage = [st.enter_context(nc.sbuf_tensor(f"a1s{i}", [128, TPC], BF16)) for i in range(2)]
        ps = [st.enter_context(nc.psum_tensor(f"a1p{i}", [128, 512], F32)) for i in range(8)]

        wsl = [ctx.slot[0], ctx.slot[1]]
        ssl = [ctx.slot[6], ctx.slot[7]]
        wbase = [s.n for s in wsl]
        sbase = [s.n for s in ssl]
        mm0, sc0, gp0 = ctx.mm.n, ctx.sc.n, ctx.gp.n
        with nc.Block("A1") as blk:
            @blk.gpsimd
            def _(g):
                for cb in range(min(2, NCB)):
                    g.dma_start(out=wstg[cb % 2][:], in_=wqkvT[cb]).then_inc(wsl[cb % 2].h, 16)
                    wsl[cb % 2].n += 16
                for cb in range(NCB):
                    g.wait_ge(wsl[cb % 2].h, wbase[cb % 2] + 16 * (cb // 2 + 1))
                    if cb >= 2:
                        g.wait_ge(ctx.mm.h, mm0 + (cb - 1) * NB_T)
                    g.tensor_copy(wbuf[cb % 2][:], wstg[cb % 2][:]).then_inc(ctx.gp.h, 1)
                    if cb + 2 < NCB:
                        g.wait_ge(ctx.gp.h, gp0 + cb + 1)
                        g.dma_start(out=wstg[cb % 2][:], in_=wqkvT[cb + 2]).then_inc(wsl[cb % 2].h, 16)
                        wsl[cb % 2].n += 16
                ctx.gp.n += NCB

            @blk.tensor
            def _(t):
                for cb in range(NCB):
                    t.wait_ge(ctx.gp.h, gp0 + cb + 1)
                    if cb >= 2:
                        t.wait_ge(ctx.sc.h, sc0 + (cb - 1) * NB_T)
                    bank = (cb % 2) * 4
                    for dc in range(DC):
                        for tb in range(NB_T):
                            ins = t.matmul(
                                ps[bank + tb][:],
                                wbuf[cb % 2][:, dc, :],
                                xb[:, dc, tb * 512:(tb + 1) * 512],
                                start=(dc == 0), stop=(dc == DC - 1),
                            )
                            if dc == DC - 1:
                                ins.then_inc(ctx.mm.h, 1)
                ctx.mm.n += NCB * NB_T

            @blk.scalar
            def _(s):
                for cb in range(NCB):
                    bank = (cb % 2) * 4
                    if cb >= 2:
                        s.wait_ge(ssl[cb % 2].h, sbase[cb % 2] + 16 * (cb // 2))
                    for tb in range(NB_T):
                        s.wait_ge(ctx.mm.h, mm0 + cb * NB_T + tb + 1)
                        s.activation(
                            stage[cb % 2][:, tb * 512:(tb + 1) * 512],
                            ps[bank + tb][:],
                            AF.Identity, bias=biasc[:, cb:cb + 1], scale=1.0,
                        ).then_inc(ctx.sc.h, 1)
                ctx.sc.n += NCB * NB_T

            @blk.sync
            def _(sy):
                for cb in range(NCB):
                    sy.wait_ge(ctx.sc.h, sc0 + (cb + 1) * NB_T)
                    dst = qT0 if cb < 16 else kT0
                    r0 = (cb % 16) * 128
                    sy.dma_start(out=dst[r0:r0 + 128, :], in_=stage[cb % 2][:]).then_inc(ssl[cb % 2].h, 16)
                    ssl[cb % 2].n += 16
                sy.wait_ge(ssl[0].h, ssl[0].n)
                sy.wait_ge(ssl[1].h, ssl[1].n)


def _emit_v_tm(nc, ctx, xb, wvTt, biasr, onesrow, v0b):
    NTC = TPC // 128
    with ExitStack() as st:
        wbuf = [st.enter_context(nc.sbuf_tensor(f"a2w{i}", [128, DC, 512], BF16)) for i in range(2)]
        stage = [st.enter_context(nc.sbuf_tensor(f"a2s{i}", [128, 512], BF16)) for i in range(4)]
        ps = [st.enter_context(nc.psum_tensor(f"a2p{i}", [128, 512], F32)) for i in range(4)]

        wsl = [ctx.slot[0], ctx.slot[1]]
        ssl = [ctx.slot[6], ctx.slot[7], ctx.slot[8], ctx.slot[9]]
        wbase = [s.n for s in wsl]
        sbase = [s.n for s in ssl]
        mm0, ev0 = ctx.mm.n, ctx.ev.n
        with nc.Block("A2") as blk:
            @blk.gpsimd
            def _(g):
                for nb in range(4):
                    if nb >= 2:
                        g.wait_ge(ctx.mm.h, mm0 + (nb - 1) * NTC)
                    g.dma_start(
                        out=wbuf[nb % 2][:], in_=wvTt[nb],
                    ).then_inc(wsl[nb % 2].h, 16)
                    wsl[nb % 2].n += 16

            @blk.tensor
            def _(t):
                for nb in range(4):
                    t.wait_ge(wsl[nb % 2].h, wbase[nb % 2] + 16 * (nb // 2 + 1))
                    for tc in range(NTC):
                        it = nb * NTC + tc
                        if it >= 4:
                            t.wait_ge(ctx.ev.h, ev0 + it - 3)
                        bank = it % 4
                        for dc in range(DC):
                            t.matmul(
                                ps[bank][:], xb[:, dc, tc * 128:(tc + 1) * 128],
                                wbuf[nb % 2][:, dc, :],
                                start=(dc == 0), stop=False,
                            )
                        t.matmul(
                            ps[bank][:], onesrow[:],
                            biasr[:, D * 2 + nb * 512:D * 2 + (nb + 1) * 512],
                            start=False, stop=True,
                        ).then_inc(ctx.mm.h, 1)
                ctx.mm.n += 64

            @blk.vector
            def _(v):
                for it in range(64):
                    v.wait_ge(ctx.mm.h, mm0 + it + 1)
                    if it >= 4:
                        v.wait_ge(ssl[it % 4].h, sbase[it % 4] + 64 * (it // 4))
                    v.tensor_copy(stage[it % 4][:], ps[it % 4][:]).then_inc(ctx.ev.h, 1)
                ctx.ev.n += 64

            @blk.sync
            def _(sy):
                for nb in range(4):
                    for tc in range(NTC):
                        it = nb * NTC + tc
                        sy.wait_ge(ctx.ev.h, ev0 + it + 1)
                        for j in range(4):
                            sy.dma_start(
                                out=v0b[nb * 4 + j, :, tc, :],
                                in_=stage[it % 4][:, j * 128:(j + 1) * 128],
                            ).then_inc(ssl[it % 4].h, 16)
                        ssl[it % 4].n += 64
                for s in ssl:
                    sy.wait_ge(s.h, s.n)


def _emit_UT(nc, ctx, kT0, v0b, dkcol, identb, U0, T0):
    with ExitStack() as st:
        kh = [st.enter_context(nc.sbuf_tensor(f"a3k{j}", [128, TPC], BF16)) for j in range(2)]
        vh = [st.enter_context(nc.sbuf_tensor(f"a3v{j}", [128, NCH, 128], BF16)) for j in range(2)]
        ktm = [st.enter_context(nc.sbuf_tensor(f"a3kt{i}", [128, 128], BF16)) for i in range(2)]
        ust = [st.enter_context(nc.sbuf_tensor(f"a3us{i}", [128, 512], F32)) for i in range(2)]
        tacc = st.enter_context(nc.sbuf_tensor("a3t", [128, 128], F32))
        pkt = [st.enter_context(nc.psum_tensor(f"a3pk{i}", [128, 128], BF16)) for i in range(2)]
        pU = [st.enter_context(nc.psum_tensor(f"a3pu{i}", [128, 512], F32)) for i in range(2)]

        lsl = [ctx.slot[10], ctx.slot[11]]
        usl = [ctx.slot[2], ctx.slot[3]]
        tsl = ctx.slot[4]
        lbase = [s.n for s in lsl]
        ubase = [s.n for s in usl]
        tbase = tsl.n
        mm0, ev0 = ctx.mm.n, ctx.ev.n
        NI = H * NCH
        NQ = NI // 4
        S = {}
        evv = ev0
        EH = {}
        for h in range(H):
            for c in range(NCH):
                i = h * NCH + c
                evv += 1; S["ek", i] = evv
                evv += 1; S["et", i] = evv
                if i % 4 == 3:
                    evv += 1; S["eu", i // 4] = evv
            EH[h] = evv
        ev_end = evv

        with nc.Block("A3") as blk:
            @blk.sync
            def _(sy):
                for h in range(H):
                    if h >= 2:
                        sy.wait_ge(ctx.mm.h, mm0 + 2 * NCH * (h - 1))
                        sy.wait_ge(ctx.ev.h, EH[h - 2])
                    sy.dma_start(out=kh[h % 2][:], in_=kT0[h * 128:(h + 1) * 128, :]).then_inc(lsl[h % 2].h, 16)
                    sy.dma_start(out=vh[h % 2][:], in_=v0b[h]).then_inc(lsl[h % 2].h, 16)
                    lsl[h % 2].n += 32
                for s in usl:
                    sy.wait_ge(s.h, s.n + 16 * (NQ // 2))
                sy.wait_ge(tsl.h, tsl.n + 16 * H)

            @blk.vector
            def _(v):
                for h in range(H):
                    for c in range(NCH):
                        i = h * NCH + c
                        v.wait_ge(ctx.mm.h, mm0 + 2 * i + 1)
                        if i >= 2:
                            v.wait_ge(ctx.mm.h, mm0 + 2 * (i - 2) + 2)
                        v.tensor_scalar_mul(ktm[i % 2][:], pkt[i % 2][:],
                                            dkcol[:, h:h + 1]).then_inc(ctx.ev.h, 1)
                        v.wait_ge(ctx.mm.h, mm0 + 2 * i + 2)
                        lam_w = float(LAMC[h] ** (NCH - 1 - c))
                        pslot = pU[(i // 4) % 2][:, (i % 4) * 128:(i % 4 + 1) * 128]
                        if c == 0:
                            if h >= 1:
                                v.wait_ge(tsl.h, tbase + 16 * h)
                            ins = v.tensor_scalar_mul(tacc[:], pslot, lam_w)
                        else:
                            ins = v.scalar_tensor_tensor(tacc[:], pslot, lam_w,
                                                         tacc[:], ALU.mult, ALU.add)
                        ins.then_inc(ctx.ev.h, 1)
                        if i % 4 == 3:
                            q = i // 4
                            if q >= 2:
                                v.wait_ge(usl[q % 2].h, ubase[q % 2] + 16 * (q // 2))
                            v.wait_ge(ctx.ev.h, S["et", i])
                            v.tensor_copy(ust[q % 2][:], pU[q % 2][:]).then_inc(ctx.ev.h, 1)
                ctx.ev.n = ev_end

            @blk.tensor
            def _(t):
                for h in range(H):
                    for c in range(NCH):
                        i = h * NCH + c
                        if c == 0:
                            t.wait_ge(lsl[h % 2].h, lbase[h % 2] + 32 * (h // 2 + 1))
                        if i >= 2:
                            t.wait_ge(ctx.ev.h, S["ek", i - 2])
                        t.transpose(pkt[i % 2][:], kh[h % 2][:, c * 128:(c + 1) * 128],
                                    identb[:]).then_inc(ctx.mm.h, 1)
                        if i >= 8:
                            t.wait_ge(ctx.ev.h, S["eu", i // 4 - 2])
                            t.wait_ge(ctx.ev.h, S["et", i - 8])
                        t.wait_ge(ctx.ev.h, S["ek", i])
                        t.matmul(pU[(i // 4) % 2][:, (i % 4) * 128:(i % 4 + 1) * 128],
                                 ktm[i % 2][:], vh[h % 2][:, c, :],
                                 start=True, stop=True).then_inc(ctx.mm.h, 1)
                ctx.mm.n += NI * 2

            @blk.gpsimd
            def _(g):
                for h in range(H):
                    for c in range(NCH):
                        i = h * NCH + c
                        if i % 4 == 3:
                            q = i // 4
                            g.wait_ge(ctx.ev.h, S["eu", q])
                            c0 = (q * 4) % NCH
                            g.dma_start(
                                out=_AP(U0.tensor, c0 * (H * 128 * 128) + h * (128 * 128),
                                        [[128, 128], [H * 128 * 128, 4], [1, 128]]),
                                in_=ust[q % 2][:]).then_inc(usl[q % 2].h, 16)
                            usl[q % 2].n += 16
                    g.wait_ge(ctx.ev.h, EH[h])
                    g.dma_start(out=T0[h * 128:(h + 1) * 128, :], in_=tacc[:]).then_inc(tsl.h, 16)
                    tsl.n += 16



def build_launch_B(phases=("B1", "B2", "B3", "B4"), dbg=False):
    nc = bass.Bass("TRN2", target_bir_lowering=False, debug=False)
    xT = nc.dram_tensor("xT", [D, TPC], F32, kind="ExternalInput").ap()
    T_all = nc.dram_tensor("T_all", [4 * H * 128, 128], F32, kind="ExternalInput").ap()
    qT0 = nc.dram_tensor("qT0", [D, TPC], BF16, kind="ExternalInput").ap()
    kT0 = nc.dram_tensor("kT0", [D, TPC], BF16, kind="ExternalInput").ap()
    v0b = nc.dram_tensor("v0h", [H, 128, NCH, 128], BF16, kind="ExternalInput").ap()
    U0 = nc.dram_tensor("U0", [NCH, H, 128, 128], F32, kind="ExternalInput").ap()
    woutT = nc.dram_tensor("woutTt", [16, 128, 16, 128], F32, kind="ExternalInput").ap()
    bout_pc = nc.dram_tensor("bout_pc", [128, 16], F32, kind="ExternalInput").ap()
    W1T = nc.dram_tensor("W1Tt", [32, 128, 16, 128], F32, kind="ExternalInput").ap()
    b1_pc = nc.dram_tensor("b1_pc", [128, 32], F32, kind="ExternalInput").ap()
    W2T = nc.dram_tensor("W2Tt", [32, 128, 32, 128], F32, kind="ExternalInput").ap()
    b2_pc = nc.dram_tensor("b2_pc", [128, 32], F32, kind="ExternalInput").ap()
    W3T = nc.dram_tensor("W3Tt", [16, 128, 32, 128], F32, kind="ExternalInput").ap()
    b3_pc = nc.dram_tensor("b3_pc", [128, 16], F32, kind="ExternalInput").ap()
    coeff = nc.dram_tensor("coeff", [128, 64], F32, kind="ExternalInput").ap()

    x2f = nc.dram_tensor("x2f", [D, TPC], F32, kind="ExternalOutput").ap()
    x2b = nc.dram_tensor("x2b", [D, TPC], BF16, kind="ExternalOutput").ap()

    ik = "ExternalOutput" if dbg else "Internal"
    oT0 = nc.dram_tensor("oT0", [D, TPC], BF16, kind=ik).ap()
    x1f = nc.dram_tensor("x1f", [D, TPC], F32, kind=ik).ap()
    x1b = nc.dram_tensor("x1b", [D, TPC], BF16, kind=ik).ap()
    rbuf = nc.dram_tensor("rbuf", [D, TPC], F32, kind="Internal").ap()

    ident_t = nc.inline_tensor(IDENT_B16, "c_ident16").ap()
    onesrow_t = nc.inline_tensor(ONES_ROW_F32, "c_onesrow").ap()
    onescol_t = nc.inline_tensor(ONES_COL_B16, "c_onescol").ap()
    dmaskT_t = nc.inline_tensor(DMASKT_T, "c_dmaskT").ap()
    decq_t = nc.inline_tensor(np.ascontiguousarray(DECAY_Q_T), "c_decq").ap()

    with ExitStack() as stack:
        ctx = Ctx(nc, stack)
        onesrow = ctx.sb("onesrow", [1, 128], F32)
        onescol = ctx.sb("onescol", [128, 1], BF16)
        identb = ctx.sb("identb", [128, 128], BF16)
        bias16a = ctx.sb("bias16a", [128, 16], F32)
        bias32a = ctx.sb("bias32a", [128, 32], F32)
        bias32b = ctx.sb("bias32b", [128, 32], F32)
        bias16b = ctx.sb("bias16b", [128, 16], F32)

        with nc.Block("B0") as blk:
            @blk.gpsimd
            def _(g):
                for dst, src in [
                    (onesrow[:], onesrow_t), (onescol[:], onescol_t), (identb[:], ident_t),
                    (bias16a[:], bout_pc), (bias32a[:], b1_pc),
                    (bias32b[:], b2_pc), (bias16b[:], b3_pc),
                ]:
                    g.dma_start(out=dst, in_=src).then_inc(ctx.ld_g.h, 16)
                    ctx.ld_g.n += 16
                g.wait_ge(ctx.ld_g.h, ctx.ld_g.n)
        _bar(nc)

        with ExitStack() as stB12:
            dmaskT = stB12.enter_context(nc.sbuf_tensor("dmaskT", [128, H, 128], F32))
            decq = stB12.enter_context(nc.sbuf_tensor("decq", [128, H, 128], BF16))
            coeff_sb = stB12.enter_context(nc.sbuf_tensor("coeff_sb", [128, 64], F32))
            sinit = stB12.enter_context(nc.sbuf_tensor("sinit", [128, H, 128], F32))
            with nc.Block("B0b") as blk:
                @blk.gpsimd
                def _(g):
                    for dst, src in [
                        (dmaskT[:], dmaskT_t.rearrange("h p e -> p h e")),
                        (decq[:], decq_t.rearrange("h p e -> p h e")),
                        (coeff_sb[:], coeff),
                    ]:
                        g.dma_start(out=dst, in_=src).then_inc(ctx.ld_g.h, 16)
                        ctx.ld_g.n += 16
                    g.wait_ge(ctx.ld_g.h, ctx.ld_g.n)
            _bar(nc)

            if "B1" in phases:
                _emit_sinit(nc, ctx, T_all, coeff_sb, sinit)
                _bar(nc)
            if "B2" in phases:
                _emit_scan(nc, ctx, qT0, kT0, v0b, U0, sinit, dmaskT, decq, oT0)
                _bar(nc)
        if "B3" in phases:
            _emit_proj_ln(nc, ctx, "B3", rhs_hbm=oT0, K_=D, M=D, w_hbm=woutT,
                          bias_col=bias16a, xold_hbm=xT, rbuf=rbuf,
                          xnew_f=x1f, xnew_b=x1b, onescol=onescol, onesrow=onesrow)
            _bar(nc)
        if "B4" in phases:
            _emit_ffn(nc, ctx, x1b, x1f, W1T, bias32a, W2T, bias32b, W3T, bias16b,
                      rbuf, x2f, x2b, onescol, onesrow)
            _bar(nc)
    return nc


def _emit_sinit(nc, ctx, T_all, coeff_sb, sinit):
    lsl = ctx.slot[10]
    lbase = lsl.n
    ev0 = ctx.ev.n
    with ExitStack() as st:
        tt = st.enter_context(nc.sbuf_tensor("b1t", [128, 4, 128], F32))
        with nc.Block("B1") as blk:
            @blk.sync
            def _(sy):
                for h in range(H):
                    if h >= 1:
                        sy.wait_ge(ctx.ev.h, ev0 + 4 * h)
                    for r in range(4):
                        sy.dma_start(out=tt[:, r, :], in_=T_all[(r * H + h) * 128:(r * H + h + 1) * 128, :]).then_inc(lsl.h, 16)
                        lsl.n += 16

            @blk.vector
            def _(v):
                for h in range(H):
                    v.wait_ge(lsl.h, lbase + 64 * (h + 1))
                    for r in range(4):
                        cc = coeff_sb[:, h * 4 + r:h * 4 + r + 1]
                        if r == 0:
                            ins = v.tensor_scalar_mul(sinit[:, h, :], tt[:, 0, :], cc)
                        else:
                            v.wait_ge(ctx.ev.h, ev0 + 4 * h + r)
                            ins = v.scalar_tensor_tensor(
                                sinit[:, h, :], tt[:, r, :], cc, sinit[:, h, :], ALU.mult, ALU.add)
                        ins.then_inc(ctx.ev.h, 1)
                ctx.ev.n += 4 * H


def _emit_scan(nc, ctx, qT0, kT0, v0b, U0, sinit, dmaskT, decq, oT0):
    lsl = [ctx.slot[8 + i] for i in range(4)]
    NP = H // 2
    with ExitStack() as st:
        osl = [ctx.slot[6], ctx.slot[7], Sem(nc, st, "b2os2"), Sem(nc, st, "b2os3")]
        lbase = [s.n for s in lsl]
        obase = [s.n for s in osl]
        store_val = {h: obase[h % 4] + 16 * (h // 4 + 1) for h in range(H)}
        qh = [st.enter_context(nc.sbuf_tensor(f"b2q{i}", [128, TPC], BF16)) for i in range(4)]
        kh = [st.enter_context(nc.sbuf_tensor(f"b2k{i}", [128, TPC], BF16)) for i in range(4)]
        vh = [st.enter_context(nc.sbuf_tensor(f"b2v{i}", [128, NCH, 128], BF16)) for i in range(4)]
        uh = [st.enter_context(nc.sbuf_tensor(f"b2u{i}", [128, NCH, 128], F32)) for i in range(4)]
        oh = [st.enter_context(nc.sbuf_tensor(f"b2o{i}", [128, TPC], BF16)) for i in range(4)]
        qd = [st.enter_context(nc.sbuf_tensor(f"b2qd{i}", [128, 128], BF16)) for i in range(4)]
        pm = [st.enter_context(nc.sbuf_tensor(f"b2pm{i}", [128, 128], BF16)) for i in range(4)]
        sf = [st.enter_context(nc.sbuf_tensor(f"b2sf{i}", [128, 128], F32)) for i in range(2)]
        sb16 = [st.enter_context(nc.sbuf_tensor(f"b2sb{i}", [128, 128], BF16)) for i in range(2)]
        pso = [st.enter_context(nc.psum_tensor(f"b2po{i}", [128, 128], F32)) for i in range(4)]
        pss = [st.enter_context(nc.psum_tensor(f"b2ps{i}", [128, 128], F32)) for i in range(4)]

        S = {}
        mmv, evv, scv, gpv = ctx.mm.n, ctx.ev.n, ctx.sc.n, ctx.gp.n
        steps = []
        for p in range(NP):
            scv += 1; S["sc_c", 2 * p, 0] = scv
            scv += 1; S["sc_c", 2 * p + 1, 0] = scv
            for c in range(NCH):
                for s in range(2):
                    h = 2 * p + s
                    steps.append((h, c))
                    gpv += 1; S["gp", h, c] = gpv
                    mmv += 2; S["mm_12", h, c] = mmv
                    mmv += 1; S["mm_3", h, c] = mmv
                    evv += 1; S["ev_pm", h, c] = evv
                    evv += 1; S["ev_s", h, c] = evv
                    scv += 1; S["sc_o", h, c] = scv
                    if c < NCH - 1:
                        scv += 1; S["sc_c", h, c + 1] = scv
        mm_end, ev_end, sc_end, gp_end = mmv, evv, scv, gpv

        with nc.Block("B2") as blk:
            @blk.sync
            def _(sy):
                for p in range(NP):
                    if p >= 2:
                        hl = 2 * p - 3
                        sy.wait_ge(ctx.ev.h, S["ev_s", hl, NCH - 1])
                        sy.wait_ge(ctx.mm.h, S["mm_3", hl, NCH - 1])
                        sy.wait_ge(ctx.gp.h, S["gp", hl, NCH - 1])
                    for s in range(2):
                        h = 2 * p + s
                        sl_ = lsl[h % 4]
                        sy.dma_start(out=qh[h % 4][:], in_=qT0[h * 128:(h + 1) * 128, :]).then_inc(sl_.h, 16)
                        sy.dma_start(out=kh[h % 4][:], in_=kT0[h * 128:(h + 1) * 128, :]).then_inc(sl_.h, 16)
                        sy.dma_start(out=vh[h % 4][:], in_=v0b[h]).then_inc(sl_.h, 16)
                        usrc = _AP(U0.tensor, h * (128 * 128),
                                   [[128, 128], [H * 128 * 128, NCH], [1, 128]])
                        sy.dma_start(out=uh[h % 4][:], in_=usrc).then_inc(sl_.h, 16)
                        sl_.n += 16 * 4

            @blk.gpsimd
            def _(g):
                for j, (h, c) in enumerate(steps):
                    if c == 0:
                        g.wait_ge(lsl[h % 4].h, lbase[h % 4] + 16 * 4 * (h // 4 + 1))
                    if j >= 4:
                        g.wait_ge(ctx.mm.h, S["mm_12", steps[j - 4][0], steps[j - 4][1]])
                    g.tensor_tensor(qd[j % 4][:], qh[h % 4][:, c * 128:(c + 1) * 128],
                                    decq[:, h, :], ALU.mult).then_inc(ctx.gp.h, 1)
                ctx.gp.n = gp_end

            @blk.tensor
            def _(t):
                for j, (h, c) in enumerate(steps):
                    t.wait_ge(ctx.sc.h, S["sc_c", h, c])
                    t.wait_ge(ctx.gp.h, S["gp", h, c])
                    if j >= 4:
                        t.wait_ge(ctx.ev.h, S["ev_pm", steps[j - 4][0], steps[j - 4][1]])
                    t.matmul(pso[j % 4][:], sb16[h % 2][:], qd[j % 4][:], start=True, stop=False)
                    t.matmul(pss[j % 4][:], kh[h % 4][:, c * 128:(c + 1) * 128],
                             qh[h % 4][:, c * 128:(c + 1) * 128], start=True, stop=True).then_inc(ctx.mm.h, 2)
                    t.wait_ge(ctx.ev.h, S["ev_pm", h, c])
                    t.matmul(pso[j % 4][:], vh[h % 4][:, c, :], pm[j % 4][:],
                             start=False, stop=True).then_inc(ctx.mm.h, 1)
                ctx.mm.n = mm_end

            @blk.vector
            def _(v):
                for j, (h, c) in enumerate(steps):
                    v.wait_ge(ctx.mm.h, S["mm_12", h, c])
                    v.tensor_tensor(pm[j % 4][:], pss[j % 4][:], dmaskT[:, h, :],
                                    ALU.mult).then_inc(ctx.ev.h, 1)
                    if c > 0:
                        v.wait_ge(ctx.ev.h, S["ev_s", h, c - 1])
                    v.scalar_tensor_tensor(sf[h % 2][:], sf[h % 2][:], float(LAMC[h]),
                                           uh[h % 4][:, c, :], ALU.mult,
                                           ALU.add).then_inc(ctx.ev.h, 1)
                ctx.ev.n = ev_end

            @blk.scalar
            def _(s):
                for p in range(NP):
                    for sx in range(2):
                        h = 2 * p + sx
                        if p >= 1:
                            s.wait_ge(ctx.mm.h, S["mm_12", h - 2, NCH - 1])
                            s.wait_ge(ctx.ev.h, S["ev_s", h - 2, NCH - 1])
                        s.copy(sf[sx][:], sinit[:, h, :])
                        s.copy(sb16[sx][:], sinit[:, h, :]).then_inc(ctx.sc.h, 1)
                    for c in range(NCH):
                        for sx in range(2):
                            h = 2 * p + sx
                            j = p * 2 * NCH + 2 * c + sx
                            s.wait_ge(ctx.mm.h, S["mm_3", h, c])
                            if c == 0 and h >= 4:
                                s.wait_ge(osl[h % 4].h, store_val[h - 4])
                            s.activation(oh[h % 4][:, c * 128:(c + 1) * 128], pso[j % 4][:],
                                         AF.Identity).then_inc(ctx.sc.h, 1)
                            if c < NCH - 1:
                                s.wait_ge(ctx.ev.h, S["ev_s", h, c])
                                s.wait_ge(ctx.mm.h, S["mm_12", h, c])
                                s.copy(sb16[sx][:], sf[sx][:]).then_inc(ctx.sc.h, 1)
                            else:
                                s.wait_ge(ctx.sc.h, S["sc_o", h, NCH - 1])
                                s.dma_start(out=oT0[h * 128:(h + 1) * 128, :],
                                            in_=oh[h % 4][:]).then_inc(osl[h % 4].h, 16)
                                osl[h % 4].n += 16
                for s_ in osl:
                    s.wait_ge(s_.h, s_.n)
                ctx.sc.n = sc_end


def _emit_proj_ln(nc, ctx, name, *, rhs_hbm=None, rhs_sb=None, K_, M, w_hbm,
                  bias_col, xold_hbm, rbuf, xnew_f, xnew_b, onescol, onesrow,
                  halves=(0, 1)):
    KC = K_ // 128
    MB = M // 128
    assert M == D
    for hf in halves:
        t0 = hf * 1024
        with ExitStack() as st:
            if rhs_sb is None:
                rh = st.enter_context(nc.sbuf_tensor(f"{name}rh{hf}", [128, KC, 1024], BF16))
                rhs_view = rh
                rhs_off = 0
            else:
                rhs_view = rhs_sb
                rhs_off = 0 if rhs_sb.shape[2] == 1024 else t0
            wbuf = [st.enter_context(nc.sbuf_tensor(f"{name}w{hf}{i}", [128, KC, 128], BF16)) for i in range(2)]
            wstg = [st.enter_context(nc.sbuf_tensor(f"{name}wf{hf}{i}", [128, 16, 128], F32)) for i in range(2)]
            xsb = [st.enter_context(nc.sbuf_tensor(f"{name}x{hf}{i}", [128, 512], F32)) for i in range(4)]
            ysb = [st.enter_context(nc.sbuf_tensor(f"{name}y{hf}{i}", [128, 512], F32)) for i in range(2)]
            rsb = [st.enter_context(nc.sbuf_tensor(f"{name}r{hf}{i}", [128, 512], F32)) for i in range(2)]
            rb16 = [st.enter_context(nc.sbuf_tensor(f"{name}rb{hf}{i}", [128, 512], BF16)) for i in range(2)]
            rq16 = [st.enter_context(nc.sbuf_tensor(f"{name}rq{hf}{i}", [128, 512], BF16)) for i in range(2)]
            rsb2 = [st.enter_context(nc.sbuf_tensor(f"{name}r2{hf}{i}", [128, 512], F32)) for i in range(2)]
            tmp = [st.enter_context(nc.sbuf_tensor(f"{name}tp{hf}{i}", [128, 512], F32)) for i in range(2)]
            ofst = [st.enter_context(nc.sbuf_tensor(f"{name}of{hf}{i}", [128, 512], F32)) for i in range(2)]
            obst = [st.enter_context(nc.sbuf_tensor(f"{name}ob{hf}{i}", [128, 512], BF16)) for i in range(2)]
            srow = st.enter_context(nc.sbuf_tensor(f"{name}sr{hf}", [1, 16, 512], F32))
            pp = [st.enter_context(nc.psum_tensor(f"{name}pp{hf}{i}", [128, 512], F32)) for i in range(4)]
            pstat = [st.enter_context(nc.psum_tensor(f"{name}pq{hf}{i}", [1, 512], F32)) for i in range(4)]

            wsl = [ctx.slot[0], ctx.slot[1]]
            rstsl = [ctx.slot[2], ctx.slot[3]]
            ostsl = [ctx.slot[4], ctx.slot[5]]
            xsl = [ctx.slot[6 + i] for i in range(4)]
            rldsl = [ctx.slot[10], ctx.slot[11]]
            wb_ = [s.n for s in wsl]
            rst_ = [s.n for s in rstsl]
            ost_ = [s.n for s in ostsl]
            xb_ = [s.n for s in xsl]
            rld_ = [s.n for s in rldsl]
            mm0, ev0, sc0, g0, gp0 = ctx.mm.n, ctx.ev.n, ctx.sc.n, ctx.ld_s.n, ctx.gp.n

            mmv = mm0
            mm_proj = {}
            mm_stats = {}
            mm_bcast = {}
            for ob in range(MB):
                for tb in range(2):
                    mmv += 1
                    mm_proj[ob * 2 + tb] = mmv
                for jj in ([ob * 2 - 2, ob * 2 - 1] if ob >= 1 else []):
                    mmv += 2
                    mm_stats[jj] = mmv
            for jj in (2 * MB - 2, 2 * MB - 1):
                mmv += 2
                mm_stats[jj] = mmv
            for tb in range(2):
                mmv += 2
                mm_bcast[tb] = mmv
            scv = sc0
            sc_y = {}
            sc_sq = {}
            for j in range(2 * MB):
                scv += 1
                sc_y[j] = scv
                if j >= 1:
                    scv += 1
                    sc_sq[j - 1] = scv
            scv += 1
            sc_sq[2 * MB - 1] = scv
            sc_m2 = {}
            sc_std = {}
            for tb in range(2):
                scv += 1
                sc_m2[tb] = scv
                scv += 1
                sc_std[tb] = scv
            evv = ev0
            ev_add = {}
            ev_cast = {}
            for j in range(2 * MB):
                evv += 1
                ev_add[j] = evv
                evv += 1
                ev_cast[j] = evv
            ev_fin = {}
            for tb in range(2):
                evv += 6
                ev_fin[tb] = evv
            ev_apply = {}
            for j in range(2 * MB):
                evv += 3
                ev_apply[j] = evv

            with nc.Block(f"{name}_{hf}") as blk:
                @blk.sync
                def _(sy):
                    if rhs_sb is None:
                        sy.dma_start(
                            out=rh[:],
                            in_=rhs_hbm[:, t0:t0 + 1024].rearrange("(c p) t -> p c t", p=128),
                        ).then_inc(ctx.ld_s.h, 16)
                        ctx.ld_s.n += 16
                    for j in range(2 * MB):
                        ob, tb = j // 2, j % 2
                        if j >= 4:
                            sy.wait_ge(ctx.ev.h, ev_add[j - 4])
                        sy.dma_start(
                            out=xsb[j % 4][:],
                            in_=xold_hbm[ob * 128:(ob + 1) * 128, t0 + tb * 512:t0 + (tb + 1) * 512],
                        ).then_inc(xsl[j % 4].h, 16)
                        xsl[j % 4].n += 16
                    for j in range(2 * MB):
                        ob, tb = j // 2, j % 2
                        sy.wait_ge(rstsl[j % 2].h, rst_[j % 2] + 16 * (j // 2 + 1))
                        if j >= 2:
                            sy.wait_ge(ctx.ev.h, ev_apply[j - 2])
                        sy.dma_start(
                            out=rsb2[j % 2][:],
                            in_=rbuf[ob * 128:(ob + 1) * 128, t0 + tb * 512:t0 + (tb + 1) * 512],
                        ).then_inc(rldsl[j % 2].h, 16)
                        rldsl[j % 2].n += 16

                @blk.gpsimd
                def _(g):
                    def rstore(j):
                        ob_, tb_ = j // 2, j % 2
                        g.wait_ge(ctx.ev.h, ev_add[j])
                        g.dma_start(
                            out=rbuf[ob_ * 128:(ob_ + 1) * 128, t0 + tb_ * 512:t0 + (tb_ + 1) * 512],
                            in_=rsb[j % 2][:],
                        ).then_inc(rstsl[j % 2].h, 16)
                        rstsl[j % 2].n += 16

                    NP = KC // 16
                    pieces = [(ob_, p_) for ob_ in range(MB) for p_ in range(NP)]

                    def pload(i):
                        ob_, p_ = pieces[i]
                        g.dma_start(out=wstg[i % 2][:],
                                    in_=w_hbm[ob_][:, p_ * 16:(p_ + 1) * 16, :]).then_inc(wsl[i % 2].h, 16)
                        wsl[i % 2].n += 16
                    for i in range(min(2, len(pieces))):
                        pload(i)
                    for i, (ob, p) in enumerate(pieces):
                        g.wait_ge(wsl[i % 2].h, wb_[i % 2] + 16 * (i // 2 + 1))
                        if p == 0 and ob >= 2:
                            g.wait_ge(ctx.mm.h, mm_proj[(ob - 1) * 2 + 1])
                        g.tensor_copy(wbuf[ob % 2][:, p * 16:(p + 1) * 16, :],
                                      wstg[i % 2][:]).then_inc(ctx.gp.h, 1)
                        if i + 2 < len(pieces):
                            g.wait_ge(ctx.gp.h, gp0 + i + 1)
                            pload(i + 2)
                        if p == NP - 1 and ob >= 1:
                            rstore(2 * (ob - 1))
                            rstore(2 * (ob - 1) + 1)
                    ctx.gp.n += len(pieces)
                    rstore(2 * MB - 2)
                    rstore(2 * MB - 1)
                    for j in range(2 * MB):
                        ob, tb = j // 2, j % 2
                        g.wait_ge(ctx.ev.h, ev_apply[j])
                        c0, c1 = t0 + tb * 512, t0 + (tb + 1) * 512
                        g.dma_start(out=xnew_f[ob * 128:(ob + 1) * 128, c0:c1],
                                    in_=ofst[j % 2][:]).then_inc(ostsl[j % 2].h, 16)
                        g.dma_start(out=xnew_b[ob * 128:(ob + 1) * 128, c0:c1],
                                    in_=obst[j % 2][:]).then_inc(ostsl[j % 2].h, 16)
                        ostsl[j % 2].n += 32
                    for s in ostsl:
                        g.wait_ge(s.h, s.n)

                @blk.tensor
                def _(t):
                    if rhs_sb is None:
                        t.wait_ge(ctx.ld_s.h, g0 + 16)
                    for ob in range(MB):
                        t.wait_ge(ctx.gp.h, gp0 + (KC // 16) * (ob + 1))
                        for tb in range(2):
                            j = ob * 2 + tb
                            bank = (ob % 2) * 2 + tb
                            if ob >= 2:
                                t.wait_ge(ctx.sc.h, sc_y[(ob - 2) * 2 + tb])
                            for kc in range(KC):
                                ins = t.matmul(
                                    pp[bank][:], wbuf[ob % 2][:, kc, :],
                                    rhs_view[:, kc, rhs_off + tb * 512:rhs_off + (tb + 1) * 512],
                                    start=(kc == 0), stop=(kc == KC - 1),
                                )
                            ins.then_inc(ctx.mm.h, 1)
                        for jj in ([(ob - 1) * 2, (ob - 1) * 2 + 1] if ob >= 1 else []):
                            ob2, tb2 = jj // 2, jj % 2
                            t.wait_ge(ctx.ev.h, ev_cast[jj])
                            t.wait_ge(ctx.sc.h, sc_sq[jj])
                            t.matmul(pstat[tb2][:], onescol[:], rb16[jj % 2][:],
                                     start=(ob2 == 0), stop=(ob2 == MB - 1))
                            t.matmul(pstat[2 + tb2][:], onescol[:], rq16[jj % 2][:],
                                     start=(ob2 == 0), stop=(ob2 == MB - 1)).then_inc(ctx.mm.h, 2)
                    for jj in (2 * MB - 2, 2 * MB - 1):
                        ob2, tb2 = jj // 2, jj % 2
                        t.wait_ge(ctx.ev.h, ev_cast[jj])
                        t.wait_ge(ctx.sc.h, sc_sq[jj])
                        t.matmul(pstat[tb2][:], onescol[:], rb16[jj % 2][:],
                                 start=(ob2 == 0), stop=(ob2 == MB - 1))
                        t.matmul(pstat[2 + tb2][:], onescol[:], rq16[jj % 2][:],
                                 start=(ob2 == 0), stop=(ob2 == MB - 1)).then_inc(ctx.mm.h, 2)
                    for tb in range(2):
                        t.wait_ge(ctx.ev.h, ev_fin[tb])
                        t.matmul(pp[tb][:], onesrow[:], srow[:, 2 + 8 * tb, :], start=True, stop=True)
                        t.matmul(pp[2 + tb][:], onesrow[:], srow[:, 6 + 8 * tb, :],
                                 start=True, stop=True).then_inc(ctx.mm.h, 2)
                    ctx.mm.n = mmv

                @blk.scalar
                def _(s):
                    for j in range(2 * MB):
                        ob, tb = j // 2, j % 2
                        bank = (ob % 2) * 2 + tb
                        s.wait_ge(ctx.mm.h, mm_proj[j])
                        if j >= 2:
                            s.wait_ge(ctx.ev.h, ev_add[j - 2])
                        if bias_col is not None:
                            s.activation(ysb[j % 2][:], pp[bank][:], AF.Identity,
                                         bias=bias_col[:, ob:ob + 1], scale=1.0).then_inc(ctx.sc.h, 1)
                        else:
                            s.activation(ysb[j % 2][:], pp[bank][:],
                                         AF.Identity).then_inc(ctx.sc.h, 1)
                        if j >= 1:
                            s.wait_ge(ctx.ev.h, ev_add[j - 1])
                            if j >= 3:
                                s.wait_ge(ctx.mm.h, mm_stats.get(j - 3, mm0))
                            s.activation(rq16[(j - 1) % 2][:], rsb[(j - 1) % 2][:],
                                         AF.Square).then_inc(ctx.sc.h, 1)
                    s.wait_ge(ctx.ev.h, ev_add[2 * MB - 1])
                    s.wait_ge(ctx.mm.h, mm_stats.get(2 * MB - 3, mm0))
                    s.activation(rq16[(2 * MB - 1) % 2][:], rsb[(2 * MB - 1) % 2][:],
                                 AF.Square).then_inc(ctx.sc.h, 1)
                    for tb in range(2):
                        base = 8 * tb
                        s.wait_ge(ctx.ev.h, ev_fin[tb] - 3)
                        s.activation(srow[:, 3 + base, :], srow[:, 2 + base, :],
                                     AF.Square).then_inc(ctx.sc.h, 1)
                        s.wait_ge(ctx.ev.h, ev_fin[tb] - 1)
                        s.activation(srow[:, 5 + base, :], srow[:, 4 + base, :],
                                     AF.Sqrt).then_inc(ctx.sc.h, 1)
                    ctx.sc.n = scv

                @blk.vector
                def _(v):
                    for j in range(2 * MB):
                        ob, tb = j // 2, j % 2
                        v.wait_ge(ctx.sc.h, sc_y[j])
                        v.wait_ge(xsl[j % 4].h, xb_[j % 4] + 16 * (j // 4 + 1))
                        if j >= 2:
                            v.wait_ge(rstsl[j % 2].h, rst_[j % 2] + 16 * (j // 2))
                            v.wait_ge(ctx.sc.h, sc_sq[j - 2])
                        v.tensor_add(rsb[j % 2][:], ysb[j % 2][:], xsb[j % 4][:]).then_inc(ctx.ev.h, 1)
                        if j >= 2:
                            v.wait_ge(ctx.mm.h, mm_stats.get(j - 2, mm0))
                        v.wait_ge(ctx.ev.h, ev_add[j])
                        v.tensor_copy(rb16[j % 2][:], rsb[j % 2][:]).then_inc(ctx.ev.h, 1)
                    for tb in range(2):
                        base = 8 * tb
                        v.wait_ge(ctx.mm.h, mm_stats[2 * MB - 1])
                        v.tensor_copy(srow[:, 0 + base, :], pstat[tb][:]).then_inc(ctx.ev.h, 1)
                        v.tensor_copy(srow[:, 1 + base, :], pstat[2 + tb][:]).then_inc(ctx.ev.h, 1)
                        v.wait_ge(ctx.ev.h, ev_fin[tb] - 4)
                        v.tensor_scalar_mul(srow[:, 2 + base, :], srow[:, 0 + base, :],
                                            1.0 / M).then_inc(ctx.ev.h, 1)
                        v.wait_ge(ctx.sc.h, sc_m2[tb])
                        v.wait_ge(ctx.ev.h, ev_fin[tb] - 3)
                        v.scalar_tensor_tensor(srow[:, 4 + base, :], srow[:, 1 + base, :],
                                               1.0 / M, srow[:, 3 + base, :],
                                               ALU.mult, ALU.subtract).then_inc(ctx.ev.h, 1)
                        v.wait_ge(ctx.ev.h, ev_fin[tb] - 2)
                        v.tensor_scalar_add(srow[:, 4 + base, :], srow[:, 4 + base, :],
                                            float(EPS)).then_inc(ctx.ev.h, 1)
                        v.wait_ge(ctx.sc.h, sc_std[tb])
                        v.reciprocal(srow[:, 6 + base, :], srow[:, 5 + base, :]).then_inc(ctx.ev.h, 1)
                    for j in range(2 * MB):
                        ob, tb = j // 2, j % 2
                        v.wait_ge(ctx.mm.h, mm_bcast[tb])
                        v.wait_ge(rldsl[j % 2].h, rld_[j % 2] + 16 * (j // 2 + 1))
                        if j >= 2:
                            v.wait_ge(ostsl[j % 2].h, ost_[j % 2] + 32 * (j // 2))
                        v.tensor_sub(tmp[j % 2][:], rsb2[j % 2][:], pp[tb][:]).then_inc(ctx.ev.h, 1)
                        v.wait_ge(ctx.ev.h, ev_apply[j] - 2)
                        v.tensor_mul(ofst[j % 2][:], tmp[j % 2][:], pp[2 + tb][:]).then_inc(ctx.ev.h, 1)
                        v.wait_ge(ctx.ev.h, ev_apply[j] - 1)
                        v.tensor_copy(obst[j % 2][:], ofst[j % 2][:]).then_inc(ctx.ev.h, 1)
                    ctx.ev.n = evv
        _bar(nc)


def _emit_proj_ln2(nc, ctx, name, *, rhs_hbm, K_, w_hbm, bias_col, xold_hbm,
                   xnew_f, xnew_b, onescol, onesrow, rh_gate=None):
    KC = K_ // 128
    MB = DC
    R = 34

    def rslot(hf, j):
        return j if hf == 0 else (32 + j) % R

    with ExitStack() as st:
        rres = st.enter_context(nc.sbuf_tensor(f"{name}rr", [128, R, 512], F32))
        rh = st.enter_context(nc.sbuf_tensor(f"{name}rh", [128, KC, 1024], BF16))
        wbuf = [st.enter_context(nc.sbuf_tensor(f"{name}w{i}", [128, KC, 128], BF16)) for i in range(2)]
        wstg = [st.enter_context(nc.sbuf_tensor(f"{name}wf{i}", [128, 16, 128], F32)) for i in range(2)]
        xsb = [st.enter_context(nc.sbuf_tensor(f"{name}x{i}", [128, 512], F32)) for i in range(4)]
        ysb = [st.enter_context(nc.sbuf_tensor(f"{name}y{i}", [128, 512], F32)) for i in range(2)]
        rb16 = [st.enter_context(nc.sbuf_tensor(f"{name}rb{i}", [128, 512], BF16)) for i in range(4)]
        rq16 = [st.enter_context(nc.sbuf_tensor(f"{name}rq{i}", [128, 512], BF16)) for i in range(4)]
        tmp = [st.enter_context(nc.sbuf_tensor(f"{name}tp{i}", [128, 512], F32)) for i in range(2)]
        ofst = [st.enter_context(nc.sbuf_tensor(f"{name}of{i}", [128, 512], F32)) for i in range(2)]
        obst = [st.enter_context(nc.sbuf_tensor(f"{name}ob{i}", [128, 512], BF16)) for i in range(2)]
        ofsp = [st.enter_context(nc.sbuf_tensor(f"{name}og{i}", [128, 512], F32)) for i in range(2)]
        obsp = [st.enter_context(nc.sbuf_tensor(f"{name}oh{i}", [128, 512], BF16)) for i in range(2)]
        tmpp = st.enter_context(nc.sbuf_tensor(f"{name}tq", [128, 512], F32))
        srow = st.enter_context(nc.sbuf_tensor(f"{name}sr", [1, 16, 512], F32))
        mbc = [st.enter_context(nc.sbuf_tensor(f"{name}mb{i}", [128, 512], F32)) for i in range(2)]
        sbc = [st.enter_context(nc.sbuf_tensor(f"{name}sc{i}", [128, 512], F32)) for i in range(2)]
        pb = [st.enter_context(nc.psum_tensor(f"{name}pp{i}", [128, 512], F32)) for i in range(4)]
        pbs = [st.enter_context(nc.psum_tensor(f"{name}pq{i}", [128, 512], F32)) for i in range(2)]
        pbc = [st.enter_context(nc.psum_tensor(f"{name}pc{i}", [128, 512], F32)) for i in range(2)]

        wsl = [ctx.slot[0], ctx.slot[1]]
        ostsl = [Sem(nc, st, f"{name}od0"), Sem(nc, st, f"{name}od1")]
        ostpl = [Sem(nc, st, f"{name}op0"), Sem(nc, st, f"{name}op1")]
        xsl = [ctx.slot[6 + i] for i in range(4)]
        rhsl = [ctx.slot[10], ctx.slot[11]]
        wb_ = [s.n for s in wsl]
        ost_ = [s.n for s in ostsl]
        ostp_ = [s.n for s in ostpl]
        xb_ = [s.n for s in xsl]
        rh_ = [s.n for s in rhsl]

        S = {}
        mmv, scv, evv, gpv = ctx.mm.n, ctx.sc.n, ctx.ev.n, ctx.gp.n
        for hf in range(2):
            for ob in range(MB):
                for tb in range(2):
                    mmv += 1; S["mm_p", hf, ob * 2 + tb] = mmv
                for jj in ([(ob - 2) * 2, (ob - 2) * 2 + 1] if ob >= 2 else []):
                    mmv += 2; S["mm_s", hf, jj] = mmv
            for jj in range(2 * MB - 4, 2 * MB):
                mmv += 2; S["mm_s", hf, jj] = mmv
            mmv += 4; S["mm_bc", hf] = mmv
        for hf in range(2):
            for j in range(2 * MB):
                scv += 1; S["sc_y", hf, j] = scv
                if j >= 1:
                    scv += 1; S["sc_b", hf, j - 1] = scv
                    scv += 1; S["sc_q", hf, j - 1] = scv
                    if hf == 1:
                        scv += 1; S["sc_o", 0, j - 1] = scv
            scv += 1; S["sc_b", hf, 2 * MB - 1] = scv
            scv += 1; S["sc_q", hf, 2 * MB - 1] = scv
            if hf == 1:
                scv += 1; S["sc_o", 0, 2 * MB - 1] = scv
            for tb in range(2):
                scv += 1; S["sc_m2", hf, tb] = scv
                scv += 1; S["sc_sd", hf, tb] = scv
            scv += 4; S["sc_mb", hf] = scv
        for jj in range(MB):
            scv += 1; S["sc_o", 1, jj] = scv
            scv += 1; S["sc_o", 1, MB + jj] = scv
        for j in range(2 * MB):
            evv += 1; S["ev_a", 0, j] = evv
        for tb in range(2):
            evv += 6; S["ev_f", 0, tb] = evv
        for j in range(2 * MB):
            evv += 2; S["ev_ap", 0, j] = evv
            evv += 1; S["ev_a", 1, j] = evv
        for tb in range(2):
            evv += 6; S["ev_f", 1, tb] = evv
        for j in range(MB):
            evv += 2; S["ev_ap", 1, j] = evv
        for hf in range(2):
            for ob in range(MB):
                gpv += 1; S["gp_w", hf, ob] = gpv
        for j in range(MB, 2 * MB):
            gpv += 2; S["gp_ap", 1, j] = gpv
        mm_end, sc_end, ev_end, gp_end = mmv, scv, evv, gpv

        with nc.Block(f"{name}") as blk:
            @blk.sync
            def _(sy):
                if rh_gate is not None:
                    sy.wait_ge(ctx.ld_s.h, rh_gate)
                sy.dma_start(
                    out=rh[:],
                    in_=rhs_hbm[:, 0:1024].rearrange("(c p) t -> p c t", p=128),
                ).then_inc(rhsl[0].h, 16)
                rhsl[0].n += 16
                for hf in range(2):
                    if hf == 1:
                        sy.wait_ge(ctx.mm.h, S["mm_p", 0, 2 * MB - 1])
                        sy.dma_start(
                            out=rh[:],
                            in_=rhs_hbm[:, 1024:2048].rearrange("(c p) t -> p c t", p=128),
                        ).then_inc(rhsl[1].h, 16)
                        rhsl[1].n += 16
                    for j in range(2 * MB):
                        ob, tb = j // 2, j % 2
                        jx = hf * 2 * MB + j
                        if jx >= 4:
                            hp, jp = divmod(jx - 4, 2 * MB)
                            sy.wait_ge(ctx.ev.h, S["ev_a", hp, jp])
                        t0 = hf * 1024
                        sy.dma_start(
                            out=xsb[jx % 4][:],
                            in_=xold_hbm[ob * 128:(ob + 1) * 128, t0 + tb * 512:t0 + (tb + 1) * 512],
                        ).then_inc(xsl[jx % 4].h, 16)
                        xsl[jx % 4].n += 16


            @blk.gpsimd
            def _(g):
                pieces = [(hf, ob) for hf in range(2) for ob in range(MB)]

                def pload(i):
                    hf_, ob_ = pieces[i]
                    g.dma_start(out=wstg[i % 2][:],
                                in_=w_hbm[ob_][:, 0:16, :]).then_inc(wsl[i % 2].h, 16)
                    wsl[i % 2].n += 16

                for i in range(2):
                    pload(i)
                for i, (hf, ob) in enumerate(pieces):
                    g.wait_ge(wsl[i % 2].h, wb_[i % 2] + 16 * (i // 2 + 1))
                    if i >= 2:
                        hp, op_ = pieces[i - 2]
                        g.wait_ge(ctx.mm.h, S["mm_p", hp, op_ * 2 + 1])
                    g.tensor_copy(wbuf[i % 2][:], wstg[i % 2][:]).then_inc(ctx.gp.h, 1)
                    if i + 2 < len(pieces):
                        g.wait_ge(ctx.gp.h, S["gp_w", hf, ob])
                        pload(i + 2)
                for jj in range(MB):
                    j = MB + jj
                    tb = j % 2
                    u = jj % 2
                    g.wait_ge(ctx.sc.h, S["sc_mb", 1])
                    if jj >= 2:
                        g.wait_ge(ostpl[u].h, ostp_[u] + 32 * (jj // 2))
                    g.wait_ge(ctx.gp.h, S["gp_ap", 1, j] - 2)
                    g.tensor_sub(tmpp[:], rres[:, rslot(1, j), :], mbc[tb][:]).then_inc(ctx.gp.h, 1)
                    g.wait_ge(ctx.gp.h, S["gp_ap", 1, j] - 1)
                    g.tensor_mul(ofsp[u][:], tmpp[:], sbc[tb][:]).then_inc(ctx.gp.h, 1)
                ctx.gp.n = gp_end

            @blk.tensor
            def _(t):
                for hf in range(2):
                    for ob in range(MB):
                        t.wait_ge(ctx.gp.h, S["gp_w", hf, ob])
                        if ob == 0:
                            t.wait_ge(rhsl[hf].h, rh_[hf] + 16)
                        for tb in range(2):
                            j = ob * 2 + tb
                            bank = (ob % 2) * 2 + tb
                            if ob >= 2:
                                t.wait_ge(ctx.sc.h, S["sc_y", hf, (ob - 2) * 2 + tb])
                            for kc in range(KC):
                                ins = t.matmul(
                                    pb[bank][:], wbuf[(hf * MB + ob) % 2][:, kc, :],
                                    rh[:, kc, tb * 512:(tb + 1) * 512],
                                    start=(kc == 0), stop=(kc == KC - 1),
                                )
                            ins.then_inc(ctx.mm.h, 1)
                        for jj in ([(ob - 2) * 2, (ob - 2) * 2 + 1] if ob >= 2 else []):
                            ob2, tb2 = jj // 2, jj % 2
                            t.wait_ge(ctx.sc.h, S["sc_q", hf, jj])
                            if hf == 1 and ob2 == 0:
                                t.wait_ge(ctx.sc.h, S["sc_mb", 0])
                            t.matmul(pbs[tb2][0:1, :], onescol[:], rb16[jj % 4][:],
                                     start=(ob2 == 0), stop=(ob2 == MB - 1))
                            t.matmul(pbs[tb2][32:33, :], onescol[:], rq16[jj % 4][:],
                                     start=(ob2 == 0), stop=(ob2 == MB - 1)).then_inc(ctx.mm.h, 2)
                    for jj in range(2 * MB - 4, 2 * MB):
                        ob2, tb2 = jj // 2, jj % 2
                        t.wait_ge(ctx.sc.h, S["sc_q", hf, jj])
                        t.matmul(pbs[tb2][0:1, :], onescol[:], rb16[jj % 4][:],
                                 start=(ob2 == 0), stop=(ob2 == MB - 1))
                        t.matmul(pbs[tb2][32:33, :], onescol[:], rq16[jj % 4][:],
                                 start=(ob2 == 0), stop=(ob2 == MB - 1)).then_inc(ctx.mm.h, 2)
                    t.wait_ge(ctx.ev.h, S["ev_f", hf, 1])
                    if hf == 0:
                        t.wait_ge(ctx.sc.h, S["sc_y", 0, 2 * MB - 1])
                    t.matmul(pbc[0][:], onesrow[:], srow[:, 2, :], start=True, stop=True)
                    t.matmul(pbc[1][:], onesrow[:], srow[:, 6, :], start=True, stop=True)
                    t.matmul(pbs[0][:], onesrow[:], srow[:, 10, :], start=True, stop=True)
                    t.matmul(pbs[1][:], onesrow[:], srow[:, 14, :],
                             start=True, stop=True).then_inc(ctx.mm.h, 4)
                ctx.mm.n = mm_end

            @blk.scalar
            def _(s):
                def store_pair(hf_, j_, pool):
                    ob_, tb_ = j_ // 2, j_ % 2
                    t0_ = hf_ * 1024
                    c0, c1 = t0_ + tb_ * 512, t0_ + (tb_ + 1) * 512
                    s.wait_ge(ctx.sc.h, S["sc_o", hf_, j_])
                    if pool:
                        u = (j_ - MB) % 2
                        s.dma_start(out=xnew_f[ob_ * 128:(ob_ + 1) * 128, c0:c1],
                                    in_=ofsp[u][:]).then_inc(ostpl[u].h, 16)
                        s.dma_start(out=xnew_b[ob_ * 128:(ob_ + 1) * 128, c0:c1],
                                    in_=obsp[u][:]).then_inc(ostpl[u].h, 16)
                        ostpl[u].n += 32
                    else:
                        s.dma_start(out=xnew_f[ob_ * 128:(ob_ + 1) * 128, c0:c1],
                                    in_=ofst[j_ % 2][:]).then_inc(ostsl[j_ % 2].h, 16)
                        s.dma_start(out=xnew_b[ob_ * 128:(ob_ + 1) * 128, c0:c1],
                                    in_=obst[j_ % 2][:]).then_inc(ostsl[j_ % 2].h, 16)
                        ostsl[j_ % 2].n += 32

                def cast_o(hf_, j_):
                    if hf_ == 1 and j_ >= MB:
                        u = (j_ - MB) % 2
                        jj = j_ - MB
                        s.wait_ge(ctx.gp.h, S["gp_ap", 1, j_])
                        if jj >= 2:
                            s.wait_ge(ostpl[u].h, ostp_[u] + 32 * (jj // 2))
                        s.activation(obsp[u][:], ofsp[u][:],
                                     AF.Identity).then_inc(ctx.sc.h, 1)
                        return
                    a = hf_ * 2 * MB + j_ if hf_ == 0 else j_
                    s.wait_ge(ctx.ev.h, S["ev_ap", hf_, j_])
                    if hf_ == 0:
                        if j_ >= 2:
                            s.wait_ge(ostsl[j_ % 2].h, ost_[j_ % 2] + 32 * (j_ // 2))
                    else:
                        s.wait_ge(ostsl[j_ % 2].h, ost_[j_ % 2] + 32 * (MB + j_ // 2))
                    s.activation(obst[j_ % 2][:], ofst[j_ % 2][:],
                                 AF.Identity).then_inc(ctx.sc.h, 1)

                def cast_sq(hf, jj):
                    s.wait_ge(ctx.ev.h, S["ev_a", hf, jj])
                    if jj >= 4:
                        s.wait_ge(ctx.mm.h, S["mm_s", hf, jj - 4])
                    elif hf == 1:
                        s.wait_ge(ctx.mm.h, S["mm_s", 0, 2 * MB - 4 + jj])
                    s.activation(rb16[jj % 4][:], rres[:, rslot(hf, jj), :],
                                 AF.Identity).then_inc(ctx.sc.h, 1)
                    s.activation(rq16[jj % 4][:], rres[:, rslot(hf, jj), :],
                                 AF.Square).then_inc(ctx.sc.h, 1)

                for hf in range(2):
                    for j in range(2 * MB):
                        ob = j // 2
                        s.wait_ge(ctx.mm.h, S["mm_p", hf, j])
                        jy = hf * 2 * MB + j
                        if jy >= 2:
                            hp, jp = divmod(jy - 2, 2 * MB)
                            s.wait_ge(ctx.ev.h, S["ev_a", hp, jp])
                        bank = (ob % 2) * 2 + j % 2
                        if bias_col is not None:
                            s.activation(ysb[j % 2][:], pb[bank][:], AF.Identity,
                                         bias=bias_col[:, ob:ob + 1], scale=1.0).then_inc(ctx.sc.h, 1)
                        else:
                            s.activation(ysb[j % 2][:], pb[bank][:],
                                         AF.Identity).then_inc(ctx.sc.h, 1)
                        if j >= 1:
                            cast_sq(hf, j - 1)
                            if hf == 1:
                                cast_o(0, j - 1)
                                store_pair(0, j - 1, False)
                    cast_sq(hf, 2 * MB - 1)
                    if hf == 1:
                        cast_o(0, 2 * MB - 1)
                        store_pair(0, 2 * MB - 1, False)
                    for tb in range(2):
                        base = 8 * tb
                        s.wait_ge(ctx.ev.h, S["ev_f", hf, tb] - 3)
                        s.activation(srow[:, 3 + base, :], srow[:, 2 + base, :],
                                     AF.Square).then_inc(ctx.sc.h, 1)
                        s.wait_ge(ctx.ev.h, S["ev_f", hf, tb] - 1)
                        s.activation(srow[:, 5 + base, :], srow[:, 4 + base, :],
                                     AF.Sqrt).then_inc(ctx.sc.h, 1)
                    s.wait_ge(ctx.mm.h, S["mm_bc", hf])
                    if hf == 1:
                        s.wait_ge(ctx.ev.h, S["ev_ap", 0, 2 * MB - 1])
                    s.activation(mbc[0][:], pbc[0][:], AF.Identity).then_inc(ctx.sc.h, 1)
                    s.activation(sbc[0][:], pbc[1][:], AF.Identity).then_inc(ctx.sc.h, 1)
                    s.activation(mbc[1][:], pbs[0][:], AF.Identity).then_inc(ctx.sc.h, 1)
                    s.activation(sbc[1][:], pbs[1][:], AF.Identity).then_inc(ctx.sc.h, 1)
                for jj in range(MB):
                    cast_o(1, jj)
                    store_pair(1, jj, False)
                    cast_o(1, MB + jj)
                    store_pair(1, MB + jj, True)
                for s_ in ostsl + ostpl:
                    s.wait_ge(s_.h, s_.n)
                ctx.sc.n = sc_end

            @blk.vector
            def _(v):
                myev = ctx.ev.n

                def vop(ins):
                    nonlocal myev
                    ins.then_inc(ctx.ev.h, 1)
                    myev += 1

                def add_op(hf, j):
                    jx = hf * 2 * MB + j
                    v.wait_ge(ctx.sc.h, S["sc_y", hf, j])
                    v.wait_ge(xsl[jx % 4].h, xb_[jx % 4] + 16 * (jx // 4 + 1))
                    if hf == 1 and j >= 2:
                        v.wait_ge(ctx.ev.h, S["ev_ap", 0, j - 2])
                    vop(v.tensor_add(rres[:, rslot(hf, j), :], ysb[j % 2][:], xsb[jx % 4][:]))

                def fin(hf):
                    for tb in range(2):
                        base = 8 * tb
                        v.wait_ge(ctx.mm.h, S["mm_s", hf, 2 * MB - 1])
                        vop(v.tensor_copy(srow[:, 0 + base, :], pbs[tb][0:1, :]))
                        vop(v.tensor_copy(srow[:, 1 + base, :], pbs[tb][32:33, :]))
                        v.wait_ge(ctx.ev.h, myev)
                        vop(v.tensor_scalar_mul(srow[:, 2 + base, :], srow[:, 0 + base, :], 1.0 / D))
                        v.wait_ge(ctx.sc.h, S["sc_m2", hf, tb])
                        v.wait_ge(ctx.ev.h, myev)
                        vop(v.scalar_tensor_tensor(srow[:, 4 + base, :], srow[:, 1 + base, :],
                                                   1.0 / D, srow[:, 3 + base, :],
                                                   ALU.mult, ALU.subtract))
                        v.wait_ge(ctx.ev.h, myev)
                        vop(v.tensor_scalar_add(srow[:, 4 + base, :], srow[:, 4 + base, :],
                                                float(EPS)))
                        v.wait_ge(ctx.sc.h, S["sc_sd", hf, tb])
                        vop(v.reciprocal(srow[:, 6 + base, :], srow[:, 5 + base, :]))

                def apply_op(hf, j):
                    tb = j % 2
                    if j == 0:
                        v.wait_ge(ctx.sc.h, S["sc_mb", hf])
                    if hf == 0:
                        if j >= 2:
                            v.wait_ge(ostsl[j % 2].h, ost_[j % 2] + 32 * (j // 2))
                    else:
                        v.wait_ge(ostsl[j % 2].h, ost_[j % 2] + 32 * (MB + j // 2))
                    v.wait_ge(ctx.ev.h, myev)
                    vop(v.tensor_sub(tmp[j % 2][:], rres[:, rslot(hf, j), :], mbc[tb][:]))
                    v.wait_ge(ctx.ev.h, myev)
                    vop(v.tensor_mul(ofst[j % 2][:], tmp[j % 2][:], sbc[tb][:]))

                for j in range(2 * MB):
                    add_op(0, j)
                fin(0)
                for j in range(2 * MB):
                    apply_op(0, j)
                    add_op(1, j)
                fin(1)
                for j in range(MB):
                    apply_op(1, j)
                ctx.ev.n = ev_end
                assert myev == ev_end, (myev, ev_end)
    _bar(nc)


def _emit_proj_ln3(nc, ctx, name, *, rhs_sb, K_, w_hbm, bias_col, xold_hbm,
                   rbuf, xnew_f, xnew_b, onescol, onesrow, hf, xb_off=0):
    KC = K_ // 128
    MB = DC
    t0 = hf * 1024
    PW = 8
    NP = KC // PW

    def idx4(j):
        return (j % 2) * 2 + (j // 2) % 2

    with ExitStack() as st:
        wbuf = [st.enter_context(nc.sbuf_tensor(f"{name}w{i}", [128, KC, 128], BF16)) for i in range(3)]
        wstg = [st.enter_context(nc.sbuf_tensor(f"{name}wf{i}", [128, PW, 128], F32)) for i in range(4)]
        xsb = [st.enter_context(nc.sbuf_tensor(f"{name}x{i}", [128, 512], F32)) for i in range(4)]
        ysb = [st.enter_context(nc.sbuf_tensor(f"{name}y{i}", [128, 512], F32)) for i in range(4)]
        rsb = [st.enter_context(nc.sbuf_tensor(f"{name}r{i}", [128, 512], F32)) for i in range(4)]
        rb16 = [st.enter_context(nc.sbuf_tensor(f"{name}rb{i}", [128, 512], BF16)) for i in range(4)]
        rq16 = [st.enter_context(nc.sbuf_tensor(f"{name}rq{i}", [128, 512], BF16)) for i in range(4)]
        rsb2 = [st.enter_context(nc.sbuf_tensor(f"{name}r2{i}", [128, 512], F32)) for i in range(4)]
        tmp = [st.enter_context(nc.sbuf_tensor(f"{name}tp{i}", [128, 512], F32)) for i in range(2)]
        ofst = [st.enter_context(nc.sbuf_tensor(f"{name}of{i}", [128, 512], F32)) for i in range(4)]
        obst = [st.enter_context(nc.sbuf_tensor(f"{name}ob{i}", [128, 512], BF16)) for i in range(4)]
        srow = st.enter_context(nc.sbuf_tensor(f"{name}sr", [1, 16, 512], F32))
        mbc = [st.enter_context(nc.sbuf_tensor(f"{name}mb{i}", [128, 512], F32)) for i in range(2)]
        sbc = [st.enter_context(nc.sbuf_tensor(f"{name}sc{i}", [128, 512], F32)) for i in range(2)]
        pp = [st.enter_context(nc.psum_tensor(f"{name}pp{i}", [128, 512], F32)) for i in range(6)]
        pbs = [st.enter_context(nc.psum_tensor(f"{name}pq{i}", [128, 512], F32)) for i in range(2)]

        wsl = [ctx.slot[0], ctx.slot[1],
               Sem(nc, st, f"{name}w2"), Sem(nc, st, f"{name}w3")]
        rstsl = [ctx.slot[2], ctx.slot[3],
                 Sem(nc, st, f"{name}rs2"), Sem(nc, st, f"{name}rs3")]
        ostsl = [ctx.slot[4], ctx.slot[5],
                 Sem(nc, st, f"{name}os2"), Sem(nc, st, f"{name}os3")]
        xsl = [ctx.slot[6 + i] for i in range(4)]
        rldsl = [ctx.slot[10], ctx.slot[11],
                 Sem(nc, st, f"{name}rl2"), Sem(nc, st, f"{name}rl3")]
        wb_ = [s.n for s in wsl]
        rst_ = [s.n for s in rstsl]
        ost_ = [s.n for s in ostsl]
        xb_ = [s.n for s in xsl]
        rld_ = [s.n for s in rldsl]

        S = {}
        mmv, scv, evv, gpv = ctx.mm.n, ctx.sc.n, ctx.ev.n, ctx.gp.n
        def statjs(ob):
            if ob < 2:
                return []
            return [(ob - 2) * 2, (ob - 2) * 2 + 1]
        for ob in range(MB):
            for tb in range(2):
                mmv += 1; S["mm_p", ob * 2 + tb] = mmv
            for jj in statjs(ob):
                mmv += 2; S["mm_s", jj] = mmv
        for jj in range(2 * MB - 4, 2 * MB):
            mmv += 2; S["mm_s", jj] = mmv
        mmv += 4; S["mm_bc"] = mmv
        for ob in range(MB):
            for p in range(NP):
                i = ob * NP + p
                if i % 2 == 1:
                    scv += 1; S["sc_w", i] = scv
            for j in ([2 * (ob - 1), 2 * (ob - 1) + 1] if ob >= 1 else []):
                scv += 1; S["sc_y", j] = scv
                if j >= 1:
                    scv += 1; S["sc_b", j - 1] = scv
                    scv += 1; S["sc_q", j - 1] = scv
        for j in (2 * MB - 2, 2 * MB - 1):
            scv += 1; S["sc_y", j] = scv
            scv += 1; S["sc_b", j - 1] = scv
            scv += 1; S["sc_q", j - 1] = scv
        scv += 1; S["sc_b", 2 * MB - 1] = scv
        scv += 1; S["sc_q", 2 * MB - 1] = scv
        for tb in range(2):
            scv += 1; S["sc_m2", tb] = scv
            scv += 1; S["sc_sd", tb] = scv
        scv += 4; S["sc_mb"] = scv
        for j in range(2 * MB):
            scv += 1; S["sc_o", j] = scv
        for j in range(2 * MB):
            evv += 1; S["ev_a", j] = evv
        for tb in range(2):
            evv += 6; S["ev_f", tb] = evv
        for j in range(0, 2 * MB, 2):
            evv += 2; S["ev_ap", j] = evv
        for i in range(MB * NP):
            if i % 2 == 0:
                gpv += 1; S["gp_w", i] = gpv
        for j in range(1, 2 * MB, 2):
            gpv += 2; S["gp_ap", j] = gpv
        mm_end, sc_end, ev_end, gp_end = mmv, scv, evv, gpv

        with nc.Block(f"{name}") as blk:
            @blk.sync
            def _(sy):
                for j in range(2 * MB):
                    ob, tb = j // 2, j % 2
                    if j >= 4:
                        sy.wait_ge(ctx.ev.h, S["ev_a", j - 4])
                    sy.dma_start(
                        out=xsb[j % 4][:],
                        in_=xold_hbm[ob * 128:(ob + 1) * 128, t0 + tb * 512:t0 + (tb + 1) * 512],
                    ).then_inc(xsl[j % 4].h, 16)
                    xsl[j % 4].n += 16
                for j in range(2 * MB):
                    ob, tb = j // 2, j % 2
                    u = idx4(j)
                    sy.wait_ge(rstsl[j % 4].h, rst_[j % 4] + 16 * (j // 4 + 1))
                    if j >= 4:
                        if j % 2 == 0:
                            sy.wait_ge(ctx.ev.h, S["ev_ap", j - 4])
                        else:
                            sy.wait_ge(ctx.gp.h, S["gp_ap", j - 4])
                    sy.dma_start(
                        out=rsb2[u][:],
                        in_=rbuf[ob * 128:(ob + 1) * 128, t0 + tb * 512:t0 + (tb + 1) * 512],
                    ).then_inc(rldsl[u].h, 16)
                    rldsl[u].n += 16

            @blk.gpsimd
            def _(g):
                pieces = [(ob, p) for ob in range(MB) for p in range(NP)]

                def rstore(j):
                    ob_, tb_ = j // 2, j % 2
                    g.wait_ge(ctx.ev.h, S["ev_a", j])
                    g.dma_start(
                        out=rbuf[ob_ * 128:(ob_ + 1) * 128, t0 + tb_ * 512:t0 + (tb_ + 1) * 512],
                        in_=rsb[j % 4][:]).then_inc(rstsl[j % 4].h, 16)
                    rstsl[j % 4].n += 16

                def pload(i):
                    ob_, p_ = pieces[i]
                    g.dma_start(out=wstg[i % 4][:],
                                in_=w_hbm[ob_][:, p_ * PW:(p_ + 1) * PW, :]).then_inc(wsl[i % 4].h, 16)
                    wsl[i % 4].n += 16

                for i in range(min(4, len(pieces))):
                    pload(i)
                for i, (ob, p) in enumerate(pieces):
                    if i % 2 == 0:
                        g.wait_ge(wsl[i % 4].h, wb_[i % 4] + 16 * (i // 4 + 1))
                        if p == 0 and ob >= 2:
                            g.wait_ge(ctx.mm.h, S["mm_p", (ob - 2) * 2 + 1])
                        g.tensor_copy(wbuf[ob % 3][:, p * PW:(p + 1) * PW, :],
                                      wstg[i % 4][:]).then_inc(ctx.gp.h, 1)
                    if i + 4 < len(pieces):
                        if i % 2 == 0:
                            g.wait_ge(ctx.gp.h, S["gp_w", i])
                        else:
                            g.wait_ge(ctx.sc.h, S["sc_w", i])
                        pload(i + 4)
                    if p == NP - 1 and ob >= 1:
                        rstore(2 * (ob - 1))
                        rstore(2 * (ob - 1) + 1)
                rstore(2 * MB - 2)
                rstore(2 * MB - 1)
                for j in range(2 * MB):
                    ob, tb = j // 2, j % 2
                    u = idx4(j)
                    c0, c1 = t0 + tb * 512, t0 + (tb + 1) * 512
                    if j % 2 == 1:
                        g.wait_ge(ctx.sc.h, S["sc_mb"])
                        g.wait_ge(rldsl[u].h, rld_[u] + 16 * (j // 4 + 1))
                        if j >= 4:
                            g.wait_ge(ostsl[u].h, ost_[u] + 32 * (j // 4))
                        g.tensor_sub(tmp[1][:], rsb2[u][:], mbc[tb][:]).then_inc(ctx.gp.h, 1)
                        g.wait_ge(ctx.gp.h, S["gp_ap", j] - 1)
                        g.tensor_mul(ofst[u][:], tmp[1][:], sbc[tb][:]).then_inc(ctx.gp.h, 1)
                        g.wait_ge(ctx.gp.h, S["gp_ap", j])
                    else:
                        g.wait_ge(ctx.ev.h, S["ev_ap", j])
                    g.wait_ge(ctx.sc.h, S["sc_o", j])
                    g.dma_start(out=xnew_f[ob * 128:(ob + 1) * 128, c0:c1],
                                in_=ofst[u][:]).then_inc(ostsl[u].h, 16)
                    g.dma_start(out=xnew_b[ob * 128:(ob + 1) * 128,
                                            c0 - xb_off:c1 - xb_off],
                                in_=obst[u][:]).then_inc(ostsl[u].h, 16)
                    ostsl[u].n += 32
                for s_ in ostsl:
                    g.wait_ge(s_.h, s_.n)
                ctx.gp.n = gp_end

            @blk.tensor
            def _(t):
                for ob in range(MB):
                    t.wait_ge(ctx.gp.h, S["gp_w", ob * NP + NP - 2])
                    t.wait_ge(ctx.sc.h, S["sc_w", ob * NP + NP - 1])
                    for tb in range(2):
                        j = ob * 2 + tb
                        bank = (ob % 3) * 2 + tb
                        if ob >= 3:
                            t.wait_ge(ctx.sc.h, S["sc_y", (ob - 3) * 2 + tb])
                        for kc in range(KC):
                            ins = t.matmul(
                                pp[bank][:], wbuf[ob % 3][:, kc, :],
                                rhs_sb[:, kc, tb * 512:(tb + 1) * 512],
                                start=(kc == 0), stop=(kc == KC - 1),
                            )
                        ins.then_inc(ctx.mm.h, 1)
                    for jj in statjs(ob):
                        ob2, tb2 = jj // 2, jj % 2
                        t.wait_ge(ctx.sc.h, S["sc_q", jj])
                        t.matmul(pbs[tb2][0:1, :], onescol[:], rb16[jj % 4][:],
                                 start=(ob2 == 0), stop=(ob2 == MB - 1))
                        t.matmul(pbs[tb2][32:33, :], onescol[:], rq16[jj % 4][:],
                                 start=(ob2 == 0), stop=(ob2 == MB - 1)).then_inc(ctx.mm.h, 2)
                for jj in range(2 * MB - 4, 2 * MB):
                    ob2, tb2 = jj // 2, jj % 2
                    t.wait_ge(ctx.sc.h, S["sc_q", jj])
                    t.matmul(pbs[tb2][0:1, :], onescol[:], rb16[jj % 4][:],
                             start=(ob2 == 0), stop=(ob2 == MB - 1))
                    t.matmul(pbs[tb2][32:33, :], onescol[:], rq16[jj % 4][:],
                             start=(ob2 == 0), stop=(ob2 == MB - 1)).then_inc(ctx.mm.h, 2)
                t.wait_ge(ctx.ev.h, S["ev_f", 1])
                t.wait_ge(ctx.sc.h, S["sc_y", 2 * MB - 1])
                t.matmul(pp[0][:], onesrow[:], srow[:, 2, :], start=True, stop=True)
                t.matmul(pp[2][:], onesrow[:], srow[:, 6, :], start=True, stop=True)
                t.matmul(pp[1][:], onesrow[:], srow[:, 10, :], start=True, stop=True)
                t.matmul(pp[3][:], onesrow[:], srow[:, 14, :],
                         start=True, stop=True).then_inc(ctx.mm.h, 4)
                ctx.mm.n = mm_end

            @blk.scalar
            def _(s):
                def cast_sq(jj):
                    s.wait_ge(ctx.ev.h, S["ev_a", jj])
                    if jj >= 4:
                        s.wait_ge(ctx.mm.h, S["mm_s", jj - 4])
                    s.activation(rb16[jj % 4][:], rsb[jj % 4][:], AF.Identity).then_inc(ctx.sc.h, 1)
                    s.activation(rq16[jj % 4][:], rsb[jj % 4][:], AF.Square).then_inc(ctx.sc.h, 1)

                def evac(j):
                    ob_ = j // 2
                    bank = (ob_ % 3) * 2 + j % 2
                    s.wait_ge(ctx.mm.h, S["mm_p", j])
                    if j >= 4:
                        s.wait_ge(ctx.ev.h, S["ev_a", j - 4])
                    if bias_col is not None:
                        s.activation(ysb[j % 4][:], pp[bank][:], AF.Identity,
                                     bias=bias_col[:, ob_:ob_ + 1], scale=1.0).then_inc(ctx.sc.h, 1)
                    else:
                        s.activation(ysb[j % 4][:], pp[bank][:],
                                     AF.Identity).then_inc(ctx.sc.h, 1)

                for ob in range(MB):
                    for p in range(NP):
                        i = ob * NP + p
                        if i % 2 == 1:
                            s.wait_ge(wsl[i % 4].h, wb_[i % 4] + 16 * (i // 4 + 1))
                            if p == 0 and ob >= 2:
                                s.wait_ge(ctx.mm.h, S["mm_p", (ob - 2) * 2 + 1])
                            s.activation(wbuf[ob % 3][:, p * PW:(p + 1) * PW, :],
                                         wstg[i % 4][:], AF.Identity).then_inc(ctx.sc.h, 1)
                    for j in ([2 * (ob - 1), 2 * (ob - 1) + 1] if ob >= 1 else []):
                        evac(j)
                        if j >= 1:
                            cast_sq(j - 1)
                for j in (2 * MB - 2, 2 * MB - 1):
                    evac(j)
                    cast_sq(j - 1)
                cast_sq(2 * MB - 1)
                for tb in range(2):
                    base = 8 * tb
                    s.wait_ge(ctx.ev.h, S["ev_f", tb] - 3)
                    s.activation(srow[:, 3 + base, :], srow[:, 2 + base, :],
                                 AF.Square).then_inc(ctx.sc.h, 1)
                    s.wait_ge(ctx.ev.h, S["ev_f", tb] - 1)
                    s.activation(srow[:, 5 + base, :], srow[:, 4 + base, :],
                                 AF.Sqrt).then_inc(ctx.sc.h, 1)
                s.wait_ge(ctx.mm.h, S["mm_bc"])
                s.activation(mbc[0][:], pp[0][:], AF.Identity).then_inc(ctx.sc.h, 1)
                s.activation(mbc[1][:], pp[1][:], AF.Identity).then_inc(ctx.sc.h, 1)
                s.activation(sbc[0][:], pp[2][:], AF.Identity).then_inc(ctx.sc.h, 1)
                s.activation(sbc[1][:], pp[3][:], AF.Identity).then_inc(ctx.sc.h, 1)
                for j in range(2 * MB):
                    u = idx4(j)
                    if j % 2 == 1:
                        s.wait_ge(ctx.gp.h, S["gp_ap", j])
                    else:
                        s.wait_ge(ctx.ev.h, S["ev_ap", j])
                    if j >= 4:
                        s.wait_ge(ostsl[u].h, ost_[u] + 32 * (j // 4))
                    s.activation(obst[u][:], ofst[u][:],
                                 AF.Identity).then_inc(ctx.sc.h, 1)
                ctx.sc.n = sc_end

            @blk.vector
            def _(v):
                myev = ctx.ev.n

                def vop(ins):
                    nonlocal myev
                    ins.then_inc(ctx.ev.h, 1)
                    myev += 1

                for j in range(2 * MB):
                    v.wait_ge(ctx.sc.h, S["sc_y", j])
                    v.wait_ge(xsl[j % 4].h, xb_[j % 4] + 16 * (j // 4 + 1))
                    if j >= 4:
                        v.wait_ge(rstsl[j % 4].h, rst_[j % 4] + 16 * (j // 4))
                        v.wait_ge(ctx.sc.h, S["sc_q", j - 4])
                    vop(v.tensor_add(rsb[j % 4][:], ysb[j % 4][:], xsb[j % 4][:]))
                for tb in range(2):
                    base = 8 * tb
                    v.wait_ge(ctx.mm.h, S["mm_s", 2 * MB - 1])
                    vop(v.tensor_copy(srow[:, 0 + base, :], pbs[tb][0:1, :]))
                    vop(v.tensor_copy(srow[:, 1 + base, :], pbs[tb][32:33, :]))
                    v.wait_ge(ctx.ev.h, myev)
                    vop(v.tensor_scalar_mul(srow[:, 2 + base, :], srow[:, 0 + base, :], 1.0 / D))
                    v.wait_ge(ctx.sc.h, S["sc_m2", tb])
                    v.wait_ge(ctx.ev.h, myev)
                    vop(v.scalar_tensor_tensor(srow[:, 4 + base, :], srow[:, 1 + base, :],
                                               1.0 / D, srow[:, 3 + base, :],
                                               ALU.mult, ALU.subtract))
                    v.wait_ge(ctx.ev.h, myev)
                    vop(v.tensor_scalar_add(srow[:, 4 + base, :], srow[:, 4 + base, :],
                                            float(EPS)))
                    v.wait_ge(ctx.sc.h, S["sc_sd", tb])
                    vop(v.reciprocal(srow[:, 6 + base, :], srow[:, 5 + base, :]))
                for j in range(0, 2 * MB, 2):
                    tb = j % 2
                    u = idx4(j)
                    if j == 0:
                        v.wait_ge(ctx.sc.h, S["sc_mb"])
                    v.wait_ge(rldsl[u].h, rld_[u] + 16 * (j // 4 + 1))
                    if j >= 4:
                        v.wait_ge(ostsl[u].h, ost_[u] + 32 * (j // 4))
                    v.wait_ge(ctx.ev.h, myev)
                    vop(v.tensor_sub(tmp[0][:], rsb2[u][:], mbc[tb][:]))
                    v.wait_ge(ctx.ev.h, myev)
                    vop(v.tensor_mul(ofst[u][:], tmp[0][:], sbc[tb][:]))
                ctx.ev.n = ev_end
                assert myev == ev_end, (myev, ev_end)
    _bar(nc)


def _emit_relu_proj(nc, ctx, name, *, rhs_sb, rhs_off, K_, M, w_hbm, bias_col,
                    out_sb, relu=True):
    KC = K_ // 128
    MB = M // 128
    NP = KC // 16
    pieces = [(ob, p) for ob in range(MB) for p in range(NP)]
    with ExitStack() as st:
        wbuf = [st.enter_context(nc.sbuf_tensor(f"{name}w{i}", [128, KC, 128], BF16)) for i in range(2)]
        wstg = [st.enter_context(nc.sbuf_tensor(f"{name}wf{i}", [128, 16, 128], F32)) for i in range(2)]
        pp = [st.enter_context(nc.psum_tensor(f"{name}p{i}", [128, 512], F32)) for i in range(4)]
        wsl = [ctx.slot[0], ctx.slot[1]]
        wb_ = [s.n for s in wsl]
        mm0, sc0, gp0 = ctx.mm.n, ctx.sc.n, ctx.gp.n
        with nc.Block(name) as blk:
            @blk.gpsimd
            def _(g):
                def pload(i):
                    ob_, p_ = pieces[i]
                    g.dma_start(out=wstg[i % 2][:],
                                in_=w_hbm[ob_][:, p_ * 16:(p_ + 1) * 16, :]).then_inc(wsl[i % 2].h, 16)
                    wsl[i % 2].n += 16
                for i in range(min(2, len(pieces))):
                    pload(i)
                for i, (ob, p) in enumerate(pieces):
                    g.wait_ge(wsl[i % 2].h, wb_[i % 2] + 16 * (i // 2 + 1))
                    if p == 0 and ob >= 2:
                        g.wait_ge(ctx.mm.h, mm0 + (ob - 1) * 2)
                    g.tensor_copy(wbuf[ob % 2][:, p * 16:(p + 1) * 16, :],
                                  wstg[i % 2][:]).then_inc(ctx.gp.h, 1)
                    if i + 2 < len(pieces):
                        g.wait_ge(ctx.gp.h, gp0 + i + 1)
                        pload(i + 2)
                ctx.gp.n += len(pieces)

            @blk.tensor
            def _(t):
                for ob in range(MB):
                    t.wait_ge(ctx.gp.h, gp0 + NP * (ob + 1))
                    if ob >= 2:
                        t.wait_ge(ctx.sc.h, sc0 + (ob - 1) * 2)
                    for tb in range(2):
                        bank = (ob % 2) * 2 + tb
                        for kc in range(KC):
                            ins = t.matmul(
                                pp[bank][:], wbuf[ob % 2][:, kc, :],
                                rhs_sb[:, kc, rhs_off + tb * 512:rhs_off + (tb + 1) * 512],
                                start=(kc == 0), stop=(kc == KC - 1),
                            )
                        ins.then_inc(ctx.mm.h, 1)
                ctx.mm.n += MB * 2

            @blk.scalar
            def _(s):
                for ob in range(MB):
                    for tb in range(2):
                        bank = (ob % 2) * 2 + tb
                        s.wait_ge(ctx.mm.h, mm0 + ob * 2 + tb + 1)
                        s.activation(out_sb[:, ob, tb * 512:(tb + 1) * 512], pp[bank][:],
                                     AF.Relu if relu else AF.Identity,
                                     bias=bias_col[:, ob:ob + 1], scale=1.0).then_inc(ctx.sc.h, 1)
                ctx.sc.n += MB * 2
    _bar(nc)


def _emit_ffn(nc, ctx, x1b, x1f, W1T, b1c, W2T, b2c, W3T, b3c, rbuf, x2f, x2b,
              onescol, onesrow, x2b_halves=None, post_half=None):
    for hf in range(2):
        t0 = hf * 1024
        with ExitStack() as st:
            h2 = st.enter_context(nc.sbuf_tensor(f"ffh2_{hf}", [128, 32, 1024], BF16))
            with ExitStack() as st2:
                h1 = st2.enter_context(nc.sbuf_tensor(f"ffh1_{hf}", [128, 32, 1024], BF16))
                xh = st2.enter_context(nc.sbuf_tensor(f"ffx_{hf}", [128, 16, 1024], BF16))
                with nc.Block(f"ffl{hf}") as blk:
                    @blk.sync
                    def _(sy):
                        sy.dma_start(
                            out=xh[:],
                            in_=x1b[:, t0:t0 + 1024].rearrange("(c p) t -> p c t", p=128),
                        ).then_inc(ctx.ld_s.h, 16)
                        ctx.ld_s.n += 16
                        sy.wait_ge(ctx.ld_s.h, ctx.ld_s.n)
                _bar(nc)
                _emit_relu_proj(nc, ctx, f"ff1_{hf}", rhs_sb=xh, rhs_off=0, K_=D,
                                M=4096, w_hbm=W1T, bias_col=b1c, out_sb=h1)
                _emit_relu_proj(nc, ctx, f"ff2_{hf}", rhs_sb=h1, rhs_off=0, K_=4096,
                                M=4096, w_hbm=W2T, bias_col=b2c, out_sb=h2)
            xb = x2b if x2b_halves is None else x2b_halves[hf]
            xoff = 0 if x2b_halves is None else hf * 1024
            _emit_proj_ln3(nc, ctx, f"ff3_{hf}", rhs_sb=h2, K_=4096,
                           w_hbm=W3T, bias_col=b3c, xold_hbm=x1f, rbuf=rbuf,
                           xnew_f=x2f, xnew_b=xb, onescol=onescol,
                           onesrow=onesrow, hf=hf, xb_off=xoff)
        if post_half is not None:
            post_half(hf)


def build_launch_C(dbg=False):
    nc = bass.Bass("TRN2", target_bir_lowering=False, debug=False)
    x2all = nc.dram_tensor("x2all", [4 * D, TPC], BF16, kind="ExternalInput").ap()
    wqT = nc.dram_tensor("wqTt", [4, 128, 16, 128], F32, kind="ExternalInput").ap()
    wkT = nc.dram_tensor("wkTt", [4, 128, 16, 128], F32, kind="ExternalInput").ap()
    wvT = nc.dram_tensor("wvTt1", [1, 128, 16, 512], F32, kind="ExternalInput").ap()

    oT1 = nc.dram_tensor("oT1r", [2048, TPC], BF16, kind="ExternalOutput").ap()
    ik = "ExternalOutput" if dbg else "Internal"
    qT1 = nc.dram_tensor("qT1", [512, N], BF16, kind=ik).ap()
    kT1 = nc.dram_tensor("kT1", [512, N], BF16, kind=ik).ap()
    v1 = nc.dram_tensor("v1h", [4, 128, N // 128, 128], BF16, kind=ik).ap()

    onesrow_t = nc.inline_tensor(ONES_ROW_F32, "c_onesrow").ap()
    onescol_t = nc.inline_tensor(ONES_COL_B16, "c_onescol").ap()
    diagm_t = nc.inline_tensor(L1_DIAG_MASKS, "c_diagm").ap()

    with ExitStack() as stack:
        ctx = Ctx(nc, stack)
        onesrow = ctx.sb("onesrow", [1, 128], F32)
        onescol = ctx.sb("onescol", [128, 1], BF16)
        diagm = ctx.sb("diagm", [128, 4, 512], BF16)

        with nc.Block("C0") as blk:
            @blk.gpsimd
            def _(g):
                for dst, src in [(onesrow[:], onesrow_t), (onescol[:], onescol_t),
                                 (diagm[:], diagm_t.rearrange("j p t -> p j t"))]:
                    g.dma_start(out=dst, in_=src).then_inc(ctx.ld_g.h, 16)
                    ctx.ld_g.n += 16
                g.wait_ge(ctx.ld_g.h, ctx.ld_g.n)
        _bar(nc)

        _emit_c1_all(nc, ctx, x2all, wqT, wkT, wvT, qT1, kT1, v1)

        _emit_l1_attn(nc, ctx, qT1, kT1, v1, oT1, onesrow, onescol, diagm)
        _bar(nc)
    return nc


def _emit_qk_fm_c(nc, ctx, r, xr, wqT, wkT, qT1, kT1):
    NB_T = TPC // 512
    with ExitStack() as st:
        wbuf = [st.enter_context(nc.sbuf_tensor(f"c1w{r}{i}", [128, DC, 128], BF16)) for i in range(2)]
        wstg = [st.enter_context(nc.sbuf_tensor(f"c1wf{r}{i}", [128, DC, 128], F32)) for i in range(2)]
        stage = [st.enter_context(nc.sbuf_tensor(f"c1s{r}{i}", [128, TPC], BF16)) for i in range(2)]
        ps = [st.enter_context(nc.psum_tensor(f"c1p{r}{i}", [128, 512], F32)) for i in range(8)]
        wsl = [ctx.slot[0], ctx.slot[1]]
        ssl = [ctx.slot[6], ctx.slot[7]]
        wbase = [s.n for s in wsl]
        sbase = [s.n for s in ssl]
        mm0, sc0, gp0 = ctx.mm.n, ctx.sc.n, ctx.gp.n
        with nc.Block(f"C1qk{r}") as blk:
            @blk.gpsimd
            def _(g):
                for cb in range(2):
                    w = wqT if cb < 4 else wkT
                    g.dma_start(out=wstg[cb % 2][:], in_=w[cb % 4]).then_inc(wsl[cb % 2].h, 16)
                    wsl[cb % 2].n += 16
                for cb in range(8):
                    g.wait_ge(wsl[cb % 2].h, wbase[cb % 2] + 16 * (cb // 2 + 1))
                    if cb >= 2:
                        g.wait_ge(ctx.mm.h, mm0 + (cb - 1) * NB_T)
                    g.tensor_copy(wbuf[cb % 2][:], wstg[cb % 2][:]).then_inc(ctx.gp.h, 1)
                    if cb + 2 < 8:
                        g.wait_ge(ctx.gp.h, gp0 + cb + 1)
                        w2 = wqT if cb + 2 < 4 else wkT
                        g.dma_start(out=wstg[cb % 2][:], in_=w2[(cb + 2) % 4]).then_inc(wsl[cb % 2].h, 16)
                        wsl[cb % 2].n += 16
                ctx.gp.n += 8

            @blk.tensor
            def _(t):
                for cb in range(8):
                    t.wait_ge(ctx.gp.h, gp0 + cb + 1)
                    if cb >= 2:
                        t.wait_ge(ctx.sc.h, sc0 + (cb - 1) * NB_T)
                    bank = (cb % 2) * 4
                    for dc in range(DC):
                        for tb in range(NB_T):
                            ins = t.matmul(
                                ps[bank + tb][:], wbuf[cb % 2][:, dc, :],
                                xr[:, dc, tb * 512:(tb + 1) * 512],
                                start=(dc == 0), stop=(dc == DC - 1),
                            )
                            if dc == DC - 1:
                                ins.then_inc(ctx.mm.h, 1)
                ctx.mm.n += 8 * NB_T

            @blk.scalar
            def _(s):
                for cb in range(8):
                    bank = (cb % 2) * 4
                    if cb >= 2:
                        s.wait_ge(ssl[cb % 2].h, sbase[cb % 2] + 16 * (cb // 2))
                    for tb in range(NB_T):
                        s.wait_ge(ctx.mm.h, mm0 + cb * NB_T + tb + 1)
                        s.activation(stage[cb % 2][:, tb * 512:(tb + 1) * 512],
                                     ps[bank + tb][:], AF.Identity).then_inc(ctx.sc.h, 1)
                ctx.sc.n += 8 * NB_T

            @blk.sync
            def _(sy):
                for cb in range(8):
                    sy.wait_ge(ctx.sc.h, sc0 + (cb + 1) * NB_T)
                    dst = qT1 if cb < 4 else kT1
                    r0 = (cb % 4) * 128
                    sy.dma_start(out=dst[r0:r0 + 128, r * TPC:(r + 1) * TPC],
                                 in_=stage[cb % 2][:]).then_inc(ssl[cb % 2].h, 16)
                    ssl[cb % 2].n += 16
                sy.wait_ge(ssl[0].h, ssl[0].n)
                sy.wait_ge(ssl[1].h, ssl[1].n)


def _emit_v_tm_c(nc, ctx, r, xr, wvT, v1):
    NTC = TPC // 128
    with ExitStack() as st:
        wbuf = st.enter_context(nc.sbuf_tensor(f"c2w{r}", [128, DC, 512], BF16))
        wstg = st.enter_context(nc.sbuf_tensor(f"c2wf{r}", [128, DC, 512], F32))
        stage = [st.enter_context(nc.sbuf_tensor(f"c2s{r}{i}", [128, 512], BF16)) for i in range(4)]
        ps = [st.enter_context(nc.psum_tensor(f"c2p{r}{i}", [128, 512], F32)) for i in range(4)]
        wsl = ctx.slot[0]
        ssl = [ctx.slot[6], ctx.slot[7], ctx.slot[8], ctx.slot[9]]
        wbase = wsl.n
        sbase = [s.n for s in ssl]
        mm0, ev0, gp0 = ctx.mm.n, ctx.ev.n, ctx.gp.n
        with nc.Block(f"C1v{r}") as blk:
            @blk.gpsimd
            def _(g):
                g.dma_start(out=wstg[:], in_=wvT[0]).then_inc(wsl.h, 16)
                wsl.n += 16
                g.wait_ge(wsl.h, wbase + 16)
                g.tensor_copy(wbuf[:], wstg[:]).then_inc(ctx.gp.h, 1)
                ctx.gp.n += 1

            @blk.tensor
            def _(t):
                t.wait_ge(ctx.gp.h, gp0 + 1)
                for tc in range(NTC):
                    if tc >= 4:
                        t.wait_ge(ctx.ev.h, ev0 + tc - 3)
                    bank = tc % 4
                    for dc in range(DC):
                        ins = t.matmul(ps[bank][:], xr[:, dc, tc * 128:(tc + 1) * 128],
                                       wbuf[:, dc, :], start=(dc == 0), stop=(dc == DC - 1))
                    ins.then_inc(ctx.mm.h, 1)
                ctx.mm.n += NTC

            @blk.vector
            def _(v):
                for tc in range(NTC):
                    v.wait_ge(ctx.mm.h, mm0 + tc + 1)
                    if tc >= 4:
                        v.wait_ge(ssl[tc % 4].h, sbase[tc % 4] + 64 * (tc // 4))
                    v.tensor_copy(stage[tc % 4][:], ps[tc % 4][:]).then_inc(ctx.ev.h, 1)
                ctx.ev.n += NTC

            @blk.sync
            def _(sy):
                for tc in range(NTC):
                    sy.wait_ge(ctx.ev.h, ev0 + tc + 1)
                    for j in range(4):
                        sy.dma_start(out=v1[j, :, r * NTC + tc, :],
                                     in_=stage[tc % 4][:, j * 128:(j + 1) * 128]).then_inc(ssl[tc % 4].h, 16)
                    ssl[tc % 4].n += 64
                for s in ssl:
                    sy.wait_ge(s.h, s.n)


def _emit_c1_all(nc, ctx, x2all, wqT, wkT, wvT, qT1, kT1, v1, cc_gate=None):
    NU = 8

    with ExitStack() as st:
        xrh = [st.enter_context(nc.sbuf_tensor(f"c1x{i}", [128, DC, 1024], BF16)) for i in range(3)]
        wbufr = st.enter_context(nc.sbuf_tensor("c1wqk", [128, 8, DC, 128], BF16))
        wstg = [st.enter_context(nc.sbuf_tensor(f"c1wf{i}", [128, 8, 128], F32)) for i in range(4)]
        wvb = st.enter_context(nc.sbuf_tensor("c1wv", [128, DC, 512], BF16))
        wstgv = [st.enter_context(nc.sbuf_tensor(f"c1wg{i}", [128, 2, 512], F32)) for i in range(4)]
        qkst = [st.enter_context(nc.sbuf_tensor(f"c1s{i}", [128, 1024], BF16)) for i in range(2)]
        vst = [st.enter_context(nc.sbuf_tensor(f"c1t{i}", [128, 512], BF16)) for i in range(4)]
        ps = [st.enter_context(nc.psum_tensor(f"c1p{i}", [128, 512], F32)) for i in range(8)]

        lsl = [Sem(nc, st, f"c1l{i}") for i in range(3)]
        wsl = [ctx.slot[0], ctx.slot[1], ctx.slot[2], ctx.slot[3]]
        qsl = [ctx.slot[6], ctx.slot[7]]
        vsl = [ctx.slot[8], ctx.slot[9], Sem(nc, st, "c1v2"), Sem(nc, st, "c1v3")]
        wb_ = [s.n for s in wsl]
        qs_ = [s.n for s in qsl]
        vs_ = [s.n for s in vsl]

        S = {}
        mmv, scv, evv, gpv = ctx.mm.n, ctx.sc.n, ctx.ev.n, ctx.gp.n
        for r in range(4):
            for cb in range(8):
                for ph in range(2):
                    gpv += 1
                S["gp_qk", r, cb] = gpv
            for p in range(8):
                gpv += 1
            S["gp_v", r] = gpv
        for u in range(NU):
            for cb in range(8):
                for tb in range(2):
                    mmv += 1; S["mm_qk", u, cb, tb] = mmv
            for tci in range(8):
                mmv += 1; S["mm_v", u, tci] = mmv
        for u in range(NU):
            for cb in range(8):
                for tb in range(2):
                    scv += 1; S["sc_e", u, cb, tb] = scv
        for u in range(NU):
            for tci in range(8):
                evv += 1; S["ev_e", u, tci] = evv
        QC = {}; VC = {}
        qcnt = [qs_[0], qs_[1]]
        vcnt = list(vs_)
        for u in range(NU):
            for cb in range(8):
                qcnt[cb % 2] += 16
                QC[u, cb] = qcnt[cb % 2]
            for tci in range(8):
                vcnt[tci % 4] += 64
                VC[u, tci] = vcnt[tci % 4]
        mm_end, sc_end, ev_end, gp_end = mmv, scv, evv, gpv

        with nc.Block("C1") as blk:
            @blk.sync
            def _(sy):
                split = isinstance(x2all, (list, tuple))

                def load(u):
                    r, hh = u // 2, u % 2
                    if cc_gate is not None:
                        if split and u < 2:
                            sy.wait_ge(ctx.cc.h, cc_gate[hh])
                        elif (not split) and u == 0:
                            sy.wait_ge(ctx.cc.h, cc_gate)
                    if u >= 3:
                        sy.wait_ge(ctx.mm.h, S["mm_v", u - 3, 7])
                    if split:
                        srcap = x2all[hh][r * D:(r + 1) * D, :].rearrange(
                            "(c p) t -> p c t", p=128)
                    else:
                        srcap = x2all[r * D:(r + 1) * D,
                                      hh * 1024:(hh + 1) * 1024].rearrange("(c p) t -> p c t", p=128)
                    sy.dma_start(out=xrh[u % 3][:], in_=srcap).then_inc(lsl[u % 3].h, 16)
                    lsl[u % 3].n += 16

                for u in range(3):
                    load(u)
                for u in range(NU):
                    r, hh = u // 2, u % 2
                    for cb in range(8):
                        sy.wait_ge(ctx.sc.h, S["sc_e", u, cb, 1])
                        dst = qT1 if cb < 4 else kT1
                        r0 = (cb % 4) * 128
                        sy.dma_start(
                            out=dst[r0:r0 + 128, r * TPC + hh * 1024:r * TPC + (hh + 1) * 1024],
                            in_=qkst[cb % 2][:]).then_inc(qsl[cb % 2].h, 16)
                        qsl[cb % 2].n += 16
                    for tci in range(8):
                        sy.wait_ge(ctx.ev.h, S["ev_e", u, tci])
                        for j in range(4):
                            sy.dma_start(out=v1[j, :, r * 16 + hh * 8 + tci, :],
                                         in_=vst[tci % 4][:, j * 128:(j + 1) * 128]).then_inc(vsl[tci % 4].h, 16)
                        vsl[tci % 4].n += 64
                    if u + 3 < NU:
                        load(u + 3)
                for s_ in qsl + vsl:
                    sy.wait_ge(s_.h, s_.n)

            @blk.gpsimd
            def _(g):
                pieces = []
                for r in range(4):
                    for cb in range(8):
                        for ph in range(2):
                            pieces.append(("qk", r, cb, ph))
                    for p in range(8):
                        pieces.append(("v", r, p, 0))

                def pload(i):
                    kind = pieces[i]
                    if kind[0] == "qk":
                        _, r_, cb_, ph_ = kind
                        w = wqT if cb_ < 4 else wkT
                        g.dma_start(out=wstg[i % 4][:],
                                    in_=w[cb_ % 4][:, ph_ * 8:(ph_ + 1) * 8, :]).then_inc(wsl[i % 4].h, 16)
                    else:
                        _, r_, p_, _ = kind
                        g.dma_start(out=wstgv[p_ % 4][:],
                                    in_=wvT[0][:, p_ * 2:(p_ + 1) * 2, :]).then_inc(wsl[i % 4].h, 16)
                    wsl[i % 4].n += 16

                for i in range(4):
                    pload(i)
                gpc = ctx.gp.n
                for i, kind in enumerate(pieces):
                    g.wait_ge(wsl[i % 4].h, wb_[i % 4] + 16 * (i // 4 + 1))
                    if kind[0] == "qk":
                        _, r, cb, ph = kind
                        if r >= 1:
                            g.wait_ge(ctx.mm.h, S["mm_qk", 2 * (r - 1) + 1, cb, 1])
                        g.tensor_copy(wbufr[:, cb, ph * 8:(ph + 1) * 8, :],
                                      wstg[i % 4][:]).then_inc(ctx.gp.h, 1)
                    else:
                        _, r, p, _ = kind
                        if r >= 1:
                            g.wait_ge(ctx.mm.h, S["mm_v", 2 * (r - 1) + 1, 7])
                        g.tensor_copy(wvb[:, p * 2:(p + 1) * 2, :],
                                      wstgv[p % 4][:]).then_inc(ctx.gp.h, 1)
                    gpc += 1
                    if i + 4 < len(pieces):
                        g.wait_ge(ctx.gp.h, gpc)
                        pload(i + 4)
                ctx.gp.n = gp_end

            @blk.tensor
            def _(t):
                for u in range(NU):
                    r, hh = u // 2, u % 2
                    t.wait_ge(lsl[u % 3].h, 16 * (u // 3 + 1))
                    for cb in range(8):
                        if hh == 0:
                            t.wait_ge(ctx.gp.h, S["gp_qk", r, cb])
                        bank = (cb % 2) * 2
                        for tb in range(2):
                            if u >= 1 or cb >= 2:
                                pu, pcb = (u, cb - 2) if cb >= 2 else (u - 1, cb + 6)
                                t.wait_ge(ctx.sc.h, S["sc_e", pu, pcb, tb])
                            for dc in range(DC):
                                ins = t.matmul(
                                    ps[bank + tb][:], wbufr[:, cb, dc, :],
                                    xrh[u % 3][:, dc, tb * 512:(tb + 1) * 512],
                                    start=(dc == 0), stop=(dc == DC - 1),
                                )
                            ins.then_inc(ctx.mm.h, 1)
                    if hh == 0:
                        t.wait_ge(ctx.gp.h, S["gp_v", r])
                    for tci in range(8):
                        bank = 4 + tci % 4
                        if u >= 1 or tci >= 4:
                            pu, ptc = (u, tci - 4) if tci >= 4 else (u - 1, tci + 4)
                            t.wait_ge(ctx.ev.h, S["ev_e", pu, ptc])
                        for dc in range(DC):
                            ins = t.matmul(ps[bank][:],
                                           xrh[u % 3][:, dc, tci * 128:(tci + 1) * 128],
                                           wvb[:, dc, :], start=(dc == 0), stop=(dc == DC - 1))
                        ins.then_inc(ctx.mm.h, 1)
                ctx.mm.n = mm_end

            @blk.scalar
            def _(s):
                for u in range(NU):
                    for cb in range(8):
                        for tb in range(2):
                            s.wait_ge(ctx.mm.h, S["mm_qk", u, cb, tb])
                            if u >= 1 or cb >= 2:
                                pu, pcb = (u, cb - 2) if cb >= 2 else (u - 1, cb + 6)
                                s.wait_ge(qsl[cb % 2].h, QC[pu, pcb])
                            s.activation(qkst[cb % 2][:, tb * 512:(tb + 1) * 512],
                                         ps[(cb % 2) * 2 + tb][:], AF.Identity).then_inc(ctx.sc.h, 1)
                ctx.sc.n = sc_end

            @blk.vector
            def _(v):
                for u in range(NU):
                    for tci in range(8):
                        v.wait_ge(ctx.mm.h, S["mm_v", u, tci])
                        if u >= 1 or tci >= 4:
                            pu, ptc = (u, tci - 4) if tci >= 4 else (u - 1, tci + 4)
                            v.wait_ge(vsl[tci % 4].h, VC[pu, ptc])
                        v.tensor_copy(vst[tci % 4][:], ps[4 + tci % 4][:]).then_inc(ctx.ev.h, 1)
                ctx.ev.n = ev_end
    _bar(nc)


def _emit_l1_attn(nc, ctx, qT1, kT1, v1, oT1, onesrow, onescol, diagm):
    NQG = N // 512
    lsl = [ctx.slot[10], ctx.slot[11]]
    osl = [ctx.slot[6], ctx.slot[7]]
    lbase = [s.n for s in lsl]
    obase = [s.n for s in osl]
    with ExitStack() as st:
        qh = [st.enter_context(nc.sbuf_tensor(f"c3q{i}", [128, N], BF16)) for i in range(2)]
        kh = [st.enter_context(nc.sbuf_tensor(f"c3k{i}", [128, N], BF16)) for i in range(2)]
        vh = [st.enter_context(nc.sbuf_tensor(f"c3v{i}", [128, N // 128, 128], BF16)) for i in range(2)]
        sq2 = [st.enter_context(nc.sbuf_tensor(f"c3sq{i}", [128, 512], BF16)) for i in range(2)]
        nrow = st.enter_context(nc.sbuf_tensor("c3nr", [1, 6, 512], F32))
        mcol = st.enter_context(nc.sbuf_tensor("c3mc", [128, 1], F32))
        pt = [st.enter_context(nc.sbuf_tensor(f"c3pt{i}", [128, 512], BF16)) for i in range(4)]
        ptq = [st.enter_context(nc.sbuf_tensor(f"c3pp{i}", [128, 512], BF16)) for i in range(2)]
        pts2 = st.enter_context(nc.sbuf_tensor("c3p2", [128, 512], BF16))
        rrow = [st.enter_context(nc.sbuf_tensor(f"c3rr{i}", [1, 512], F32)) for i in range(2)]
        rbc = [st.enter_context(nc.sbuf_tensor(f"c3rb{i}", [128, 512], F32)) for i in range(2)]
        ohst = [st.enter_context(nc.sbuf_tensor(f"c3o{i}", [128, 512], BF16)) for i in range(2)]
        pssc = [st.enter_context(nc.psum_tensor(f"c3ps{i}", [128, 512], F32)) for i in range(3)]
        psO = [st.enter_context(nc.psum_tensor(f"c3pO{i}", [128, 512], F32)) for i in range(2)]
        pssum = [st.enter_context(nc.psum_tensor(f"c3pS{i}", [1, 512], F32)) for i in range(2)]
        psB = st.enter_context(nc.psum_tensor("c3pB", [128, 512], F32))

        bsteps = [(side, tbb) for side in range(2) for tbb in range(16)]
        seq = []
        for qg in range(NQG):
            kcs = list(range(4 * qg, 4 * qg + 4)) + list(range(4 * qg))
            for pos, kc in enumerate(kcs):
                seq.append((qg, kc, pos))
        nst = len(seq)
        L = {qg: 4 * qg + 4 for qg in range(NQG)}
        lastq = {qg: (qg + 1) - 1 for qg in range(NQG)}

        S = {}

        def plan(emit):
            c = {"mm": ctx.mm.n, "sc": ctx.sc.n, "ev": ctx.ev.n, "gp": ctx.gp.n}

            def bump(kind, key):
                c[kind] += 1
                S[key] = c[kind]

            for h in range(4):
                for side, tbb in bsteps:
                    bump("sc", ("sc_sq", h, side, tbb))
                    bump("mm", ("mm_bs", h, side, tbb))
                    bump("ev", ("ev_rm", h, side, tbb))
                    bump("ev", ("ev_mx", h, side, tbb))
                bump("ev", ("ev_mul", h))
                bump("sc", ("sc_sqrt", h))
                bump("ev", ("ev_neg", h))
                bump("mm", ("mm_mcol", h))
                bump("ev", ("ev_mc", h))
                for i, (qg, kc, pos) in enumerate(seq):
                    bump("mm", ("mm_sc", h, qg, kc))
                    bump("sc", ("sc_exp", h, qg, kc))
                    if pos < 4:
                        bump("gp", ("gp_msk", h, qg, kc))
                    if pos % 2 == 1:
                        if pos % 4 == 1:
                            bump("ev", ("ev_pa", h, qg, pos))
                        else:
                            bump("ev", ("ev_pa", h, qg, pos))
                            bump("ev", ("ev_q", h, qg, pos // 4))
                        kcm1 = seq[i - 1][1]
                        S["ev_rd", h, qg, kc] = c["ev"]
                        S["ev_rd", h, qg, kcm1] = c["ev"]
                    if i >= 2:
                        bump("mm", ("mm_av", h, seq[i - 2][0], seq[i - 2][1]))
                    if i >= 4 and seq[i - 4][2] % 4 == 3:
                        bump("mm", ("mm_sum", h, seq[i - 4][0], seq[i - 4][2] // 4))
                    if i >= 6 and seq[i - 6][2] == L[seq[i - 6][0]] - 1:
                        bump("ev", ("ev_rec", h, seq[i - 6][0]))
                    if i >= 10 and seq[i - 10][2] == L[seq[i - 10][0]] - 1:
                        bump("mm", ("mm_bc", h, seq[i - 10][0]))
                    for dep in (11, 12):
                        if i >= dep and seq[i - dep][2] == L[seq[i - dep][0]] - 1 \
                                and (seq[i - dep][0] == 0) == (dep == 11):
                            qq = seq[i - dep][0]
                            bump("ev", ("ev_rbc", h, qq))
                            bump("ev", ("ev_o", h, qq))
                for i in range(nst, nst + 12):
                    if i - 2 < nst:
                        bump("mm", ("mm_av", h, seq[i - 2][0], seq[i - 2][1]))
                    if i - 4 < nst and seq[i - 4][2] % 4 == 3:
                        bump("mm", ("mm_sum", h, seq[i - 4][0], seq[i - 4][2] // 4))
                    if i - 6 < nst and seq[i - 6][2] == L[seq[i - 6][0]] - 1:
                        bump("ev", ("ev_rec", h, seq[i - 6][0]))
                    if i - 10 < nst and i >= 10 and seq[i - 10][2] == L[seq[i - 10][0]] - 1:
                        bump("mm", ("mm_bc", h, seq[i - 10][0]))
                    if i - 12 < nst and i >= 12 and seq[i - 12][2] == L[seq[i - 12][0]] - 1:
                        qq = seq[i - 12][0]
                        bump("ev", ("ev_rbc", h, qq))
                        bump("ev", ("ev_o", h, qq))
            return c

        ends = plan(False)
        mm_end, sc_end, ev_end, gp_end = ends["mm"], ends["sc"], ends["ev"], ends["gp"]
        gsteps = [(h,) + s for h in range(4) for s in seq]
        gindex = {(h, qg, kc): gi for gi, (h, qg, kc, pos) in enumerate(gsteps)}
        gquads = [(h, qg, qi) for h in range(4) for qg in range(NQG) for qi in range(qg + 1)]
        gqid = {k: i for i, k in enumerate(gquads)}
        last_step = seq[-1]

        with nc.Block("C3") as blk:
            @blk.sync
            def _(sy):
                for h in range(4):
                    if h >= 2:
                        sy.wait_ge(ctx.ev.h, S["ev_o", h - 2, NQG - 1])
                        sy.wait_ge(ctx.mm.h, S["mm_av", h - 2, last_step[0], last_step[1]])
                    sy.dma_start(out=qh[h % 2][:], in_=qT1[h * 128:(h + 1) * 128, :]).then_inc(lsl[h % 2].h, 16)
                    sy.dma_start(out=kh[h % 2][:], in_=kT1[h * 128:(h + 1) * 128, :]).then_inc(lsl[h % 2].h, 16)
                    sy.dma_start(out=vh[h % 2][:], in_=v1[h]).then_inc(lsl[h % 2].h, 16)
                    lsl[h % 2].n += 48
                    for qg in range(NQG):
                        j = h * NQG + qg
                        sy.wait_ge(ctx.ev.h, S["ev_o", h, qg])
                        rd, tq = qg // 4, qg % 4
                        sy.dma_start(
                            out=oT1[rd * 512 + h * 128:rd * 512 + (h + 1) * 128,
                                    tq * 512:(tq + 1) * 512],
                            in_=ohst[qg % 2][:]).then_inc(osl[j % 2].h, 16)
                        osl[j % 2].n += 16
                for s_ in osl:
                    sy.wait_ge(s_.h, s_.n)

            @blk.scalar
            def _(s):
                for h in range(4):
                    s.wait_ge(lsl[h % 2].h, lbase[h % 2] + 48 * (h // 2 + 1))
                    for bi, (side, tbb) in enumerate(bsteps):
                        if bi >= 2 or h >= 1:
                            pb = bsteps[bi - 2] if bi >= 2 else bsteps[bi - 2 + 32]
                            ph = h if bi >= 2 else h - 1
                            s.wait_ge(ctx.mm.h, S["mm_bs", ph, pb[0], pb[1]])
                        src = qh[h % 2] if side == 0 else kh[h % 2]
                        s.activation(sq2[bi % 2][:], src[:, tbb * 512:(tbb + 1) * 512],
                                     AF.Square).then_inc(ctx.sc.h, 1)
                    s.wait_ge(ctx.ev.h, S["ev_mul", h])
                    s.activation(nrow[:, 3, 0:1], nrow[:, 3, 0:1], AF.Sqrt).then_inc(ctx.sc.h, 1)
                    for i, (qg, kc, pos) in enumerate(seq):
                        s.wait_ge(ctx.mm.h, S["mm_sc", h, qg, kc])
                        gi = h * nst + i
                        if gi >= 4:
                            pk = gsteps[gi - 4]
                            s.wait_ge(ctx.mm.h, S["mm_av", pk[0], pk[1], pk[2]])
                            s.wait_ge(ctx.ev.h, S.get(("ev_rd", pk[0], pk[1], pk[2]), 0))
                        if i == 0:
                            s.wait_ge(ctx.ev.h, S["ev_mc", h])
                        s.activation(pt[kc % 4][:], pssc[i % 3][:], AF.Exp,
                                     bias=mcol[:], scale=float(INV_SQRT_DH)).then_inc(ctx.sc.h, 1)
                ctx.sc.n = sc_end

            @blk.tensor
            def _(t):
                def do_sum(h, qg2, pos2):
                    qi = pos2 // 4
                    t.wait_ge(ctx.ev.h, S["ev_q", h, qg2, qi])
                    if qi == 0:
                        if qg2 >= 2:
                            t.wait_ge(ctx.ev.h, S["ev_rec", h, qg2 - 2])
                        elif h >= 1 and qg2 == 0:
                            t.wait_ge(ctx.ev.h, S["ev_rec", h - 1, NQG - 2])
                        elif h >= 1 and qg2 == 1:
                            t.wait_ge(ctx.ev.h, S["ev_rec", h - 1, NQG - 1])
                    t.matmul(pssum[qg2 % 2][:], onescol[:], ptq[gqid[h, qg2, qi] % 2][:],
                             start=(qi == 0), stop=(qi == lastq[qg2])).then_inc(ctx.mm.h, 1)

                def sums_av(h, qg2, kc2, pos2):
                    if ("gp_msk", h, qg2, kc2) in S:
                        t.wait_ge(ctx.gp.h, S["gp_msk", h, qg2, kc2])
                    else:
                        t.wait_ge(ctx.sc.h, S["sc_exp", h, qg2, kc2])
                    if pos2 == 0:
                        if qg2 >= 2:
                            t.wait_ge(ctx.ev.h, S["ev_o", h, qg2 - 2])
                        elif h >= 1 and qg2 == 0:
                            t.wait_ge(ctx.ev.h, S["ev_o", h - 1, NQG - 2])
                        elif h >= 1 and qg2 == 1:
                            t.wait_ge(ctx.ev.h, S["ev_o", h - 1, NQG - 1])
                    t.matmul(psO[qg2 % 2][:], vh[h % 2][:, kc2, :], pt[kc2 % 4][:],
                             start=(pos2 == 0), stop=(pos2 == L[qg2] - 1)).then_inc(ctx.mm.h, 1)

                def bc(h, qq):
                    t.wait_ge(ctx.ev.h, S["ev_rec", h, qq])
                    if qq >= 2:
                        t.wait_ge(ctx.ev.h, S["ev_rbc", h, qq - 2])
                    elif h >= 1 and qq == 0:
                        t.wait_ge(ctx.ev.h, S["ev_rbc", h - 1, NQG - 2])
                    elif h >= 1 and qq == 1:
                        t.wait_ge(ctx.ev.h, S["ev_rbc", h - 1, NQG - 1])
                    t.matmul(psB[:], onesrow[:], rrow[qq % 2][:], start=True,
                             stop=True).then_inc(ctx.mm.h, 1)

                for h in range(4):
                    t.wait_ge(lsl[h % 2].h, lbase[h % 2] + 48 * (h // 2 + 1))
                    if h >= 1:
                        t.wait_ge(ctx.sc.h, S["sc_exp", h - 1, last_step[0], last_step[1]])
                    for bi, (side, tbb) in enumerate(bsteps):
                        t.wait_ge(ctx.sc.h, S["sc_sq", h, side, tbb])
                        if bi >= 3:
                            pb = bsteps[bi - 3]
                            t.wait_ge(ctx.ev.h, S["ev_rm", h, pb[0], pb[1]])
                        t.matmul(pssc[bi % 3][0:1, :], onescol[:], sq2[bi % 2][:],
                                 start=True, stop=True).then_inc(ctx.mm.h, 1)
                    t.wait_ge(ctx.ev.h, S["ev_neg", h])
                    t.matmul(psB[:, 0:1], onesrow[:], nrow[:, 4, 0:1], start=True,
                             stop=True).then_inc(ctx.mm.h, 1)
                    for i, (qg, kc, pos) in enumerate(seq):
                        gi = h * nst + i
                        if gi >= 3:
                            pk = gsteps[gi - 3]
                            t.wait_ge(ctx.sc.h, S["sc_exp", pk[0], pk[1], pk[2]])
                        if i < 3:
                            bb = bsteps[(30, 31, 29)[i]]
                            t.wait_ge(ctx.ev.h, S["ev_rm", h, bb[0], bb[1]])
                        t.matmul(pssc[i % 3][:], kh[h % 2][:, kc * 128:(kc + 1) * 128],
                                 qh[h % 2][:, qg * 512:(qg + 1) * 512], start=True,
                                 stop=True).then_inc(ctx.mm.h, 1)
                        if i >= 2:
                            qg2, kc2, pos2 = seq[i - 2]
                            sums_av(h, qg2, kc2, pos2)
                        if i >= 4 and seq[i - 4][2] % 4 == 3:
                            do_sum(h, seq[i - 4][0], seq[i - 4][2])
                        if i >= 10 and seq[i - 10][2] == L[seq[i - 10][0]] - 1:
                            bc(h, seq[i - 10][0])
                    for i in range(nst, nst + 12):
                        if i - 2 < nst:
                            qg2, kc2, pos2 = seq[i - 2]
                            sums_av(h, qg2, kc2, pos2)
                        if i - 4 < nst and seq[i - 4][2] % 4 == 3:
                            do_sum(h, seq[i - 4][0], seq[i - 4][2])
                        if i - 10 < nst and i >= 10 and seq[i - 10][2] == L[seq[i - 10][0]] - 1:
                            bc(h, seq[i - 10][0])
                ctx.mm.n = mm_end

            @blk.gpsimd
            def _(g):
                for h in range(4):
                    for qg in range(NQG):
                        for kc in range(4 * qg, 4 * qg + 4):
                            g.wait_ge(ctx.sc.h, S["sc_exp", h, qg, kc])
                            g.tensor_tensor(pt[kc % 4][:], pt[kc % 4][:],
                                            diagm[:, kc - 4 * qg, :], ALU.mult).then_inc(ctx.gp.h, 1)
                ctx.gp.n = gp_end

            @blk.vector
            def _(v):
                myev = ctx.ev.n
                def vop(ins):
                    nonlocal myev
                    ins.then_inc(ctx.ev.h, 1)
                    myev += 1
                for h in range(4):
                    for bi, (side, tbb) in enumerate(bsteps):
                        acc = nrow[:, 1 + side, 0:1]
                        v.wait_ge(ctx.mm.h, S["mm_bs", h, side, tbb])
                        v.wait_ge(ctx.ev.h, myev)
                        vop(v.reduce_max(nrow[:, 5, 0:1], pssc[bi % 3][0:1, :], axis=AX))
                        v.wait_ge(ctx.ev.h, myev)
                        if tbb == 0:
                            vop(v.tensor_copy(acc, nrow[:, 5, 0:1]))
                        else:
                            vop(v.tensor_max(acc, acc, nrow[:, 5, 0:1]))
                    v.wait_ge(ctx.ev.h, myev)
                    vop(v.tensor_mul(nrow[:, 3, 0:1], nrow[:, 1, 0:1], nrow[:, 2, 0:1]))
                    v.wait_ge(ctx.sc.h, S["sc_sqrt", h])
                    vop(v.tensor_scalar_mul(nrow[:, 4, 0:1], nrow[:, 3, 0:1],
                                            -float(INV_SQRT_DH)))
                    v.wait_ge(ctx.mm.h, S["mm_mcol", h])
                    vop(v.tensor_copy(mcol[:], psB[:, 0:1]))
                    for i, (qg, kc, pos) in enumerate(seq):
                        if pos % 2 == 1:
                            kcm1 = seq[i - 1][1]
                            if pos < 4 and ("gp_msk", h, qg, kc) in S:
                                v.wait_ge(ctx.gp.h, S["gp_msk", h, qg, kc])
                                v.wait_ge(ctx.gp.h, S["gp_msk", h, qg, kcm1])
                            else:
                                v.wait_ge(ctx.sc.h, S["sc_exp", h, qg, kc])
                            qi = pos // 4
                            gq = gqid[h, qg, qi]
                            if pos % 4 == 1:
                                if gq >= 2:
                                    pq = gquads[gq - 2]
                                    v.wait_ge(ctx.mm.h, S["mm_sum", pq[0], pq[1], pq[2]])
                                v.wait_ge(ctx.ev.h, myev)
                                vop(v.tensor_add(ptq[gq % 2][:], pt[kcm1 % 4][:], pt[kc % 4][:]))
                            else:
                                v.wait_ge(ctx.ev.h, myev)
                                vop(v.tensor_add(pts2[:], pt[kcm1 % 4][:], pt[kc % 4][:]))
                                v.wait_ge(ctx.ev.h, myev)
                                vop(v.tensor_add(ptq[gq % 2][:], ptq[gq % 2][:], pts2[:]))
                        if i >= 6 and seq[i - 6][2] == L[seq[i - 6][0]] - 1:
                            qq = seq[i - 6][0]
                            v.wait_ge(ctx.mm.h, S["mm_sum", h, qq, lastq[qq]])
                            if qq >= 2:
                                v.wait_ge(ctx.mm.h, S["mm_bc", h, qq - 2])
                            elif h >= 1 and qq == 0:
                                v.wait_ge(ctx.mm.h, S["mm_bc", h - 1, NQG - 2])
                            elif h >= 1 and qq == 1:
                                v.wait_ge(ctx.mm.h, S["mm_bc", h - 1, NQG - 1])
                            v.wait_ge(ctx.ev.h, myev)
                            vop(v.reciprocal(rrow[qq % 2][:], pssum[qq % 2][:]))
                        for dep in (11, 12):
                            if i >= dep and seq[i - dep][2] == L[seq[i - dep][0]] - 1 \
                                    and (seq[i - dep][0] == 0) == (dep == 11):
                                qq = seq[i - dep][0]
                                j = h * NQG + qq
                                v.wait_ge(ctx.mm.h, S["mm_bc", h, qq])
                                v.wait_ge(ctx.ev.h, myev)
                                vop(v.tensor_copy(rbc[qq % 2][:], psB[:]))
                                if j >= 2:
                                    v.wait_ge(osl[j % 2].h, obase[j % 2] + 16 * (j // 2))
                                v.wait_ge(ctx.ev.h, myev)
                                vop(v.tensor_mul(ohst[qq % 2][:], psO[qq % 2][:], rbc[qq % 2][:]))
                    for i in range(nst, nst + 12):
                        if i - 6 < nst and seq[i - 6][2] == L[seq[i - 6][0]] - 1:
                            qq = seq[i - 6][0]
                            v.wait_ge(ctx.mm.h, S["mm_sum", h, qq, lastq[qq]])
                            v.wait_ge(ctx.mm.h, S["mm_bc", h, qq - 2])
                            v.wait_ge(ctx.ev.h, myev)
                            vop(v.reciprocal(rrow[qq % 2][:], pssum[qq % 2][:]))
                        if i - 12 < nst and i >= 12 and seq[i - 12][2] == L[seq[i - 12][0]] - 1:
                            qq = seq[i - 12][0]
                            j = h * NQG + qq
                            v.wait_ge(ctx.mm.h, S["mm_bc", h, qq])
                            v.wait_ge(ctx.ev.h, myev)
                            vop(v.tensor_copy(rbc[qq % 2][:], psB[:]))
                            if j >= 2:
                                v.wait_ge(osl[j % 2].h, obase[j % 2] + 16 * (j // 2))
                            v.wait_ge(ctx.ev.h, myev)
                            vop(v.tensor_mul(ohst[qq % 2][:], psO[qq % 2][:], rbc[qq % 2][:]))
                ctx.ev.n = ev_end
                assert myev == ev_end, (myev, ev_end)



def build_launch_D(dbg=False):
    nc = bass.Bass("TRN2", target_bir_lowering=False, debug=False)
    oTm = nc.dram_tensor("oTm", [D, TPC], BF16, kind="ExternalInput").ap()
    x2f = nc.dram_tensor("x2f", [D, TPC], F32, kind="ExternalInput").ap()
    woT = nc.dram_tensor("woTt", [16, 128, 16, 128], F32, kind="ExternalInput").ap()
    acat = nc.dram_tensor("acatt", [128, 16, 512], F32, kind="ExternalInput").ap()
    wgT = nc.dram_tensor("wgTt", [128, 16, 8], F32, kind="ExternalInput").ap()
    bwT = nc.dram_tensor("bwTt", [128, 4, 2048], F32, kind="ExternalInput").ap()

    out_c = nc.dram_tensor("out_c", [TPC, D], F32, kind="ExternalOutput").ap()
    ik = "ExternalOutput" if dbg else "Internal"
    x3f = nc.dram_tensor("x3f", [D, TPC], F32, kind=ik).ap()
    x3b = nc.dram_tensor("x3b", [D, TPC], BF16, kind=ik).ap()
    ybuf = nc.dram_tensor("ybuf", [D, TPC], F32, kind=ik).ap()
    rbuf = nc.dram_tensor("rbuf", [D, TPC], F32, kind="Internal").ap()

    onesrow_t = nc.inline_tensor(ONES_ROW_F32, "c_onesrow").ap()
    onescol_t = nc.inline_tensor(ONES_COL_B16, "c_onescol").ap()
    identf_t = nc.inline_tensor(IDENT_F32, "c_identf").ap()
    identb_t = nc.inline_tensor(IDENT_B16, "c_identb").ap()

    with ExitStack() as stack:
        ctx = Ctx(nc, stack)
        onesrow = ctx.sb("onesrow", [1, 128], F32)
        onescol = ctx.sb("onescol", [128, 1], BF16)
        identf = ctx.sb("identf", [128, 128], F32)
        identb = ctx.sb("identb", [128, 128], BF16)

        with nc.Block("D0") as blk:
            @blk.gpsimd
            def _(g):
                for dst, src in [(onesrow[:], onesrow_t), (onescol[:], onescol_t),
                                 (identf[:], identf_t), (identb[:], identb_t)]:
                    g.dma_start(out=dst, in_=src).then_inc(ctx.ld_g.h, 16)
                    ctx.ld_g.n += 16
                g.wait_ge(ctx.ld_g.h, ctx.ld_g.n)
        _bar(nc)

        _emit_proj_ln2(nc, ctx, "D1", rhs_hbm=oTm, K_=D, w_hbm=woT,
                       bias_col=None, xold_hbm=x2f,
                       xnew_f=x3f, xnew_b=x3b, onescol=onescol, onesrow=onesrow)
        _emit_moe(nc, ctx, x3b, acat, wgT, bwT, ybuf, identb)
        _bar(nc)
        _emit_final(nc, ctx, ybuf, x3f, out_c, onescol, onesrow, identf)
        _bar(nc)
    return nc


def _emit_moe(nc, ctx, x3b, acat, wgT, bwT, ybuf, identb):
    NTC = TPC // 128
    YG = {}
    with ExitStack() as st:
        xb3 = st.enter_context(nc.sbuf_tensor("moex", [128, DC, TPC], BF16))
        acat_sb = st.enter_context(nc.sbuf_tensor("moea", [128, DC, 512], BF16))
        wg_sb = st.enter_context(nc.sbuf_tensor("moew", [128, DC, 8], BF16))
        bw_sb = st.enter_context(nc.sbuf_tensor("moeb", [128, 4, D], BF16))
        hmT = st.enter_context(nc.sbuf_tensor("moeh", [128, 4, TPC], BF16))
        hm = [st.enter_context(nc.sbuf_tensor(f"moem{i}", [128, 512], BF16)) for i in range(2)]
        pex = [st.enter_context(nc.sbuf_tensor(f"moep{i}", [128, 8], F32)) for i in range(2)]
        gx = st.enter_context(nc.sbuf_tensor("moegx", [128, 6], F32))
        msk = st.enter_context(nc.sbuf_tensor("moemk", [128, 8], F32))
        ystage = [st.enter_context(nc.sbuf_tensor(f"moey{i}", [128, 512], F32)) for i in range(2)]
        stA = ExitStack()
        psh = [stA.enter_context(nc.psum_tensor(f"moeph{i}", [128, 512], F32)) for i in range(2)]
        psl = [stA.enter_context(nc.psum_tensor(f"moepl{i}", [128, 8], F32)) for i in range(2)]
        pst = [stA.enter_context(nc.psum_tensor(f"moept{i}", [128, 128], BF16)) for i in range(2)]

        S = {}
        mmv, scv, evv = ctx.mm.n, ctx.sc.n, ctx.ev.n
        for tc in range(NTC):
            mmv += 1; S["mm_hl", tc] = mmv
            evv += 2; S["ev_gneg", tc] = evv
            scv += 1; S["sc_exp", tc] = scv
            evv += 5; S["ev_mskg", tc] = evv
            evv += 8; S["ev_app", tc] = evv
            for i in range(4):
                mmv += 1; S["mm_tr", tc, i] = mmv
                evv += 1; S["ev_tr", tc, i] = evv
        mm_end, sc_end, ev_end = mmv, scv, evv

        g0 = ctx.ld_g.n
        with nc.Block("MOE_A") as blk:
            @blk.gpsimd
            def _(g):
                g.dma_start(out=xb3[:], in_=x3b.rearrange("(c p) t -> p c t", p=128)).then_inc(ctx.ld_g.h, 16)
                g.dma_start(out=acat_sb[:], in_=acat).then_inc(ctx.ld_g.h, 16)
                g.dma_start(out=wg_sb[:], in_=wgT).then_inc(ctx.ld_g.h, 16)
                g.dma_start(out=bw_sb[:], in_=bwT).then_inc(ctx.ld_g.h, 16)
                ctx.ld_g.n += 64
                g.wait_ge(ctx.ld_g.h, ctx.ld_g.n)

            @blk.tensor
            def _(t):
                t.wait_ge(ctx.ld_g.h, g0 + 64)
                for tc in range(NTC):
                    if tc >= 2:
                        t.wait_ge(ctx.ev.h, S["ev_app", tc - 2])
                    for dc in range(DC):
                        t.matmul(psh[tc % 2][:], xb3[:, dc, tc * 128:(tc + 1) * 128],
                                 acat_sb[:, dc, :], start=(dc == 0), stop=(dc == DC - 1))
                        ins = t.matmul(psl[tc % 2][:], xb3[:, dc, tc * 128:(tc + 1) * 128],
                                       wg_sb[:, dc, :], start=(dc == 0), stop=(dc == DC - 1))
                    ins.then_inc(ctx.mm.h, 1)
                    for i in range(4):
                        t.wait_ge(ctx.ev.h, S["ev_app", tc])
                        if tc >= 1 or i >= 2:
                            pi = (tc, i - 2) if i >= 2 else (tc - 1, i + 2)
                            t.wait_ge(ctx.ev.h, S["ev_tr", pi[0], pi[1]])
                        t.transpose(pst[i % 2][:], hm[tc % 2][:, i * 128:(i + 1) * 128],
                                    identb[:]).then_inc(ctx.mm.h, 1)
                ctx.mm.n = mm_end

            @blk.scalar
            def _(s):
                for tc in range(NTC):
                    s.wait_ge(ctx.ev.h, S["ev_gneg", tc])
                    if tc >= 2:
                        s.wait_ge(ctx.ev.h, S["ev_mskg", tc - 2])
                    s.activation(pex[tc % 2][:], psl[tc % 2][:], AF.Exp,
                                 bias=gx[:, 1:2], scale=1.0).then_inc(ctx.sc.h, 1)
                ctx.sc.n = sc_end

            @blk.vector
            def _(v):
                myev = ctx.ev.n
                def vop(ins):
                    nonlocal myev
                    ins.then_inc(ctx.ev.h, 1)
                    myev += 1
                for tc in range(NTC):
                    v.wait_ge(ctx.mm.h, S["mm_hl", tc])
                    if tc >= 1:
                        v.wait_ge(ctx.sc.h, S["sc_exp", tc - 1])
                    v.wait_ge(ctx.ev.h, myev)
                    vop(v.reduce_max(gx[:, 0:1], psl[tc % 2][:], axis=AX))
                    v.wait_ge(ctx.ev.h, myev)
                    vop(v.tensor_scalar_mul(gx[:, 1:2], gx[:, 0:1], -1.0))
                    v.wait_ge(ctx.sc.h, S["sc_exp", tc])
                    vop(v.reduce_sum(gx[:, 2:3], pex[tc % 2][:], axis=AX))
                    v.wait_ge(ctx.ev.h, myev)
                    vop(v.reciprocal(gx[:, 3:4], gx[:, 2:3]))
                    v.wait_ge(ctx.ev.h, myev)
                    vop(v.tensor_scalar_mul(gx[:, 4:5], gx[:, 3:4], 4.0))
                    vop(v.tensor_single_scalar(msk[:], psl[tc % 2][:], gx[:, 0:1], ALU.is_ge))
                    v.wait_ge(ctx.ev.h, myev)
                    vop(v.tensor_scalar_mul(msk[:], msk[:], gx[:, 4:5]))
                    v.wait_ge(ctx.ev.h, myev)
                    if tc >= 2:
                        v.wait_ge(ctx.mm.h, S["mm_tr", tc - 2, 3])
                    for e in range(8):
                        vop(v.tensor_scalar_mul(hm[tc % 2][:, e * 64:(e + 1) * 64],
                                                psh[tc % 2][:, e * 64:(e + 1) * 64],
                                                msk[:, e:e + 1]))
                    for i in range(4):
                        v.wait_ge(ctx.mm.h, S["mm_tr", tc, i])
                        vop(v.tensor_copy(hmT[:, i, tc * 128:(tc + 1) * 128], pst[i % 2][:]))
                ctx.ev.n = ev_end
                assert myev == ev_end
        _bar(nc)
        stA.close()

        psy = [st.enter_context(nc.psum_tensor(f"moepy{i}", [128, 512], F32)) for i in range(4)]
        ssl = [ctx.slot[2], ctx.slot[3]]
        sbase = [s.n for s in ssl]
        mm0, sc0 = ctx.mm.n, ctx.sc.n
        with nc.Block("MOE_B") as blk:
            @blk.tensor
            def _(t):
                for ob in range(16):
                    for tb in range(4):
                        j = ob * 4 + tb
                        if j >= 4:
                            t.wait_ge(ctx.sc.h, sc0 + j - 3)
                        for rc in range(4):
                            ins = t.matmul(psy[j % 4][:], bw_sb[:, rc, ob * 128:(ob + 1) * 128],
                                           hmT[:, rc, tb * 512:(tb + 1) * 512],
                                           start=(rc == 0), stop=(rc == 3))
                        ins.then_inc(ctx.mm.h, 1)
                ctx.mm.n += 64

            @blk.scalar
            def _(s):
                for j in range(64):
                    s.wait_ge(ctx.mm.h, mm0 + j + 1)
                    if j >= 2:
                        s.wait_ge(ssl[j % 2].h, sbase[j % 2] + 16 * (j // 2))
                    s.activation(ystage[j % 2][:], psy[j % 4][:], AF.Identity).then_inc(ctx.sc.h, 1)
                ctx.sc.n += 64

            @blk.gpsimd
            def _(g):
                for ob in range(16):
                    for tb in range(4):
                        j = ob * 4 + tb
                        g.wait_ge(ctx.sc.h, sc0 + j + 1)
                        g.dma_start(out=ybuf[ob * 128:(ob + 1) * 128, tb * 512:(tb + 1) * 512],
                                    in_=ystage[j % 2][:]).then_inc(ssl[j % 2].h, 16)
                        ssl[j % 2].n += 16
                        YG[ob, tb] = (ssl[j % 2], ssl[j % 2].n)
                for s_ in ssl:
                    g.wait_ge(s_.h, s_.n)
    return YG


def _emit_final(nc, ctx, ybuf, x3f, out_c, onescol, onesrow, identf,
                ygate=None, pre=None):
    MB = DC
    for hf in range(2):
        t0 = hf * 1024
        with ExitStack() as st:
            if pre is not None:
                rres, ysb, xsb = pre
            else:
                rres = st.enter_context(nc.sbuf_tensor(f"fnr{hf}", [128, 2 * MB, 512], F32))
                ysb = [st.enter_context(nc.sbuf_tensor(f"fny{hf}{i}", [128, 512], F32)) for i in range(2)]
                xsb = [st.enter_context(nc.sbuf_tensor(f"fnx{hf}{i}", [128, 512], F32)) for i in range(2)]
            rb16 = [st.enter_context(nc.sbuf_tensor(f"fnb{hf}{i}", [128, 512], BF16)) for i in range(2)]
            rq16 = [st.enter_context(nc.sbuf_tensor(f"fnq{hf}{i}", [128, 512], BF16)) for i in range(2)]
            tmp = [st.enter_context(nc.sbuf_tensor(f"fnt{hf}{i}", [128, 512], F32)) for i in range(2)]
            x4t = [st.enter_context(nc.sbuf_tensor(f"fn4{hf}{i}", [128, 512], F32)) for i in range(2)]
            outst = [st.enter_context(nc.sbuf_tensor(f"fno{hf}{i}", [128, 4, D], F32)) for i in range(2)]
            srow = st.enter_context(nc.sbuf_tensor(f"fns{hf}", [1, 16, 512], F32))
            pgen = [st.enter_context(nc.psum_tensor(f"fnp{hf}{i}", [128, 4, 128], F32)) for i in range(4)]
            pbc = [st.enter_context(nc.psum_tensor(f"fnc{hf}{i}", [128, 512], F32)) for i in range(4)]

            ysl = [ctx.slot[6], ctx.slot[7]]
            xsl = [ctx.slot[8], ctx.slot[9]]
            ostsl = [ctx.slot[u] for u in range(4)]
            yb_ = [s.n for s in ysl]
            xb_ = [s.n for s in xsl]
            ost_ = [s.n for s in ostsl]
            gp0 = ctx.gp.n

            S = {}
            mmv, scv, evv = ctx.mm.n, ctx.sc.n, ctx.ev.n
            for j in range(2 * MB):
                evv += 1; S["ev_add", j] = evv
                scv += 1; S["sc_sq", j] = scv
                mmv += 2; S["mm_st", j] = mmv
            for tb in range(2):
                evv += 6; S["ev_fin", tb] = evv
                scv += 2
                mmv += 2; S["mm_bc", tb] = mmv
            for tb in range(2):
                for ob in range(MB):
                    evv += 2; S["ev_ap", tb, ob] = evv
                    mmv += 4; S["mm_tr", tb, ob] = mmv
                    scv += 1; S["sc_ev", tb, ob] = scv
            mm_end, sc_end, ev_end = mmv, scv, evv

            def step(i):
                return (i // MB, i % MB)

            with nc.Block(f"FN{hf}") as blk:
                @blk.sync
                def _(sy):
                    for j in range(2 * MB):
                        ob, tb = j // 2, j % 2
                        if ygate is not None and hf == 0:
                            gs, gv = ygate[ob, hf * 2 + tb]
                            sy.wait_ge(gs.h, gv)
                        if j >= 2:
                            sy.wait_ge(ctx.ev.h, S["ev_add", j - 2])
                        c0, c1 = t0 + tb * 512, t0 + (tb + 1) * 512
                        sy.dma_start(out=ysb[j % 2][:], in_=ybuf[ob * 128:(ob + 1) * 128, c0:c1]).then_inc(ysl[j % 2].h, 16)
                        ysl[j % 2].n += 16
                        sy.dma_start(out=xsb[j % 2][:], in_=x3f[ob * 128:(ob + 1) * 128, c0:c1]).then_inc(xsl[j % 2].h, 16)
                        xsl[j % 2].n += 16

                @blk.gpsimd
                def _(g):
                    for j in range(2 * MB):
                        g.wait_ge(ctx.ev.h, S["ev_add", j])
                        if j >= 2:
                            g.wait_ge(ctx.mm.h, S["mm_st", j - 2])
                        g.tensor_copy(rb16[j % 2][:], rres[:, j, :]).then_inc(ctx.gp.h, 1)
                    ctx.gp.n += 2 * MB
                    for tb in range(2):
                        for u in range(4):
                            tc = tb * 4 + u
                            g.wait_ge(ctx.sc.h, S["sc_ev", tb, MB - 1])
                            g.dma_start(out=out_c[t0 + tc * 128:t0 + (tc + 1) * 128, :],
                                        in_=outst[tb][:, u, :]).then_inc(ostsl[u].h, 16)
                            ostsl[u].n += 16
                    for s_ in ostsl:
                        g.wait_ge(s_.h, s_.n)

                @blk.tensor
                def _(t):
                    for j in range(2 * MB):
                        ob, tb = j // 2, j % 2
                        t.wait_ge(ctx.gp.h, gp0 + j + 1)
                        t.wait_ge(ctx.sc.h, S["sc_sq", j])
                        t.matmul(pbc[tb][0:1, :], onescol[:], rb16[j % 2][:],
                                 start=(ob == 0), stop=(ob == MB - 1))
                        t.matmul(pbc[2 + tb][0:1, :], onescol[:], rq16[j % 2][:],
                                 start=(ob == 0), stop=(ob == MB - 1)).then_inc(ctx.mm.h, 2)
                    for tb in range(2):
                        t.wait_ge(ctx.ev.h, S["ev_fin", tb])
                        t.matmul(pbc[tb][:], onesrow[:], srow[:, 2 + 8 * tb, :], start=True, stop=True)
                        t.matmul(pbc[2 + tb][:], onesrow[:], srow[:, 6 + 8 * tb, :],
                                 start=True, stop=True).then_inc(ctx.mm.h, 2)
                    for i in range(2 * MB):
                        tb, ob = step(i)
                        t.wait_ge(ctx.ev.h, S["ev_ap", tb, ob])
                        if i >= 4:
                            t.wait_ge(ctx.sc.h, S["sc_ev", step(i - 4)[0], step(i - 4)[1]])
                        for u in range(4):
                            ins = t.transpose(pgen[i % 4][:, u, :],
                                              x4t[i % 2][:, u * 128:(u + 1) * 128], identf[:])
                        ins.then_inc(ctx.mm.h, 4)
                    ctx.mm.n = mm_end

                @blk.scalar
                def _(s):
                    for j in range(2 * MB):
                        s.wait_ge(ctx.ev.h, S["ev_add", j])
                        if j >= 2:
                            s.wait_ge(ctx.mm.h, S["mm_st", j - 2])
                        s.activation(rq16[j % 2][:], rres[:, j, :], AF.Square).then_inc(ctx.sc.h, 1)
                    for tb in range(2):
                        base = 8 * tb
                        s.wait_ge(ctx.ev.h, S["ev_fin", tb] - 3)
                        s.activation(srow[:, 3 + base, :], srow[:, 2 + base, :],
                                     AF.Square).then_inc(ctx.sc.h, 1)
                        s.wait_ge(ctx.ev.h, S["ev_fin", tb] - 1)
                        s.activation(srow[:, 5 + base, :], srow[:, 4 + base, :],
                                     AF.Sqrt).then_inc(ctx.sc.h, 1)
                    for i in range(2 * MB):
                        tb, ob = step(i)
                        s.wait_ge(ctx.mm.h, S["mm_tr", tb, ob])
                        s.activation(outst[tb][:, :, ob * 128:(ob + 1) * 128],
                                     pgen[i % 4][:], AF.Identity).then_inc(ctx.sc.h, 1)
                    ctx.sc.n = sc_end

                @blk.vector
                def _(v):
                    myev = ctx.ev.n
                    def vop(ins):
                        nonlocal myev
                        ins.then_inc(ctx.ev.h, 1)
                        myev += 1
                    for j in range(2 * MB):
                        v.wait_ge(ysl[j % 2].h, yb_[j % 2] + 16 * (j // 2 + 1))
                        v.wait_ge(xsl[j % 2].h, xb_[j % 2] + 16 * (j // 2 + 1))
                        vop(v.tensor_add(rres[:, j, :], ysb[j % 2][:], xsb[j % 2][:]))
                    for tb in range(2):
                        base = 8 * tb
                        v.wait_ge(ctx.mm.h, S["mm_st", 2 * MB - 1])
                        vop(v.tensor_copy(srow[:, 0 + base, :], pbc[tb][0:1, :]))
                        vop(v.tensor_copy(srow[:, 1 + base, :], pbc[2 + tb][0:1, :]))
                        v.wait_ge(ctx.ev.h, myev)
                        vop(v.tensor_scalar_mul(srow[:, 2 + base, :], srow[:, 0 + base, :], 1.0 / D))
                        v.wait_ge(ctx.sc.h, S["sc_sq", 2 * MB - 1] + 2 * tb + 1)
                        v.wait_ge(ctx.ev.h, myev)
                        vop(v.scalar_tensor_tensor(srow[:, 4 + base, :], srow[:, 1 + base, :],
                                                   1.0 / D, srow[:, 3 + base, :],
                                                   ALU.mult, ALU.subtract))
                        v.wait_ge(ctx.ev.h, myev)
                        vop(v.tensor_scalar_add(srow[:, 4 + base, :], srow[:, 4 + base, :], float(EPS)))
                        v.wait_ge(ctx.sc.h, S["sc_sq", 2 * MB - 1] + 2 * tb + 2)
                        vop(v.reciprocal(srow[:, 6 + base, :], srow[:, 5 + base, :]))
                    for i in range(2 * MB):
                        tb, ob = step(i)
                        v.wait_ge(ctx.mm.h, S["mm_bc", tb])
                        v.wait_ge(ctx.ev.h, myev)
                        vop(v.tensor_sub(tmp[i % 2][:], rres[:, ob * 2 + tb, :], pbc[tb][:]))
                        if i >= 2:
                            v.wait_ge(ctx.mm.h, S["mm_tr", step(i - 2)[0], step(i - 2)[1]])
                        v.wait_ge(ctx.ev.h, myev)
                        vop(v.tensor_mul(x4t[i % 2][:], tmp[i % 2][:], pbc[2 + tb][:]))
                    ctx.ev.n = ev_end
                    assert myev == ev_end, (myev, ev_end)
        _bar(nc)



RG_BATCH = [[0, 1, 2, 3], [4, 5, 6, 7]]


def _emit_ag(nc, ctx, name, src_ap, dst_ap):
    with nc.Block(name) as blk:
        @blk.gpsimd
        def _(g):
            g.collective_compute(
                "AllGather", ALU.bypass, replica_groups=RG_BATCH,
                ins=[src_ap.opt()], outs=[dst_ap.opt()],
            ).then_inc(ctx.cc.h)
            ctx.cc.n += 1
            g.wait_ge(ctx.cc.h, ctx.cc.n)
    _bar(nc)


def build_fused():
    from concourse.ap import AP as _AP
    nc = bass.Bass("TRN2", target_bir_lowering=False, debug=False)
    xT = nc.dram_tensor("xT", [D, TPC], F32, kind="ExternalInput").ap()
    wqkvT = nc.dram_tensor("wqkTt", [32, 128, 16, 128], F32, kind="ExternalInput").ap()
    wvTt = nc.dram_tensor("wvTt", [4, 128, 16, 512], F32, kind="ExternalInput").ap()
    bqkv_pc = nc.dram_tensor("bqkv_pc", [128, 48], F32, kind="ExternalInput").ap()
    bqkv_row = nc.dram_tensor("bqkv_row", [1, 3 * D], F32, kind="ExternalInput").ap()
    woutT = nc.dram_tensor("woutTt", [16, 128, 16, 128], F32, kind="ExternalInput").ap()
    bout_pc = nc.dram_tensor("bout_pc", [128, 16], F32, kind="ExternalInput").ap()
    W1T = nc.dram_tensor("W1Tt", [32, 128, 16, 128], F32, kind="ExternalInput").ap()
    b1_pc = nc.dram_tensor("b1_pc", [128, 32], F32, kind="ExternalInput").ap()
    W2T = nc.dram_tensor("W2Tt", [32, 128, 32, 128], F32, kind="ExternalInput").ap()
    b2_pc = nc.dram_tensor("b2_pc", [128, 32], F32, kind="ExternalInput").ap()
    W3T = nc.dram_tensor("W3Tt", [16, 128, 32, 128], F32, kind="ExternalInput").ap()
    b3_pc = nc.dram_tensor("b3_pc", [128, 16], F32, kind="ExternalInput").ap()
    coeff = nc.dram_tensor("coeff", [128, 64], F32, kind="ExternalInput").ap()
    wqT = nc.dram_tensor("wqTt", [4, 128, 16, 128], F32, kind="ExternalInput").ap()
    wkT = nc.dram_tensor("wkTt", [4, 128, 16, 128], F32, kind="ExternalInput").ap()
    wvT = nc.dram_tensor("wvTt1", [1, 128, 16, 512], F32, kind="ExternalInput").ap()
    woT = nc.dram_tensor("woTt", [16, 128, 16, 128], F32, kind="ExternalInput").ap()
    acat = nc.dram_tensor("acatt", [128, 16, 512], F32, kind="ExternalInput").ap()
    wgT = nc.dram_tensor("wgTt", [128, 16, 8], F32, kind="ExternalInput").ap()
    bwT = nc.dram_tensor("bwTt", [128, 4, 2048], F32, kind="ExternalInput").ap()

    out_c = nc.dram_tensor("out_c", [TPC, D], F32, kind="ExternalOutput").ap()

    qT0 = nc.dram_tensor("qT0", [D, TPC], BF16, kind="Internal").ap()
    kT0 = nc.dram_tensor("kT0", [D, TPC], BF16, kind="Internal").ap()
    v0b = nc.dram_tensor("v0h", [H, 128, NCH, 128], BF16, kind="Internal").ap()
    U0 = nc.dram_tensor("U0", [NCH, H, 128, 128], F32, kind="Internal").ap()
    T0 = nc.dram_tensor("T0", [H * 128, 128], F32, kind="Internal").ap()
    T_all = nc.dram_tensor("T_all", [4 * H * 128, 128], F32, kind="Internal").ap()
    oT0 = nc.dram_tensor("oT0", [D, TPC], BF16, kind="Internal").ap()
    x1f = nc.dram_tensor("x1f", [D, TPC], F32, kind="Internal").ap()
    x1b = nc.dram_tensor("x1b", [D, TPC], BF16, kind="Internal").ap()
    rbuf = nc.dram_tensor("rbuf", [D, TPC], F32, kind="Internal").ap()
    x2f = nc.dram_tensor("x2f", [D, TPC], F32, kind="Internal").ap()
    x2bh = [nc.dram_tensor(f"x2bh{i}", [D, 1024], BF16, kind="Internal").ap()
            for i in range(2)]
    x2allh = [nc.dram_tensor(f"x2allh{i}", [4 * D, 1024], BF16, kind="Internal").ap()
              for i in range(2)]
    qT1 = nc.dram_tensor("qT1", [512, N], BF16, kind="Internal").ap()
    kT1 = nc.dram_tensor("kT1", [512, N], BF16, kind="Internal").ap()
    v1 = nc.dram_tensor("v1h", [4, 128, N // 128, 128], BF16, kind="Internal").ap()
    oT1 = nc.dram_tensor("oT1r", [2048, TPC], BF16, kind="Internal").ap()
    oTall = nc.dram_tensor("oTall", [4 * 2048, TPC], BF16, kind="Internal").ap()
    oTm = nc.dram_tensor("oTm", [D, TPC], BF16, kind="Internal").ap()
    x3f = nc.dram_tensor("x3f", [D, TPC], F32, kind="Internal").ap()
    x3b = nc.dram_tensor("x3b", [D, TPC], BF16, kind="Internal").ap()
    ybuf = nc.dram_tensor("ybuf", [D, TPC], F32, kind="Internal").ap()

    ident_t = nc.inline_tensor(IDENT_B16, "c_ident16").ap()
    identf_t = nc.inline_tensor(IDENT_F32, "c_identf").ap()
    onesrow_t = nc.inline_tensor(ONES_ROW_F32, "c_onesrow").ap()
    onescol_t = nc.inline_tensor(ONES_COL_B16, "c_onescol").ap()
    dkcol_t = nc.inline_tensor(DECAY_K_COL, "c_dkcol").ap()
    dmaskT_t = nc.inline_tensor(DMASKT_T, "c_dmaskT").ap()
    decq_t = nc.inline_tensor(np.ascontiguousarray(DECAY_Q_T), "c_decq").ap()
    diagm_t = nc.inline_tensor(L1_DIAG_MASKS, "c_diagm").ap()

    with ExitStack() as stack:
        ctx = Ctx(nc, stack)
        ctx.cc = Sem(nc, stack, "s_cc")
        onesrow = ctx.sb("onesrow", [1, 128], F32)
        onescol = ctx.sb("onescol", [128, 1], BF16)
        identb = ctx.sb("identb", [128, 128], BF16)
        identf = ctx.sb("identf", [128, 128], F32)
        bias16a = ctx.sb("bias16a", [128, 16], F32)
        bias32a = ctx.sb("bias32a", [128, 32], F32)
        bias32b = ctx.sb("bias32b", [128, 32], F32)
        bias16b = ctx.sb("bias16b", [128, 16], F32)

        with nc.Block("F0") as blk:
            @blk.gpsimd
            def _(g):
                for dst, src in [
                    (onesrow[:], onesrow_t), (onescol[:], onescol_t),
                    (identb[:], ident_t), (identf[:], identf_t),
                    (bias16a[:], bout_pc), (bias32a[:], b1_pc),
                    (bias32b[:], b2_pc), (bias16b[:], b3_pc),
                ]:
                    g.dma_start(out=dst, in_=src).then_inc(ctx.ld_g.h, 16)
                    ctx.ld_g.n += 16
                g.wait_ge(ctx.ld_g.h, ctx.ld_g.n)
        _bar(nc)

        with ExitStack() as stAB:
            dkcol = stAB.enter_context(nc.sbuf_tensor("dkcol", [128, H], F32))
            dmaskT = stAB.enter_context(nc.sbuf_tensor("dmaskT", [128, H, 128], F32))
            decq = stAB.enter_context(nc.sbuf_tensor("decq", [128, H, 128], BF16))
            coeff_sb = stAB.enter_context(nc.sbuf_tensor("coeff_sb", [128, 64], F32))
            sinit = stAB.enter_context(nc.sbuf_tensor("sinit", [128, H, 128], F32))
            with nc.Block("F0b") as blk:
                @blk.gpsimd
                def _(g):
                    for dst, src in [
                        (dkcol[:], dkcol_t),
                        (dmaskT[:], dmaskT_t.rearrange("h p e -> p h e")),
                        (decq[:], decq_t.rearrange("h p e -> p h e")),
                        (coeff_sb[:], coeff),
                    ]:
                        g.dma_start(out=dst, in_=src).then_inc(ctx.ld_g.h, 16)
                        ctx.ld_g.n += 16
                    g.wait_ge(ctx.ld_g.h, ctx.ld_g.n)
            _bar(nc)
            with ExitStack() as stA:
                xb = stA.enter_context(nc.sbuf_tensor("xb", [128, DC, TPC], BF16))
                biasqkv = stA.enter_context(nc.sbuf_tensor("biasqkv", [128, 48], F32))
                biasr = stA.enter_context(nc.sbuf_tensor("biasr", [1, 3 * D], F32))
                with nc.Block("A0") as blk:
                    @blk.gpsimd
                    def _(g):
                        for dst, src in [(xb[:], xT.rearrange("(c p) t -> p c t", p=128)),
                                         (biasqkv[:], bqkv_pc), (biasr[:], bqkv_row)]:
                            g.dma_start(out=dst, in_=src).then_inc(ctx.ld_g.h, 16)
                            ctx.ld_g.n += 16
                        g.wait_ge(ctx.ld_g.h, ctx.ld_g.n)
                _bar(nc)
                _emit_qk_fm(nc, ctx, xb, wqkvT, biasqkv, qT0, kT0)
                _bar(nc)
                _emit_v_tm(nc, ctx, xb, wvTt, biasr, onesrow, v0b)
                _bar(nc)
            _emit_UT(nc, ctx, kT0, v0b, dkcol, identb, U0, T0)
            _bar(nc)

            _emit_ag(nc, ctx, "AG1", T0, T_all)

            _emit_sinit(nc, ctx, T_all, coeff_sb, sinit)
            _bar(nc)
            _emit_scan(nc, ctx, qT0, kT0, v0b, U0, sinit, dmaskT, decq, oT0)
            _bar(nc)
        _emit_proj_ln2(nc, ctx, "B3", rhs_hbm=oT0, K_=D, w_hbm=woutT,
                       bias_col=bias16a, xold_hbm=xT,
                       xnew_f=x1f, xnew_b=x1b, onescol=onescol, onesrow=onesrow)
        cc_ag2h = [None, None]

        def _ag2_half(hf):
            with nc.Block(f"AG2i{hf}") as blk:
                @blk.gpsimd
                def _(g):
                    g.collective_compute(
                        "AllGather", ALU.bypass, replica_groups=RG_BATCH,
                        ins=[x2bh[hf].opt()], outs=[x2allh[hf].opt()],
                    ).then_inc(ctx.cc.h)
                    ctx.cc.n += 1
            cc_ag2h[hf] = ctx.cc.n

        _emit_ffn(nc, ctx, x1b, x1f, W1T, bias32a, W2T, bias32b, W3T, bias16b,
                  rbuf, x2f, None, onescol, onesrow, x2b_halves=x2bh,
                  post_half=_ag2_half)

        diagm = stack.enter_context(nc.sbuf_tensor("diagm", [128, 4, 512], BF16))
        with nc.Block("F0c") as blk:
            @blk.gpsimd
            def _(g):
                g.dma_start(out=diagm[:], in_=diagm_t.rearrange("j p t -> p j t")).then_inc(ctx.ld_g.h, 16)
                ctx.ld_g.n += 16
                g.wait_ge(ctx.ld_g.h, ctx.ld_g.n)
        _emit_c1_all(nc, ctx, x2allh, wqT, wkT, wvT, qT1, kT1, v1,
                     cc_gate=tuple(cc_ag2h))
        _emit_l1_attn(nc, ctx, qT1, kT1, v1, oT1, onesrow, onescol, diagm)
        _bar(nc)

        with nc.Block("AG3i") as blk:
            @blk.gpsimd
            def _(g):
                g.collective_compute(
                    "AllGather", ALU.bypass, replica_groups=RG_BATCH,
                    ins=[oT1.opt()], outs=[oTall.opt()],
                ).then_inc(ctx.cc.h)
                ctx.cc.n += 1
        cc_ag3 = ctx.cc.n
        with nc.Block("RSH") as blk:
            @blk.sync
            def _(sy):
                sy.wait_ge(ctx.cc.h, cc_ag3)
                pid = nc.partition_id()
                myr = pid % 4
                for gg in range(4):
                    src = _AP(oTall.tensor, (gg * 2048 + myr * 512) * TPC,
                              [[TPC, 512], [1, TPC]])
                    sy.dma_start(out=oTm[gg * 512:(gg + 1) * 512, :], in_=src).then_inc(ctx.ld_s.h, 16)
                    ctx.ld_s.n += 16
        rsh_gate = ctx.ld_s.n

        _emit_proj_ln2(nc, ctx, "D1", rhs_hbm=oTm, K_=D, w_hbm=woT,
                       bias_col=None, xold_hbm=x2f,
                       xnew_f=x3f, xnew_b=x3b, onescol=onescol, onesrow=onesrow,
                       rh_gate=rsh_gate)
        fn_rres = stack.enter_context(nc.sbuf_tensor("fnrr", [128, 2 * DC, 512], F32))
        fn_ysb = [stack.enter_context(nc.sbuf_tensor(f"fnyy{i}", [128, 512], F32)) for i in range(2)]
        fn_xsb = [stack.enter_context(nc.sbuf_tensor(f"fnxx{i}", [128, 512], F32)) for i in range(2)]
        yg = _emit_moe(nc, ctx, x3b, acat, wgT, bwT, ybuf, identb)
        _emit_final(nc, ctx, ybuf, x3f, out_c, onescol, onesrow, identf,
                    ygate=yg, pre=(fn_rres, fn_ysb, fn_xsb))
        _bar(nc)
    return nc


def _prep_fused(inputs, core):
    d = _prep_A(inputs, core)
    r = core % 4
    cs = slice(r * 512, (r + 1) * 512)
    d.update({
        "woutTt": _tile_w(inputs["L0_attn_w_out"].T),
        "bout_pc": np.ascontiguousarray(inputs["L0_attn_b_out"].reshape(16, 128).T),
        "W1Tt": _tile_w(inputs["L0_ffn_W1"].T),
        "b1_pc": np.ascontiguousarray(inputs["L0_ffn_b1"].reshape(32, 128).T),
        "W2Tt": _tile_w(inputs["L0_ffn_W2"].T),
        "b2_pc": np.ascontiguousarray(inputs["L0_ffn_b2"].reshape(32, 128).T),
        "W3Tt": _tile_w(inputs["L0_ffn_W3"].T),
        "b3_pc": np.ascontiguousarray(inputs["L0_ffn_b3"].reshape(16, 128).T),
        "coeff": _sinit_coeff_table(core % 4),
        "wqTt": _tile_w(inputs["L1_attn_Wq"].T[:, cs], 128),
        "wkTt": _tile_w(inputs["L1_attn_Wk"].T[:, cs], 128),
        "wvTt1": _tile_w(inputs["L1_attn_Wv"].T[:, cs], 512),
        "woTt": _tile_w(inputs["L1_attn_Wo"].T),
        "acatt": _tile_w(np.concatenate([inputs["L1_ffn_A"][e].T for e in range(8)], axis=1), 512)[0],
        "wgTt": _tile_w(inputs["L1_ffn_Wg"].T, 8)[0],
        "bwTt": _tile_w(np.concatenate([inputs["L1_ffn_Bw"][e].T for e in range(8)], axis=0), 2048)[0],
    })
    return d


def _tile_w(WT, mblk=128):
    K_, M = WT.shape
    return np.ascontiguousarray(
        WT.reshape(K_ // 128, 128, M // mblk, mblk).transpose(2, 1, 0, 3))


def _prep_A(inputs, core):
    xT = np.ascontiguousarray(
        inputs["L0_attn_x"].reshape(NTOK, D).T[:, core * TPC:(core + 1) * TPC]
    )
    wqkvT = inputs["L0_attn_w_qkv"].T
    b = inputs["L0_attn_b_qkv"]
    return {
        "xT": xT,
        "wqkTt": _tile_w(wqkvT[:, :2 * D], 128),
        "wvTt": _tile_w(wqkvT[:, 2 * D:], 512),
        "bqkv_pc": np.ascontiguousarray(b.reshape(48, 128).T),
        "bqkv_row": np.ascontiguousarray(b.reshape(1, 3 * D)),
    }


def _prep_B(inputs, core, aouts):
    grp = range(0, 4) if core < 4 else range(4, 8)
    T_all = np.concatenate([np.asarray(aouts[r]["T0"]) for r in grp], axis=0)
    d = {
        "xT": np.ascontiguousarray(
            inputs["L0_attn_x"].reshape(NTOK, D).T[:, core * TPC:(core + 1) * TPC]),
        "T_all": T_all,
        "qT0": np.asarray(aouts[core]["qT0"]),
        "kT0": np.asarray(aouts[core]["kT0"]),
        "v0h": np.asarray(aouts[core]["v0h"]),
        "U0": np.asarray(aouts[core]["U0"]),
        "woutTt": _tile_w(inputs["L0_attn_w_out"].T),
        "bout_pc": np.ascontiguousarray(inputs["L0_attn_b_out"].reshape(16, 128).T),
        "W1Tt": _tile_w(inputs["L0_ffn_W1"].T),
        "b1_pc": np.ascontiguousarray(inputs["L0_ffn_b1"].reshape(32, 128).T),
        "W2Tt": _tile_w(inputs["L0_ffn_W2"].T),
        "b2_pc": np.ascontiguousarray(inputs["L0_ffn_b2"].reshape(32, 128).T),
        "W3Tt": _tile_w(inputs["L0_ffn_W3"].T),
        "b3_pc": np.ascontiguousarray(inputs["L0_ffn_b3"].reshape(16, 128).T),
        "coeff": _sinit_coeff_table(core % 4),
    }
    return d


def _prep_C(inputs, core, bouts):
    grp = range(0, 4) if core < 4 else range(4, 8)
    r = core % 4
    cs = slice(r * 512, (r + 1) * 512)
    return {
        "x2all": np.concatenate([np.asarray(bouts[g]["x2b"]) for g in grp], axis=0),
        "wqTt": _tile_w(inputs["L1_attn_Wq"].T[:, cs], 128),
        "wkTt": _tile_w(inputs["L1_attn_Wk"].T[:, cs], 128),
        "wvTt1": _tile_w(inputs["L1_attn_Wv"].T[:, cs], 512),
    }


def _prep_D(inputs, core, bouts, couts):
    grp = list(range(0, 4)) if core < 4 else list(range(4, 8))
    r = core % 4
    oTm = np.concatenate(
        [np.asarray(couts[g]["oT1r"])[r * 512:(r + 1) * 512, :] for g in grp], axis=0)
    return {
        "oTm": np.ascontiguousarray(oTm),
        "x2f": np.asarray(bouts[core]["x2f"]),
        "woTt": _tile_w(inputs["L1_attn_Wo"].T),
        "acatt": _tile_w(np.concatenate([inputs["L1_ffn_A"][e].T for e in range(8)], axis=1), 512)[0],
        "wgTt": _tile_w(inputs["L1_ffn_Wg"].T, 8)[0],
        "bwTt": _tile_w(np.concatenate([inputs["L1_ffn_Bw"][e].T for e in range(8)], axis=0), 2048)[0],
    }


_NC_CACHE = {}


def _get_nc(name, builder):
    if name not in _NC_CACHE:
        _NC_CACHE[name] = builder()
    return _NC_CACHE[name]


def kernel(**inputs):
    import os
    if os.environ.get("MK_UNFUSED", "0") != "1":
        inputs = {k: np.asarray(v) for k, v in inputs.items()}
        cores = list(range(8))
        ncF = _get_nc("F", build_fused)
        res = run_bass_kernel_spmd(ncF, [_prep_fused(inputs, c) for c in cores], core_ids=cores)
        out = np.empty((NTOK, D), dtype=np.float32)
        for c in cores:
            out[c * TPC:(c + 1) * TPC] = np.asarray(res.results[c]["out_c"])
        return out.reshape(B, N, D)
    return _kernel_unfused(**inputs)


def _kernel_unfused(**inputs):
    inputs = {k: np.asarray(v) for k, v in inputs.items()}
    cores = list(range(8))

    ncA = _get_nc("A", build_launch_A)
    resA = run_bass_kernel_spmd(ncA, [_prep_A(inputs, c) for c in cores], core_ids=cores)
    aouts = resA.results

    ncB = _get_nc("B", build_launch_B)
    resB = run_bass_kernel_spmd(ncB, [_prep_B(inputs, c, aouts) for c in cores], core_ids=cores)
    bouts = resB.results

    ncC = _get_nc("C", build_launch_C)
    resC = run_bass_kernel_spmd(ncC, [_prep_C(inputs, c, bouts) for c in cores], core_ids=cores)
    couts = resC.results

    ncD = _get_nc("D", build_launch_D)
    resD = run_bass_kernel_spmd(ncD, [_prep_D(inputs, c, bouts, couts) for c in cores], core_ids=cores)

    out = np.empty((NTOK, D), dtype=np.float32)
    for c in cores:
        out[c * TPC:(c + 1) * TPC] = np.asarray(resD.results[c]["out_c"])
    return out.reshape(B, N, D)


# version 35 (final)
# speedup vs baseline: 1.0545x; 1.0054x over previous
import math
from contextlib import ExitStack

import numpy as np
import ml_dtypes

import concourse.bass as bass
import concourse.mybir as mybir
from concourse.bass_utils import run_bass_kernel_spmd
from concourse.ap import AP as _AP

F32 = mybir.dt.float32
BF16 = mybir.dt.bfloat16
AF = mybir.ActivationFunctionType
ALU = mybir.AluOpType
AX = mybir.AxisListType.X
BF = ml_dtypes.bfloat16

H, DH, D, B, N = 16, 128, 2048, 2, 8192
NTOK = B * N
TPC = NTOK // 8
CH = 128
NCH = TPC // CH
DC = D // 128
EPS = 1e-5



def _slopes(n_heads):
    start = 2.0 ** (-2.0 ** (-(math.log2(n_heads) - 3)))
    return np.array([start ** (i + 1) for i in range(n_heads)], dtype=np.float64)


LAM64 = np.exp(-_slopes(H))
LAM = LAM64.astype(np.float32)
LAMC = (LAM.astype(np.float64) ** CH)

_pos = np.arange(CH, dtype=np.float64)
DECAY_Q_T = np.broadcast_to(
    (LAM64[:, None] ** (_pos + 1.0))[:, None, :], (H, 128, CH)
).astype(BF)
DECAY_K_COL = (LAM64[None, :] ** (CH - 1.0 - _pos[:, None])).astype(np.float32)
_diff = _pos[None, :] - _pos[:, None]
DMASKT_T = np.where(
    _diff >= 0, LAM64[:, None, None] ** np.maximum(_diff, 0.0)[None], 0.0
).astype(np.float32)

IDENT_B16 = np.eye(128, dtype=BF)
IDENT_F32 = np.eye(128, dtype=np.float32)
ONES_COL_B16 = np.ones((128, 1), dtype=BF)
ONES_ROW_F32 = np.ones((1, 128), dtype=np.float32)

_L1M = np.zeros((4, 128, 512), dtype=np.float64)
for _j in range(4):
    for _i in range(4):
        blk = _L1M[_j][:, _i * 128:(_i + 1) * 128]
        if _i > _j:
            blk[:] = 1.0
        elif _i == _j:
            kk = np.arange(128)[:, None]
            qq = np.arange(128)[None, :]
            blk[:] = (kk <= qq)
L1_DIAG_MASKS = _L1M.astype(BF)

INV_SQRT_DH = 1.0 / math.sqrt(DH)


def _sinit_coeff_table(crank):
    tab = np.zeros((64,), dtype=np.float64)
    for h in range(H):
        for r in range(4):
            if r < crank:
                tab[h * 4 + r] = LAMC[h] ** (16 * (crank - 1 - r))
    return np.broadcast_to(tab[None, :], (128, 64)).astype(np.float32).copy()


class Sem:
    def __init__(self, nc, ctx, name):
        self.h = ctx.enter_context(nc.semaphore(name))
        self.n = 0

    def inc(self, instr, k=1):
        instr.then_inc(self.h, k)
        self.n += k
        return self.n


class Ctx:

    def __init__(self, nc, stack):
        self.nc = nc
        self.stack = stack
        self.ld_g = Sem(nc, stack, "s_ld_g")
        self.ld_s = Sem(nc, stack, "s_ld_s")
        self.mm = Sem(nc, stack, "s_mm")
        self.ev = Sem(nc, stack, "s_ev")
        self.sc = Sem(nc, stack, "s_sc")
        self.gp = Sem(nc, stack, "s_gp")
        self.slot = [Sem(nc, stack, f"s_sl{i}") for i in range(12)]

    def sb(self, name, shape, dtype):
        return self.stack.enter_context(self.nc.sbuf_tensor(name, shape, dtype))


def _bar(nc):
    nc.all_engine_barrier()


def build_launch_A(phases=("A1", "A2", "A3")):
    nc = bass.Bass("TRN2", target_bir_lowering=False, debug=False)
    xT = nc.dram_tensor("xT", [D, TPC], F32, kind="ExternalInput").ap()
    wqkvT = nc.dram_tensor("wqkTt", [32, 128, 16, 128], F32, kind="ExternalInput").ap()
    wvTt = nc.dram_tensor("wvTt", [4, 128, 16, 512], F32, kind="ExternalInput").ap()
    bqkv_pc = nc.dram_tensor("bqkv_pc", [128, 48], F32, kind="ExternalInput").ap()
    bqkv_row = nc.dram_tensor("bqkv_row", [1, 3 * D], F32, kind="ExternalInput").ap()

    qT0 = nc.dram_tensor("qT0", [D, TPC], BF16, kind="ExternalOutput").ap()
    kT0 = nc.dram_tensor("kT0", [D, TPC], BF16, kind="ExternalOutput").ap()
    v0b = nc.dram_tensor("v0h", [H, 128, NCH, 128], BF16, kind="ExternalOutput").ap()
    U0 = nc.dram_tensor("U0", [NCH, H, 128, 128], F32, kind="ExternalOutput").ap()
    T0 = nc.dram_tensor("T0", [H * 128, 128], F32, kind="ExternalOutput").ap()

    ident_t = nc.inline_tensor(IDENT_B16, "c_ident16").ap()
    onesrow_t = nc.inline_tensor(ONES_ROW_F32, "c_onesrow").ap()
    dkcol_t = nc.inline_tensor(DECAY_K_COL, "c_dkcol").ap()

    with ExitStack() as stack:
        ctx = Ctx(nc, stack)
        xb = ctx.sb("xb", [128, DC, TPC], BF16)
        biasc = ctx.sb("biasc", [128, 48], F32)
        biasr = ctx.sb("biasr", [1, 3 * D], F32)
        onesrow = ctx.sb("onesrow", [1, 128], F32)
        identb = ctx.sb("identb", [128, 128], BF16)
        dkcol = ctx.sb("dkcol", [128, H], F32)

        with nc.Block("A0") as blk:
            @blk.gpsimd
            def _(g):
                g.dma_start(out=xb[:], in_=xT.rearrange("(c p) t -> p c t", p=128)).then_inc(ctx.ld_g.h, 16)
                g.dma_start(out=biasc[:], in_=bqkv_pc).then_inc(ctx.ld_g.h, 16)
                g.dma_start(out=biasr[:], in_=bqkv_row).then_inc(ctx.ld_g.h, 16)
                g.dma_start(out=onesrow[:], in_=onesrow_t).then_inc(ctx.ld_g.h, 16)
                g.dma_start(out=identb[:], in_=ident_t).then_inc(ctx.ld_g.h, 16)
                g.dma_start(out=dkcol[:], in_=dkcol_t).then_inc(ctx.ld_g.h, 16)
                ctx.ld_g.n += 96
                g.wait_ge(ctx.ld_g.h, ctx.ld_g.n)
        _bar(nc)

        if "A1" in phases:
            _emit_qk_fm(nc, ctx, xb, wqkvT, biasc, qT0, kT0)
            _bar(nc)

        if "A2" in phases:
            _emit_v_tm(nc, ctx, xb, wvTt, biasr, onesrow, v0b)
            _bar(nc)

        if "A3" in phases:
            _emit_UT(nc, ctx, kT0, v0b, dkcol, identb, U0, T0)
            _bar(nc)
    return nc


def _emit_qk_fm(nc, ctx, xb, wqkvT, biasc, qT0, kT0):
    NB_T = TPC // 512
    NCB = 32
    with ExitStack() as st:
        wbuf = [st.enter_context(nc.sbuf_tensor(f"a1w{i}", [128, DC, 128], BF16)) for i in range(2)]
        wstg = [st.enter_context(nc.sbuf_tensor(f"a1wf{i}", [128, DC, 128], F32)) for i in range(2)]
        stage = [st.enter_context(nc.sbuf_tensor(f"a1s{i}", [128, TPC], BF16)) for i in range(2)]
        ps = [st.enter_context(nc.psum_tensor(f"a1p{i}", [128, 512], F32)) for i in range(8)]

        wsl = [ctx.slot[0], ctx.slot[1]]
        ssl = [ctx.slot[6], ctx.slot[7]]
        wbase = [s.n for s in wsl]
        sbase = [s.n for s in ssl]
        mm0, sc0, gp0 = ctx.mm.n, ctx.sc.n, ctx.gp.n
        with nc.Block("A1") as blk:
            @blk.gpsimd
            def _(g):
                for cb in range(min(2, NCB)):
                    g.dma_start(out=wstg[cb % 2][:], in_=wqkvT[cb]).then_inc(wsl[cb % 2].h, 16)
                    wsl[cb % 2].n += 16
                for cb in range(NCB):
                    g.wait_ge(wsl[cb % 2].h, wbase[cb % 2] + 16 * (cb // 2 + 1))
                    if cb >= 2:
                        g.wait_ge(ctx.mm.h, mm0 + (cb - 1) * NB_T)
                    g.tensor_copy(wbuf[cb % 2][:], wstg[cb % 2][:]).then_inc(ctx.gp.h, 1)
                    if cb + 2 < NCB:
                        g.wait_ge(ctx.gp.h, gp0 + cb + 1)
                        g.dma_start(out=wstg[cb % 2][:], in_=wqkvT[cb + 2]).then_inc(wsl[cb % 2].h, 16)
                        wsl[cb % 2].n += 16
                ctx.gp.n += NCB

            @blk.tensor
            def _(t):
                for cb in range(NCB):
                    t.wait_ge(ctx.gp.h, gp0 + cb + 1)
                    if cb >= 2:
                        t.wait_ge(ctx.sc.h, sc0 + (cb - 1) * NB_T)
                    bank = (cb % 2) * 4
                    for dc in range(DC):
                        for tb in range(NB_T):
                            ins = t.matmul(
                                ps[bank + tb][:],
                                wbuf[cb % 2][:, dc, :],
                                xb[:, dc, tb * 512:(tb + 1) * 512],
                                start=(dc == 0), stop=(dc == DC - 1),
                            )
                            if dc == DC - 1:
                                ins.then_inc(ctx.mm.h, 1)
                ctx.mm.n += NCB * NB_T

            @blk.scalar
            def _(s):
                for cb in range(NCB):
                    bank = (cb % 2) * 4
                    if cb >= 2:
                        s.wait_ge(ssl[cb % 2].h, sbase[cb % 2] + 16 * (cb // 2))
                    for tb in range(NB_T):
                        s.wait_ge(ctx.mm.h, mm0 + cb * NB_T + tb + 1)
                        s.activation(
                            stage[cb % 2][:, tb * 512:(tb + 1) * 512],
                            ps[bank + tb][:],
                            AF.Identity, bias=biasc[:, cb:cb + 1], scale=1.0,
                        ).then_inc(ctx.sc.h, 1)
                ctx.sc.n += NCB * NB_T

            @blk.sync
            def _(sy):
                for cb in range(NCB):
                    sy.wait_ge(ctx.sc.h, sc0 + (cb + 1) * NB_T)
                    dst = qT0 if cb < 16 else kT0
                    r0 = (cb % 16) * 128
                    sy.dma_start(out=dst[r0:r0 + 128, :], in_=stage[cb % 2][:]).then_inc(ssl[cb % 2].h, 16)
                    ssl[cb % 2].n += 16
                sy.wait_ge(ssl[0].h, ssl[0].n)
                sy.wait_ge(ssl[1].h, ssl[1].n)


def _emit_v_tm(nc, ctx, xb, wvTt, biasr, onesrow, v0b):
    NTC = TPC // 128
    with ExitStack() as st:
        wbuf = [st.enter_context(nc.sbuf_tensor(f"a2w{i}", [128, DC, 512], BF16)) for i in range(2)]
        stage = [st.enter_context(nc.sbuf_tensor(f"a2s{i}", [128, 512], BF16)) for i in range(4)]
        ps = [st.enter_context(nc.psum_tensor(f"a2p{i}", [128, 512], F32)) for i in range(4)]

        wsl = [ctx.slot[0], ctx.slot[1]]
        ssl = [ctx.slot[6], ctx.slot[7], ctx.slot[8], ctx.slot[9]]
        wbase = [s.n for s in wsl]
        sbase = [s.n for s in ssl]
        mm0, ev0 = ctx.mm.n, ctx.ev.n
        with nc.Block("A2") as blk:
            @blk.gpsimd
            def _(g):
                for nb in range(4):
                    if nb >= 2:
                        g.wait_ge(ctx.mm.h, mm0 + (nb - 1) * NTC)
                    g.dma_start(
                        out=wbuf[nb % 2][:], in_=wvTt[nb],
                    ).then_inc(wsl[nb % 2].h, 16)
                    wsl[nb % 2].n += 16

            @blk.tensor
            def _(t):
                for nb in range(4):
                    t.wait_ge(wsl[nb % 2].h, wbase[nb % 2] + 16 * (nb // 2 + 1))
                    for tc in range(NTC):
                        it = nb * NTC + tc
                        if it >= 4:
                            t.wait_ge(ctx.ev.h, ev0 + it - 3)
                        bank = it % 4
                        for dc in range(DC):
                            t.matmul(
                                ps[bank][:], xb[:, dc, tc * 128:(tc + 1) * 128],
                                wbuf[nb % 2][:, dc, :],
                                start=(dc == 0), stop=False,
                            )
                        t.matmul(
                            ps[bank][:], onesrow[:],
                            biasr[:, D * 2 + nb * 512:D * 2 + (nb + 1) * 512],
                            start=False, stop=True,
                        ).then_inc(ctx.mm.h, 1)
                ctx.mm.n += 64

            @blk.vector
            def _(v):
                for it in range(64):
                    v.wait_ge(ctx.mm.h, mm0 + it + 1)
                    if it >= 4:
                        v.wait_ge(ssl[it % 4].h, sbase[it % 4] + 64 * (it // 4))
                    v.tensor_copy(stage[it % 4][:], ps[it % 4][:]).then_inc(ctx.ev.h, 1)
                ctx.ev.n += 64

            @blk.sync
            def _(sy):
                for nb in range(4):
                    for tc in range(NTC):
                        it = nb * NTC + tc
                        sy.wait_ge(ctx.ev.h, ev0 + it + 1)
                        for j in range(4):
                            sy.dma_start(
                                out=v0b[nb * 4 + j, :, tc, :],
                                in_=stage[it % 4][:, j * 128:(j + 1) * 128],
                            ).then_inc(ssl[it % 4].h, 16)
                        ssl[it % 4].n += 64
                for s in ssl:
                    sy.wait_ge(s.h, s.n)


def _emit_UT(nc, ctx, kT0, v0b, dkcol, identb, U0, T0):
    with ExitStack() as st:
        kh = [st.enter_context(nc.sbuf_tensor(f"a3k{j}", [128, TPC], BF16)) for j in range(2)]
        vh = [st.enter_context(nc.sbuf_tensor(f"a3v{j}", [128, NCH, 128], BF16)) for j in range(2)]
        ktm = [st.enter_context(nc.sbuf_tensor(f"a3kt{i}", [128, 128], BF16)) for i in range(2)]
        ust = [st.enter_context(nc.sbuf_tensor(f"a3us{i}", [128, 512], F32)) for i in range(2)]
        tacc = st.enter_context(nc.sbuf_tensor("a3t", [128, 128], F32))
        pkt = [st.enter_context(nc.psum_tensor(f"a3pk{i}", [128, 128], BF16)) for i in range(2)]
        pU = [st.enter_context(nc.psum_tensor(f"a3pu{i}", [128, 512], F32)) for i in range(2)]

        lsl = [ctx.slot[10], ctx.slot[11]]
        usl = [ctx.slot[2], ctx.slot[3]]
        tsl = ctx.slot[4]
        lbase = [s.n for s in lsl]
        ubase = [s.n for s in usl]
        tbase = tsl.n
        mm0, ev0 = ctx.mm.n, ctx.ev.n
        NI = H * NCH
        NQ = NI // 4
        S = {}
        evv = ev0
        EH = {}
        for h in range(H):
            for c in range(NCH):
                i = h * NCH + c
                evv += 1; S["ek", i] = evv
                evv += 1; S["et", i] = evv
                if i % 4 == 3:
                    evv += 1; S["eu", i // 4] = evv
            EH[h] = evv
        ev_end = evv

        with nc.Block("A3") as blk:
            @blk.sync
            def _(sy):
                for h in range(H):
                    if h >= 2:
                        sy.wait_ge(ctx.mm.h, mm0 + 2 * NCH * (h - 1))
                        sy.wait_ge(ctx.ev.h, EH[h - 2])
                    sy.dma_start(out=kh[h % 2][:], in_=kT0[h * 128:(h + 1) * 128, :]).then_inc(lsl[h % 2].h, 16)
                    sy.dma_start(out=vh[h % 2][:], in_=v0b[h]).then_inc(lsl[h % 2].h, 16)
                    lsl[h % 2].n += 32
                for s in usl:
                    sy.wait_ge(s.h, s.n + 16 * (NQ // 2))
                sy.wait_ge(tsl.h, tsl.n + 16 * H)

            @blk.vector
            def _(v):
                for h in range(H):
                    for c in range(NCH):
                        i = h * NCH + c
                        v.wait_ge(ctx.mm.h, mm0 + 2 * i + 1)
                        if i >= 2:
                            v.wait_ge(ctx.mm.h, mm0 + 2 * (i - 2) + 2)
                        v.tensor_scalar_mul(ktm[i % 2][:], pkt[i % 2][:],
                                            dkcol[:, h:h + 1]).then_inc(ctx.ev.h, 1)
                        v.wait_ge(ctx.mm.h, mm0 + 2 * i + 2)
                        lam_w = float(LAMC[h] ** (NCH - 1 - c))
                        pslot = pU[(i // 4) % 2][:, (i % 4) * 128:(i % 4 + 1) * 128]
                        if c == 0:
                            if h >= 1:
                                v.wait_ge(tsl.h, tbase + 16 * h)
                            ins = v.tensor_scalar_mul(tacc[:], pslot, lam_w)
                        else:
                            ins = v.scalar_tensor_tensor(tacc[:], pslot, lam_w,
                                                         tacc[:], ALU.mult, ALU.add)
                        ins.then_inc(ctx.ev.h, 1)
                        if i % 4 == 3:
                            q = i // 4
                            if q >= 2:
                                v.wait_ge(usl[q % 2].h, ubase[q % 2] + 16 * (q // 2))
                            v.wait_ge(ctx.ev.h, S["et", i])
                            v.tensor_copy(ust[q % 2][:], pU[q % 2][:]).then_inc(ctx.ev.h, 1)
                ctx.ev.n = ev_end

            @blk.tensor
            def _(t):
                for h in range(H):
                    for c in range(NCH):
                        i = h * NCH + c
                        if c == 0:
                            t.wait_ge(lsl[h % 2].h, lbase[h % 2] + 32 * (h // 2 + 1))
                        if i >= 2:
                            t.wait_ge(ctx.ev.h, S["ek", i - 2])
                        t.transpose(pkt[i % 2][:], kh[h % 2][:, c * 128:(c + 1) * 128],
                                    identb[:]).then_inc(ctx.mm.h, 1)
                        if i >= 8:
                            t.wait_ge(ctx.ev.h, S["eu", i // 4 - 2])
                            t.wait_ge(ctx.ev.h, S["et", i - 8])
                        t.wait_ge(ctx.ev.h, S["ek", i])
                        t.matmul(pU[(i // 4) % 2][:, (i % 4) * 128:(i % 4 + 1) * 128],
                                 ktm[i % 2][:], vh[h % 2][:, c, :],
                                 start=True, stop=True).then_inc(ctx.mm.h, 1)
                ctx.mm.n += NI * 2

            @blk.gpsimd
            def _(g):
                for h in range(H):
                    for c in range(NCH):
                        i = h * NCH + c
                        if i % 4 == 3:
                            q = i // 4
                            g.wait_ge(ctx.ev.h, S["eu", q])
                            c0 = (q * 4) % NCH
                            g.dma_start(
                                out=_AP(U0.tensor, c0 * (H * 128 * 128) + h * (128 * 128),
                                        [[128, 128], [H * 128 * 128, 4], [1, 128]]),
                                in_=ust[q % 2][:]).then_inc(usl[q % 2].h, 16)
                            usl[q % 2].n += 16
                    g.wait_ge(ctx.ev.h, EH[h])
                    g.dma_start(out=T0[h * 128:(h + 1) * 128, :], in_=tacc[:]).then_inc(tsl.h, 16)
                    tsl.n += 16



def build_launch_B(phases=("B1", "B2", "B3", "B4"), dbg=False):
    nc = bass.Bass("TRN2", target_bir_lowering=False, debug=False)
    xT = nc.dram_tensor("xT", [D, TPC], F32, kind="ExternalInput").ap()
    T_all = nc.dram_tensor("T_all", [4 * H * 128, 128], F32, kind="ExternalInput").ap()
    qT0 = nc.dram_tensor("qT0", [D, TPC], BF16, kind="ExternalInput").ap()
    kT0 = nc.dram_tensor("kT0", [D, TPC], BF16, kind="ExternalInput").ap()
    v0b = nc.dram_tensor("v0h", [H, 128, NCH, 128], BF16, kind="ExternalInput").ap()
    U0 = nc.dram_tensor("U0", [NCH, H, 128, 128], F32, kind="ExternalInput").ap()
    woutT = nc.dram_tensor("woutTt", [16, 128, 16, 128], F32, kind="ExternalInput").ap()
    bout_pc = nc.dram_tensor("bout_pc", [128, 16], F32, kind="ExternalInput").ap()
    W1T = nc.dram_tensor("W1Tt", [32, 128, 16, 128], F32, kind="ExternalInput").ap()
    b1_pc = nc.dram_tensor("b1_pc", [128, 32], F32, kind="ExternalInput").ap()
    W2T = nc.dram_tensor("W2Tt", [32, 128, 32, 128], F32, kind="ExternalInput").ap()
    b2_pc = nc.dram_tensor("b2_pc", [128, 32], F32, kind="ExternalInput").ap()
    W3T = nc.dram_tensor("W3Tt", [16, 128, 32, 128], F32, kind="ExternalInput").ap()
    b3_pc = nc.dram_tensor("b3_pc", [128, 16], F32, kind="ExternalInput").ap()
    coeff = nc.dram_tensor("coeff", [128, 64], F32, kind="ExternalInput").ap()

    x2f = nc.dram_tensor("x2f", [D, TPC], F32, kind="ExternalOutput").ap()
    x2b = nc.dram_tensor("x2b", [D, TPC], BF16, kind="ExternalOutput").ap()

    ik = "ExternalOutput" if dbg else "Internal"
    oT0 = nc.dram_tensor("oT0", [D, TPC], BF16, kind=ik).ap()
    x1f = nc.dram_tensor("x1f", [D, TPC], F32, kind=ik).ap()
    x1b = nc.dram_tensor("x1b", [D, TPC], BF16, kind=ik).ap()
    rbuf = nc.dram_tensor("rbuf", [D, TPC], F32, kind="Internal").ap()

    ident_t = nc.inline_tensor(IDENT_B16, "c_ident16").ap()
    onesrow_t = nc.inline_tensor(ONES_ROW_F32, "c_onesrow").ap()
    onescol_t = nc.inline_tensor(ONES_COL_B16, "c_onescol").ap()
    dmaskT_t = nc.inline_tensor(DMASKT_T, "c_dmaskT").ap()
    decq_t = nc.inline_tensor(np.ascontiguousarray(DECAY_Q_T), "c_decq").ap()

    with ExitStack() as stack:
        ctx = Ctx(nc, stack)
        onesrow = ctx.sb("onesrow", [1, 128], F32)
        onescol = ctx.sb("onescol", [128, 1], BF16)
        identb = ctx.sb("identb", [128, 128], BF16)
        bias16a = ctx.sb("bias16a", [128, 16], F32)
        bias32a = ctx.sb("bias32a", [128, 32], F32)
        bias32b = ctx.sb("bias32b", [128, 32], F32)
        bias16b = ctx.sb("bias16b", [128, 16], F32)

        with nc.Block("B0") as blk:
            @blk.gpsimd
            def _(g):
                for dst, src in [
                    (onesrow[:], onesrow_t), (onescol[:], onescol_t), (identb[:], ident_t),
                    (bias16a[:], bout_pc), (bias32a[:], b1_pc),
                    (bias32b[:], b2_pc), (bias16b[:], b3_pc),
                ]:
                    g.dma_start(out=dst, in_=src).then_inc(ctx.ld_g.h, 16)
                    ctx.ld_g.n += 16
                g.wait_ge(ctx.ld_g.h, ctx.ld_g.n)
        _bar(nc)

        with ExitStack() as stB12:
            dmaskT = stB12.enter_context(nc.sbuf_tensor("dmaskT", [128, H, 128], F32))
            decq = stB12.enter_context(nc.sbuf_tensor("decq", [128, H, 128], BF16))
            coeff_sb = stB12.enter_context(nc.sbuf_tensor("coeff_sb", [128, 64], F32))
            sinit = stB12.enter_context(nc.sbuf_tensor("sinit", [128, H, 128], F32))
            with nc.Block("B0b") as blk:
                @blk.gpsimd
                def _(g):
                    for dst, src in [
                        (dmaskT[:], dmaskT_t.rearrange("h p e -> p h e")),
                        (decq[:], decq_t.rearrange("h p e -> p h e")),
                        (coeff_sb[:], coeff),
                    ]:
                        g.dma_start(out=dst, in_=src).then_inc(ctx.ld_g.h, 16)
                        ctx.ld_g.n += 16
                    g.wait_ge(ctx.ld_g.h, ctx.ld_g.n)
            _bar(nc)

            if "B1" in phases:
                _emit_sinit(nc, ctx, T_all, coeff_sb, sinit)
                _bar(nc)
            if "B2" in phases:
                _emit_scan(nc, ctx, qT0, kT0, v0b, U0, sinit, dmaskT, decq, oT0)
                _bar(nc)
        if "B3" in phases:
            _emit_proj_ln(nc, ctx, "B3", rhs_hbm=oT0, K_=D, M=D, w_hbm=woutT,
                          bias_col=bias16a, xold_hbm=xT, rbuf=rbuf,
                          xnew_f=x1f, xnew_b=x1b, onescol=onescol, onesrow=onesrow)
            _bar(nc)
        if "B4" in phases:
            _emit_ffn(nc, ctx, x1b, x1f, W1T, bias32a, W2T, bias32b, W3T, bias16b,
                      rbuf, x2f, x2b, onescol, onesrow)
            _bar(nc)
    return nc


def _emit_sinit(nc, ctx, T_all, coeff_sb, sinit):
    lsl = ctx.slot[10]
    lbase = lsl.n
    ev0 = ctx.ev.n
    with ExitStack() as st:
        tt = st.enter_context(nc.sbuf_tensor("b1t", [128, 4, 128], F32))
        with nc.Block("B1") as blk:
            @blk.sync
            def _(sy):
                for h in range(H):
                    if h >= 1:
                        sy.wait_ge(ctx.ev.h, ev0 + 4 * h)
                    for r in range(4):
                        sy.dma_start(out=tt[:, r, :], in_=T_all[(r * H + h) * 128:(r * H + h + 1) * 128, :]).then_inc(lsl.h, 16)
                        lsl.n += 16

            @blk.vector
            def _(v):
                for h in range(H):
                    v.wait_ge(lsl.h, lbase + 64 * (h + 1))
                    for r in range(4):
                        cc = coeff_sb[:, h * 4 + r:h * 4 + r + 1]
                        if r == 0:
                            ins = v.tensor_scalar_mul(sinit[:, h, :], tt[:, 0, :], cc)
                        else:
                            v.wait_ge(ctx.ev.h, ev0 + 4 * h + r)
                            ins = v.scalar_tensor_tensor(
                                sinit[:, h, :], tt[:, r, :], cc, sinit[:, h, :], ALU.mult, ALU.add)
                        ins.then_inc(ctx.ev.h, 1)
                ctx.ev.n += 4 * H


def _emit_scan(nc, ctx, qT0, kT0, v0b, U0, sinit, dmaskT, decq, oT0):
    lsl = [ctx.slot[8 + i] for i in range(4)]
    NP = H // 2
    with ExitStack() as st:
        osl = [ctx.slot[6], ctx.slot[7], Sem(nc, st, "b2os2"), Sem(nc, st, "b2os3")]
        lbase = [s.n for s in lsl]
        obase = [s.n for s in osl]
        store_val = {h: obase[h % 4] + 16 * (h // 4 + 1) for h in range(H)}
        qh = [st.enter_context(nc.sbuf_tensor(f"b2q{i}", [128, TPC], BF16)) for i in range(4)]
        kh = [st.enter_context(nc.sbuf_tensor(f"b2k{i}", [128, TPC], BF16)) for i in range(4)]
        vh = [st.enter_context(nc.sbuf_tensor(f"b2v{i}", [128, NCH, 128], BF16)) for i in range(4)]
        uh = [st.enter_context(nc.sbuf_tensor(f"b2u{i}", [128, NCH, 128], F32)) for i in range(4)]
        oh = [st.enter_context(nc.sbuf_tensor(f"b2o{i}", [128, TPC], BF16)) for i in range(4)]
        qd = [st.enter_context(nc.sbuf_tensor(f"b2qd{i}", [128, 128], BF16)) for i in range(4)]
        pm = [st.enter_context(nc.sbuf_tensor(f"b2pm{i}", [128, 128], BF16)) for i in range(4)]
        sf = [st.enter_context(nc.sbuf_tensor(f"b2sf{i}", [128, 128], F32)) for i in range(2)]
        sb16 = [st.enter_context(nc.sbuf_tensor(f"b2sb{i}", [128, 128], BF16)) for i in range(2)]
        pso = [st.enter_context(nc.psum_tensor(f"b2po{i}", [128, 128], F32)) for i in range(4)]
        pss = [st.enter_context(nc.psum_tensor(f"b2ps{i}", [128, 128], F32)) for i in range(4)]

        S = {}
        mmv, evv, scv, gpv = ctx.mm.n, ctx.ev.n, ctx.sc.n, ctx.gp.n
        steps = []
        for p in range(NP):
            scv += 1; S["sc_c", 2 * p, 0] = scv
            scv += 1; S["sc_c", 2 * p + 1, 0] = scv
            for c in range(NCH):
                for s in range(2):
                    h = 2 * p + s
                    steps.append((h, c))
                    gpv += 1; S["gp", h, c] = gpv
                    mmv += 2; S["mm_12", h, c] = mmv
                    mmv += 1; S["mm_3", h, c] = mmv
                    evv += 1; S["ev_pm", h, c] = evv
                    evv += 1; S["ev_s", h, c] = evv
                    scv += 1; S["sc_o", h, c] = scv
                    if c < NCH - 1:
                        scv += 1; S["sc_c", h, c + 1] = scv
        mm_end, ev_end, sc_end, gp_end = mmv, evv, scv, gpv

        with nc.Block("B2") as blk:
            @blk.sync
            def _(sy):
                for p in range(NP):
                    if p >= 2:
                        hl = 2 * p - 3
                        sy.wait_ge(ctx.ev.h, S["ev_s", hl, NCH - 1])
                        sy.wait_ge(ctx.mm.h, S["mm_3", hl, NCH - 1])
                        sy.wait_ge(ctx.gp.h, S["gp", hl, NCH - 1])
                    for s in range(2):
                        h = 2 * p + s
                        sl_ = lsl[h % 4]
                        sy.dma_start(out=qh[h % 4][:], in_=qT0[h * 128:(h + 1) * 128, :]).then_inc(sl_.h, 16)
                        sy.dma_start(out=kh[h % 4][:], in_=kT0[h * 128:(h + 1) * 128, :]).then_inc(sl_.h, 16)
                        sy.dma_start(out=vh[h % 4][:], in_=v0b[h]).then_inc(sl_.h, 16)
                        usrc = _AP(U0.tensor, h * (128 * 128),
                                   [[128, 128], [H * 128 * 128, NCH], [1, 128]])
                        sy.dma_start(out=uh[h % 4][:], in_=usrc).then_inc(sl_.h, 16)
                        sl_.n += 16 * 4

            @blk.gpsimd
            def _(g):
                for j, (h, c) in enumerate(steps):
                    if c == 0:
                        g.wait_ge(lsl[h % 4].h, lbase[h % 4] + 16 * 4 * (h // 4 + 1))
                    if j >= 4:
                        g.wait_ge(ctx.mm.h, S["mm_12", steps[j - 4][0], steps[j - 4][1]])
                    g.tensor_tensor(qd[j % 4][:], qh[h % 4][:, c * 128:(c + 1) * 128],
                                    decq[:, h, :], ALU.mult).then_inc(ctx.gp.h, 1)
                ctx.gp.n = gp_end

            @blk.tensor
            def _(t):
                for j, (h, c) in enumerate(steps):
                    t.wait_ge(ctx.sc.h, S["sc_c", h, c])
                    t.wait_ge(ctx.gp.h, S["gp", h, c])
                    if j >= 4:
                        t.wait_ge(ctx.ev.h, S["ev_pm", steps[j - 4][0], steps[j - 4][1]])
                    t.matmul(pso[j % 4][:], sb16[h % 2][:], qd[j % 4][:], start=True, stop=False)
                    t.matmul(pss[j % 4][:], kh[h % 4][:, c * 128:(c + 1) * 128],
                             qh[h % 4][:, c * 128:(c + 1) * 128], start=True, stop=True).then_inc(ctx.mm.h, 2)
                    t.wait_ge(ctx.ev.h, S["ev_pm", h, c])
                    t.matmul(pso[j % 4][:], vh[h % 4][:, c, :], pm[j % 4][:],
                             start=False, stop=True).then_inc(ctx.mm.h, 1)
                ctx.mm.n = mm_end

            @blk.vector
            def _(v):
                for j, (h, c) in enumerate(steps):
                    v.wait_ge(ctx.mm.h, S["mm_12", h, c])
                    v.tensor_tensor(pm[j % 4][:], pss[j % 4][:], dmaskT[:, h, :],
                                    ALU.mult).then_inc(ctx.ev.h, 1)
                    if c > 0:
                        v.wait_ge(ctx.ev.h, S["ev_s", h, c - 1])
                    v.scalar_tensor_tensor(sf[h % 2][:], sf[h % 2][:], float(LAMC[h]),
                                           uh[h % 4][:, c, :], ALU.mult,
                                           ALU.add).then_inc(ctx.ev.h, 1)
                ctx.ev.n = ev_end

            @blk.scalar
            def _(s):
                for p in range(NP):
                    for sx in range(2):
                        h = 2 * p + sx
                        if p >= 1:
                            s.wait_ge(ctx.mm.h, S["mm_12", h - 2, NCH - 1])
                            s.wait_ge(ctx.ev.h, S["ev_s", h - 2, NCH - 1])
                        s.copy(sf[sx][:], sinit[:, h, :])
                        s.copy(sb16[sx][:], sinit[:, h, :]).then_inc(ctx.sc.h, 1)
                    for c in range(NCH):
                        for sx in range(2):
                            h = 2 * p + sx
                            j = p * 2 * NCH + 2 * c + sx
                            s.wait_ge(ctx.mm.h, S["mm_3", h, c])
                            if c == 0 and h >= 4:
                                s.wait_ge(osl[h % 4].h, store_val[h - 4])
                            s.activation(oh[h % 4][:, c * 128:(c + 1) * 128], pso[j % 4][:],
                                         AF.Identity).then_inc(ctx.sc.h, 1)
                            if c < NCH - 1:
                                s.wait_ge(ctx.ev.h, S["ev_s", h, c])
                                s.wait_ge(ctx.mm.h, S["mm_12", h, c])
                                s.copy(sb16[sx][:], sf[sx][:]).then_inc(ctx.sc.h, 1)
                            else:
                                s.wait_ge(ctx.sc.h, S["sc_o", h, NCH - 1])
                                s.dma_start(out=oT0[h * 128:(h + 1) * 128, :],
                                            in_=oh[h % 4][:]).then_inc(osl[h % 4].h, 16)
                                osl[h % 4].n += 16
                for s_ in osl:
                    s.wait_ge(s_.h, s_.n)
                ctx.sc.n = sc_end


def _emit_proj_ln(nc, ctx, name, *, rhs_hbm=None, rhs_sb=None, K_, M, w_hbm,
                  bias_col, xold_hbm, rbuf, xnew_f, xnew_b, onescol, onesrow,
                  halves=(0, 1)):
    KC = K_ // 128
    MB = M // 128
    assert M == D
    for hf in halves:
        t0 = hf * 1024
        with ExitStack() as st:
            if rhs_sb is None:
                rh = st.enter_context(nc.sbuf_tensor(f"{name}rh{hf}", [128, KC, 1024], BF16))
                rhs_view = rh
                rhs_off = 0
            else:
                rhs_view = rhs_sb
                rhs_off = 0 if rhs_sb.shape[2] == 1024 else t0
            wbuf = [st.enter_context(nc.sbuf_tensor(f"{name}w{hf}{i}", [128, KC, 128], BF16)) for i in range(2)]
            wstg = [st.enter_context(nc.sbuf_tensor(f"{name}wf{hf}{i}", [128, 16, 128], F32)) for i in range(2)]
            xsb = [st.enter_context(nc.sbuf_tensor(f"{name}x{hf}{i}", [128, 512], F32)) for i in range(4)]
            ysb = [st.enter_context(nc.sbuf_tensor(f"{name}y{hf}{i}", [128, 512], F32)) for i in range(2)]
            rsb = [st.enter_context(nc.sbuf_tensor(f"{name}r{hf}{i}", [128, 512], F32)) for i in range(2)]
            rb16 = [st.enter_context(nc.sbuf_tensor(f"{name}rb{hf}{i}", [128, 512], BF16)) for i in range(2)]
            rq16 = [st.enter_context(nc.sbuf_tensor(f"{name}rq{hf}{i}", [128, 512], BF16)) for i in range(2)]
            rsb2 = [st.enter_context(nc.sbuf_tensor(f"{name}r2{hf}{i}", [128, 512], F32)) for i in range(2)]
            tmp = [st.enter_context(nc.sbuf_tensor(f"{name}tp{hf}{i}", [128, 512], F32)) for i in range(2)]
            ofst = [st.enter_context(nc.sbuf_tensor(f"{name}of{hf}{i}", [128, 512], F32)) for i in range(2)]
            obst = [st.enter_context(nc.sbuf_tensor(f"{name}ob{hf}{i}", [128, 512], BF16)) for i in range(2)]
            srow = st.enter_context(nc.sbuf_tensor(f"{name}sr{hf}", [1, 16, 512], F32))
            pp = [st.enter_context(nc.psum_tensor(f"{name}pp{hf}{i}", [128, 512], F32)) for i in range(4)]
            pstat = [st.enter_context(nc.psum_tensor(f"{name}pq{hf}{i}", [1, 512], F32)) for i in range(4)]

            wsl = [ctx.slot[0], ctx.slot[1]]
            rstsl = [ctx.slot[2], ctx.slot[3]]
            ostsl = [ctx.slot[4], ctx.slot[5]]
            xsl = [ctx.slot[6 + i] for i in range(4)]
            rldsl = [ctx.slot[10], ctx.slot[11]]
            wb_ = [s.n for s in wsl]
            rst_ = [s.n for s in rstsl]
            ost_ = [s.n for s in ostsl]
            xb_ = [s.n for s in xsl]
            rld_ = [s.n for s in rldsl]
            mm0, ev0, sc0, g0, gp0 = ctx.mm.n, ctx.ev.n, ctx.sc.n, ctx.ld_s.n, ctx.gp.n

            mmv = mm0
            mm_proj = {}
            mm_stats = {}
            mm_bcast = {}
            for ob in range(MB):
                for tb in range(2):
                    mmv += 1
                    mm_proj[ob * 2 + tb] = mmv
                for jj in ([ob * 2 - 2, ob * 2 - 1] if ob >= 1 else []):
                    mmv += 2
                    mm_stats[jj] = mmv
            for jj in (2 * MB - 2, 2 * MB - 1):
                mmv += 2
                mm_stats[jj] = mmv
            for tb in range(2):
                mmv += 2
                mm_bcast[tb] = mmv
            scv = sc0
            sc_y = {}
            sc_sq = {}
            for j in range(2 * MB):
                scv += 1
                sc_y[j] = scv
                if j >= 1:
                    scv += 1
                    sc_sq[j - 1] = scv
            scv += 1
            sc_sq[2 * MB - 1] = scv
            sc_m2 = {}
            sc_std = {}
            for tb in range(2):
                scv += 1
                sc_m2[tb] = scv
                scv += 1
                sc_std[tb] = scv
            evv = ev0
            ev_add = {}
            ev_cast = {}
            for j in range(2 * MB):
                evv += 1
                ev_add[j] = evv
                evv += 1
                ev_cast[j] = evv
            ev_fin = {}
            for tb in range(2):
                evv += 6
                ev_fin[tb] = evv
            ev_apply = {}
            for j in range(2 * MB):
                evv += 3
                ev_apply[j] = evv

            with nc.Block(f"{name}_{hf}") as blk:
                @blk.sync
                def _(sy):
                    if rhs_sb is None:
                        sy.dma_start(
                            out=rh[:],
                            in_=rhs_hbm[:, t0:t0 + 1024].rearrange("(c p) t -> p c t", p=128),
                        ).then_inc(ctx.ld_s.h, 16)
                        ctx.ld_s.n += 16
                    for j in range(2 * MB):
                        ob, tb = j // 2, j % 2
                        if j >= 4:
                            sy.wait_ge(ctx.ev.h, ev_add[j - 4])
                        sy.dma_start(
                            out=xsb[j % 4][:],
                            in_=xold_hbm[ob * 128:(ob + 1) * 128, t0 + tb * 512:t0 + (tb + 1) * 512],
                        ).then_inc(xsl[j % 4].h, 16)
                        xsl[j % 4].n += 16
                    for j in range(2 * MB):
                        ob, tb = j // 2, j % 2
                        sy.wait_ge(rstsl[j % 2].h, rst_[j % 2] + 16 * (j // 2 + 1))
                        if j >= 2:
                            sy.wait_ge(ctx.ev.h, ev_apply[j - 2])
                        sy.dma_start(
                            out=rsb2[j % 2][:],
                            in_=rbuf[ob * 128:(ob + 1) * 128, t0 + tb * 512:t0 + (tb + 1) * 512],
                        ).then_inc(rldsl[j % 2].h, 16)
                        rldsl[j % 2].n += 16

                @blk.gpsimd
                def _(g):
                    def rstore(j):
                        ob_, tb_ = j // 2, j % 2
                        g.wait_ge(ctx.ev.h, ev_add[j])
                        g.dma_start(
                            out=rbuf[ob_ * 128:(ob_ + 1) * 128, t0 + tb_ * 512:t0 + (tb_ + 1) * 512],
                            in_=rsb[j % 2][:],
                        ).then_inc(rstsl[j % 2].h, 16)
                        rstsl[j % 2].n += 16

                    NP = KC // 16
                    pieces = [(ob_, p_) for ob_ in range(MB) for p_ in range(NP)]

                    def pload(i):
                        ob_, p_ = pieces[i]
                        g.dma_start(out=wstg[i % 2][:],
                                    in_=w_hbm[ob_][:, p_ * 16:(p_ + 1) * 16, :]).then_inc(wsl[i % 2].h, 16)
                        wsl[i % 2].n += 16
                    for i in range(min(2, len(pieces))):
                        pload(i)
                    for i, (ob, p) in enumerate(pieces):
                        g.wait_ge(wsl[i % 2].h, wb_[i % 2] + 16 * (i // 2 + 1))
                        if p == 0 and ob >= 2:
                            g.wait_ge(ctx.mm.h, mm_proj[(ob - 1) * 2 + 1])
                        g.tensor_copy(wbuf[ob % 2][:, p * 16:(p + 1) * 16, :],
                                      wstg[i % 2][:]).then_inc(ctx.gp.h, 1)
                        if i + 2 < len(pieces):
                            g.wait_ge(ctx.gp.h, gp0 + i + 1)
                            pload(i + 2)
                        if p == NP - 1 and ob >= 1:
                            rstore(2 * (ob - 1))
                            rstore(2 * (ob - 1) + 1)
                    ctx.gp.n += len(pieces)
                    rstore(2 * MB - 2)
                    rstore(2 * MB - 1)
                    for j in range(2 * MB):
                        ob, tb = j // 2, j % 2
                        g.wait_ge(ctx.ev.h, ev_apply[j])
                        c0, c1 = t0 + tb * 512, t0 + (tb + 1) * 512
                        g.dma_start(out=xnew_f[ob * 128:(ob + 1) * 128, c0:c1],
                                    in_=ofst[j % 2][:]).then_inc(ostsl[j % 2].h, 16)
                        g.dma_start(out=xnew_b[ob * 128:(ob + 1) * 128, c0:c1],
                                    in_=obst[j % 2][:]).then_inc(ostsl[j % 2].h, 16)
                        ostsl[j % 2].n += 32
                    for s in ostsl:
                        g.wait_ge(s.h, s.n)

                @blk.tensor
                def _(t):
                    if rhs_sb is None:
                        t.wait_ge(ctx.ld_s.h, g0 + 16)
                    for ob in range(MB):
                        t.wait_ge(ctx.gp.h, gp0 + (KC // 16) * (ob + 1))
                        for tb in range(2):
                            j = ob * 2 + tb
                            bank = (ob % 2) * 2 + tb
                            if ob >= 2:
                                t.wait_ge(ctx.sc.h, sc_y[(ob - 2) * 2 + tb])
                            for kc in range(KC):
                                ins = t.matmul(
                                    pp[bank][:], wbuf[ob % 2][:, kc, :],
                                    rhs_view[:, kc, rhs_off + tb * 512:rhs_off + (tb + 1) * 512],
                                    start=(kc == 0), stop=(kc == KC - 1),
                                )
                            ins.then_inc(ctx.mm.h, 1)
                        for jj in ([(ob - 1) * 2, (ob - 1) * 2 + 1] if ob >= 1 else []):
                            ob2, tb2 = jj // 2, jj % 2
                            t.wait_ge(ctx.ev.h, ev_cast[jj])
                            t.wait_ge(ctx.sc.h, sc_sq[jj])
                            t.matmul(pstat[tb2][:], onescol[:], rb16[jj % 2][:],
                                     start=(ob2 == 0), stop=(ob2 == MB - 1))
                            t.matmul(pstat[2 + tb2][:], onescol[:], rq16[jj % 2][:],
                                     start=(ob2 == 0), stop=(ob2 == MB - 1)).then_inc(ctx.mm.h, 2)
                    for jj in (2 * MB - 2, 2 * MB - 1):
                        ob2, tb2 = jj // 2, jj % 2
                        t.wait_ge(ctx.ev.h, ev_cast[jj])
                        t.wait_ge(ctx.sc.h, sc_sq[jj])
                        t.matmul(pstat[tb2][:], onescol[:], rb16[jj % 2][:],
                                 start=(ob2 == 0), stop=(ob2 == MB - 1))
                        t.matmul(pstat[2 + tb2][:], onescol[:], rq16[jj % 2][:],
                                 start=(ob2 == 0), stop=(ob2 == MB - 1)).then_inc(ctx.mm.h, 2)
                    for tb in range(2):
                        t.wait_ge(ctx.ev.h, ev_fin[tb])
                        t.matmul(pp[tb][:], onesrow[:], srow[:, 2 + 8 * tb, :], start=True, stop=True)
                        t.matmul(pp[2 + tb][:], onesrow[:], srow[:, 6 + 8 * tb, :],
                                 start=True, stop=True).then_inc(ctx.mm.h, 2)
                    ctx.mm.n = mmv

                @blk.scalar
                def _(s):
                    for j in range(2 * MB):
                        ob, tb = j // 2, j % 2
                        bank = (ob % 2) * 2 + tb
                        s.wait_ge(ctx.mm.h, mm_proj[j])
                        if j >= 2:
                            s.wait_ge(ctx.ev.h, ev_add[j - 2])
                        if bias_col is not None:
                            s.activation(ysb[j % 2][:], pp[bank][:], AF.Identity,
                                         bias=bias_col[:, ob:ob + 1], scale=1.0).then_inc(ctx.sc.h, 1)
                        else:
                            s.activation(ysb[j % 2][:], pp[bank][:],
                                         AF.Identity).then_inc(ctx.sc.h, 1)
                        if j >= 1:
                            s.wait_ge(ctx.ev.h, ev_add[j - 1])
                            if j >= 3:
                                s.wait_ge(ctx.mm.h, mm_stats.get(j - 3, mm0))
                            s.activation(rq16[(j - 1) % 2][:], rsb[(j - 1) % 2][:],
                                         AF.Square).then_inc(ctx.sc.h, 1)
                    s.wait_ge(ctx.ev.h, ev_add[2 * MB - 1])
                    s.wait_ge(ctx.mm.h, mm_stats.get(2 * MB - 3, mm0))
                    s.activation(rq16[(2 * MB - 1) % 2][:], rsb[(2 * MB - 1) % 2][:],
                                 AF.Square).then_inc(ctx.sc.h, 1)
                    for tb in range(2):
                        base = 8 * tb
                        s.wait_ge(ctx.ev.h, ev_fin[tb] - 3)
                        s.activation(srow[:, 3 + base, :], srow[:, 2 + base, :],
                                     AF.Square).then_inc(ctx.sc.h, 1)
                        s.wait_ge(ctx.ev.h, ev_fin[tb] - 1)
                        s.activation(srow[:, 5 + base, :], srow[:, 4 + base, :],
                                     AF.Sqrt).then_inc(ctx.sc.h, 1)
                    ctx.sc.n = scv

                @blk.vector
                def _(v):
                    for j in range(2 * MB):
                        ob, tb = j // 2, j % 2
                        v.wait_ge(ctx.sc.h, sc_y[j])
                        v.wait_ge(xsl[j % 4].h, xb_[j % 4] + 16 * (j // 4 + 1))
                        if j >= 2:
                            v.wait_ge(rstsl[j % 2].h, rst_[j % 2] + 16 * (j // 2))
                            v.wait_ge(ctx.sc.h, sc_sq[j - 2])
                        v.tensor_add(rsb[j % 2][:], ysb[j % 2][:], xsb[j % 4][:]).then_inc(ctx.ev.h, 1)
                        if j >= 2:
                            v.wait_ge(ctx.mm.h, mm_stats.get(j - 2, mm0))
                        v.wait_ge(ctx.ev.h, ev_add[j])
                        v.tensor_copy(rb16[j % 2][:], rsb[j % 2][:]).then_inc(ctx.ev.h, 1)
                    for tb in range(2):
                        base = 8 * tb
                        v.wait_ge(ctx.mm.h, mm_stats[2 * MB - 1])
                        v.tensor_copy(srow[:, 0 + base, :], pstat[tb][:]).then_inc(ctx.ev.h, 1)
                        v.tensor_copy(srow[:, 1 + base, :], pstat[2 + tb][:]).then_inc(ctx.ev.h, 1)
                        v.wait_ge(ctx.ev.h, ev_fin[tb] - 4)
                        v.tensor_scalar_mul(srow[:, 2 + base, :], srow[:, 0 + base, :],
                                            1.0 / M).then_inc(ctx.ev.h, 1)
                        v.wait_ge(ctx.sc.h, sc_m2[tb])
                        v.wait_ge(ctx.ev.h, ev_fin[tb] - 3)
                        v.scalar_tensor_tensor(srow[:, 4 + base, :], srow[:, 1 + base, :],
                                               1.0 / M, srow[:, 3 + base, :],
                                               ALU.mult, ALU.subtract).then_inc(ctx.ev.h, 1)
                        v.wait_ge(ctx.ev.h, ev_fin[tb] - 2)
                        v.tensor_scalar_add(srow[:, 4 + base, :], srow[:, 4 + base, :],
                                            float(EPS)).then_inc(ctx.ev.h, 1)
                        v.wait_ge(ctx.sc.h, sc_std[tb])
                        v.reciprocal(srow[:, 6 + base, :], srow[:, 5 + base, :]).then_inc(ctx.ev.h, 1)
                    for j in range(2 * MB):
                        ob, tb = j // 2, j % 2
                        v.wait_ge(ctx.mm.h, mm_bcast[tb])
                        v.wait_ge(rldsl[j % 2].h, rld_[j % 2] + 16 * (j // 2 + 1))
                        if j >= 2:
                            v.wait_ge(ostsl[j % 2].h, ost_[j % 2] + 32 * (j // 2))
                        v.tensor_sub(tmp[j % 2][:], rsb2[j % 2][:], pp[tb][:]).then_inc(ctx.ev.h, 1)
                        v.wait_ge(ctx.ev.h, ev_apply[j] - 2)
                        v.tensor_mul(ofst[j % 2][:], tmp[j % 2][:], pp[2 + tb][:]).then_inc(ctx.ev.h, 1)
                        v.wait_ge(ctx.ev.h, ev_apply[j] - 1)
                        v.tensor_copy(obst[j % 2][:], ofst[j % 2][:]).then_inc(ctx.ev.h, 1)
                    ctx.ev.n = evv
        _bar(nc)


def _emit_proj_ln2(nc, ctx, name, *, rhs_hbm, K_, w_hbm, bias_col, xold_hbm,
                   xnew_f, xnew_b, onescol, onesrow, rh_gate=None):
    KC = K_ // 128
    MB = DC
    R = 34

    def rslot(hf, j):
        return j if hf == 0 else (32 + j) % R

    with ExitStack() as st:
        rres = st.enter_context(nc.sbuf_tensor(f"{name}rr", [128, R, 512], F32))
        rh = st.enter_context(nc.sbuf_tensor(f"{name}rh", [128, KC, 1024], BF16))
        wbuf = [st.enter_context(nc.sbuf_tensor(f"{name}w{i}", [128, KC, 128], BF16)) for i in range(2)]
        wstg = [st.enter_context(nc.sbuf_tensor(f"{name}wf{i}", [128, 16, 128], F32)) for i in range(2)]
        xsb = [st.enter_context(nc.sbuf_tensor(f"{name}x{i}", [128, 512], F32)) for i in range(4)]
        ysb = [st.enter_context(nc.sbuf_tensor(f"{name}y{i}", [128, 512], F32)) for i in range(2)]
        rb16 = [st.enter_context(nc.sbuf_tensor(f"{name}rb{i}", [128, 512], BF16)) for i in range(4)]
        rq16 = [st.enter_context(nc.sbuf_tensor(f"{name}rq{i}", [128, 512], BF16)) for i in range(4)]
        tmp = [st.enter_context(nc.sbuf_tensor(f"{name}tp{i}", [128, 512], F32)) for i in range(2)]
        ofst = [st.enter_context(nc.sbuf_tensor(f"{name}of{i}", [128, 512], F32)) for i in range(2)]
        obst = [st.enter_context(nc.sbuf_tensor(f"{name}ob{i}", [128, 512], BF16)) for i in range(2)]
        ofsp = [st.enter_context(nc.sbuf_tensor(f"{name}og{i}", [128, 512], F32)) for i in range(2)]
        obsp = [st.enter_context(nc.sbuf_tensor(f"{name}oh{i}", [128, 512], BF16)) for i in range(2)]
        tmpp = st.enter_context(nc.sbuf_tensor(f"{name}tq", [128, 512], F32))
        srow = st.enter_context(nc.sbuf_tensor(f"{name}sr", [1, 16, 512], F32))
        mbc = [st.enter_context(nc.sbuf_tensor(f"{name}mb{i}", [128, 512], F32)) for i in range(2)]
        sbc = [st.enter_context(nc.sbuf_tensor(f"{name}sc{i}", [128, 512], F32)) for i in range(2)]
        pb = [st.enter_context(nc.psum_tensor(f"{name}pp{i}", [128, 512], F32)) for i in range(4)]
        pbs = [st.enter_context(nc.psum_tensor(f"{name}pq{i}", [128, 512], F32)) for i in range(2)]
        pbc = [st.enter_context(nc.psum_tensor(f"{name}pc{i}", [128, 512], F32)) for i in range(2)]

        wsl = [ctx.slot[0], ctx.slot[1]]
        ostsl = [Sem(nc, st, f"{name}od0"), Sem(nc, st, f"{name}od1")]
        ostpl = [Sem(nc, st, f"{name}op0"), Sem(nc, st, f"{name}op1")]
        xsl = [ctx.slot[6 + i] for i in range(4)]
        rhsl = [ctx.slot[10], ctx.slot[11]]
        wb_ = [s.n for s in wsl]
        ost_ = [s.n for s in ostsl]
        ostp_ = [s.n for s in ostpl]
        xb_ = [s.n for s in xsl]
        rh_ = [s.n for s in rhsl]

        S = {}
        mmv, scv, evv, gpv = ctx.mm.n, ctx.sc.n, ctx.ev.n, ctx.gp.n
        for hf in range(2):
            for ob in range(MB):
                for tb in range(2):
                    mmv += 1; S["mm_p", hf, ob * 2 + tb] = mmv
                for jj in ([(ob - 2) * 2, (ob - 2) * 2 + 1] if ob >= 2 else []):
                    mmv += 2; S["mm_s", hf, jj] = mmv
            for jj in range(2 * MB - 4, 2 * MB):
                mmv += 2; S["mm_s", hf, jj] = mmv
            mmv += 4; S["mm_bc", hf] = mmv
        for hf in range(2):
            for j in range(2 * MB):
                scv += 1; S["sc_y", hf, j] = scv
                if j >= 1:
                    scv += 1; S["sc_b", hf, j - 1] = scv
                    scv += 1; S["sc_q", hf, j - 1] = scv
                    if hf == 1:
                        scv += 1; S["sc_o", 0, j - 1] = scv
            scv += 1; S["sc_b", hf, 2 * MB - 1] = scv
            scv += 1; S["sc_q", hf, 2 * MB - 1] = scv
            if hf == 1:
                scv += 1; S["sc_o", 0, 2 * MB - 1] = scv
            for tb in range(2):
                scv += 1; S["sc_m2", hf, tb] = scv
                scv += 1; S["sc_sd", hf, tb] = scv
            scv += 4; S["sc_mb", hf] = scv
        for jj in range(MB):
            scv += 1; S["sc_o", 1, jj] = scv
            scv += 1; S["sc_o", 1, MB + jj] = scv
        for j in range(2 * MB):
            evv += 1; S["ev_a", 0, j] = evv
        for tb in range(2):
            evv += 6; S["ev_f", 0, tb] = evv
        for j in range(2 * MB):
            evv += 2; S["ev_ap", 0, j] = evv
            evv += 1; S["ev_a", 1, j] = evv
        for tb in range(2):
            evv += 6; S["ev_f", 1, tb] = evv
        for j in range(MB):
            evv += 2; S["ev_ap", 1, j] = evv
        for hf in range(2):
            for ob in range(MB):
                gpv += 1; S["gp_w", hf, ob] = gpv
        for j in range(MB, 2 * MB):
            gpv += 2; S["gp_ap", 1, j] = gpv
        mm_end, sc_end, ev_end, gp_end = mmv, scv, evv, gpv

        with nc.Block(f"{name}") as blk:
            @blk.sync
            def _(sy):
                if rh_gate is not None:
                    sy.wait_ge(ctx.ld_s.h, rh_gate)
                sy.dma_start(
                    out=rh[:],
                    in_=rhs_hbm[:, 0:1024].rearrange("(c p) t -> p c t", p=128),
                ).then_inc(rhsl[0].h, 16)
                rhsl[0].n += 16
                for hf in range(2):
                    if hf == 1:
                        sy.wait_ge(ctx.mm.h, S["mm_p", 0, 2 * MB - 1])
                        sy.dma_start(
                            out=rh[:],
                            in_=rhs_hbm[:, 1024:2048].rearrange("(c p) t -> p c t", p=128),
                        ).then_inc(rhsl[1].h, 16)
                        rhsl[1].n += 16
                    for j in range(2 * MB):
                        ob, tb = j // 2, j % 2
                        jx = hf * 2 * MB + j
                        if jx >= 4:
                            hp, jp = divmod(jx - 4, 2 * MB)
                            sy.wait_ge(ctx.ev.h, S["ev_a", hp, jp])
                        t0 = hf * 1024
                        sy.dma_start(
                            out=xsb[jx % 4][:],
                            in_=xold_hbm[ob * 128:(ob + 1) * 128, t0 + tb * 512:t0 + (tb + 1) * 512],
                        ).then_inc(xsl[jx % 4].h, 16)
                        xsl[jx % 4].n += 16


            @blk.gpsimd
            def _(g):
                pieces = [(hf, ob) for hf in range(2) for ob in range(MB)]

                def pload(i):
                    hf_, ob_ = pieces[i]
                    g.dma_start(out=wstg[i % 2][:],
                                in_=w_hbm[ob_][:, 0:16, :]).then_inc(wsl[i % 2].h, 16)
                    wsl[i % 2].n += 16

                for i in range(2):
                    pload(i)
                for i, (hf, ob) in enumerate(pieces):
                    g.wait_ge(wsl[i % 2].h, wb_[i % 2] + 16 * (i // 2 + 1))
                    if i >= 2:
                        hp, op_ = pieces[i - 2]
                        g.wait_ge(ctx.mm.h, S["mm_p", hp, op_ * 2 + 1])
                    g.tensor_copy(wbuf[i % 2][:], wstg[i % 2][:]).then_inc(ctx.gp.h, 1)
                    if i + 2 < len(pieces):
                        g.wait_ge(ctx.gp.h, S["gp_w", hf, ob])
                        pload(i + 2)
                for jj in range(MB):
                    j = MB + jj
                    tb = j % 2
                    u = jj % 2
                    g.wait_ge(ctx.sc.h, S["sc_mb", 1])
                    if jj >= 2:
                        g.wait_ge(ostpl[u].h, ostp_[u] + 32 * (jj // 2))
                    g.wait_ge(ctx.gp.h, S["gp_ap", 1, j] - 2)
                    g.tensor_sub(tmpp[:], rres[:, rslot(1, j), :], mbc[tb][:]).then_inc(ctx.gp.h, 1)
                    g.wait_ge(ctx.gp.h, S["gp_ap", 1, j] - 1)
                    g.tensor_mul(ofsp[u][:], tmpp[:], sbc[tb][:]).then_inc(ctx.gp.h, 1)
                ctx.gp.n = gp_end

            @blk.tensor
            def _(t):
                for hf in range(2):
                    for ob in range(MB):
                        t.wait_ge(ctx.gp.h, S["gp_w", hf, ob])
                        if ob == 0:
                            t.wait_ge(rhsl[hf].h, rh_[hf] + 16)
                        for tb in range(2):
                            j = ob * 2 + tb
                            bank = (ob % 2) * 2 + tb
                            if ob >= 2:
                                t.wait_ge(ctx.sc.h, S["sc_y", hf, (ob - 2) * 2 + tb])
                            for kc in range(KC):
                                ins = t.matmul(
                                    pb[bank][:], wbuf[(hf * MB + ob) % 2][:, kc, :],
                                    rh[:, kc, tb * 512:(tb + 1) * 512],
                                    start=(kc == 0), stop=(kc == KC - 1),
                                )
                            ins.then_inc(ctx.mm.h, 1)
                        for jj in ([(ob - 2) * 2, (ob - 2) * 2 + 1] if ob >= 2 else []):
                            ob2, tb2 = jj // 2, jj % 2
                            t.wait_ge(ctx.sc.h, S["sc_q", hf, jj])
                            if hf == 1 and ob2 == 0:
                                t.wait_ge(ctx.sc.h, S["sc_mb", 0])
                            t.matmul(pbs[tb2][0:1, :], onescol[:], rb16[jj % 4][:],
                                     start=(ob2 == 0), stop=(ob2 == MB - 1))
                            t.matmul(pbs[tb2][32:33, :], onescol[:], rq16[jj % 4][:],
                                     start=(ob2 == 0), stop=(ob2 == MB - 1)).then_inc(ctx.mm.h, 2)
                    for jj in range(2 * MB - 4, 2 * MB):
                        ob2, tb2 = jj // 2, jj % 2
                        t.wait_ge(ctx.sc.h, S["sc_q", hf, jj])
                        t.matmul(pbs[tb2][0:1, :], onescol[:], rb16[jj % 4][:],
                                 start=(ob2 == 0), stop=(ob2 == MB - 1))
                        t.matmul(pbs[tb2][32:33, :], onescol[:], rq16[jj % 4][:],
                                 start=(ob2 == 0), stop=(ob2 == MB - 1)).then_inc(ctx.mm.h, 2)
                    t.wait_ge(ctx.ev.h, S["ev_f", hf, 1])
                    if hf == 0:
                        t.wait_ge(ctx.sc.h, S["sc_y", 0, 2 * MB - 1])
                    t.matmul(pbc[0][:], onesrow[:], srow[:, 2, :], start=True, stop=True)
                    t.matmul(pbc[1][:], onesrow[:], srow[:, 6, :], start=True, stop=True)
                    t.matmul(pbs[0][:], onesrow[:], srow[:, 10, :], start=True, stop=True)
                    t.matmul(pbs[1][:], onesrow[:], srow[:, 14, :],
                             start=True, stop=True).then_inc(ctx.mm.h, 4)
                ctx.mm.n = mm_end

            @blk.scalar
            def _(s):
                def store_pair(hf_, j_, pool):
                    ob_, tb_ = j_ // 2, j_ % 2
                    t0_ = hf_ * 1024
                    c0, c1 = t0_ + tb_ * 512, t0_ + (tb_ + 1) * 512
                    s.wait_ge(ctx.sc.h, S["sc_o", hf_, j_])
                    if pool:
                        u = (j_ - MB) % 2
                        s.dma_start(out=xnew_f[ob_ * 128:(ob_ + 1) * 128, c0:c1],
                                    in_=ofsp[u][:]).then_inc(ostpl[u].h, 16)
                        s.dma_start(out=xnew_b[ob_ * 128:(ob_ + 1) * 128, c0:c1],
                                    in_=obsp[u][:]).then_inc(ostpl[u].h, 16)
                        ostpl[u].n += 32
                    else:
                        s.dma_start(out=xnew_f[ob_ * 128:(ob_ + 1) * 128, c0:c1],
                                    in_=ofst[j_ % 2][:]).then_inc(ostsl[j_ % 2].h, 16)
                        s.dma_start(out=xnew_b[ob_ * 128:(ob_ + 1) * 128, c0:c1],
                                    in_=obst[j_ % 2][:]).then_inc(ostsl[j_ % 2].h, 16)
                        ostsl[j_ % 2].n += 32

                def cast_o(hf_, j_):
                    if hf_ == 1 and j_ >= MB:
                        u = (j_ - MB) % 2
                        jj = j_ - MB
                        s.wait_ge(ctx.gp.h, S["gp_ap", 1, j_])
                        if jj >= 2:
                            s.wait_ge(ostpl[u].h, ostp_[u] + 32 * (jj // 2))
                        s.activation(obsp[u][:], ofsp[u][:],
                                     AF.Identity).then_inc(ctx.sc.h, 1)
                        return
                    a = hf_ * 2 * MB + j_ if hf_ == 0 else j_
                    s.wait_ge(ctx.ev.h, S["ev_ap", hf_, j_])
                    if hf_ == 0:
                        if j_ >= 2:
                            s.wait_ge(ostsl[j_ % 2].h, ost_[j_ % 2] + 32 * (j_ // 2))
                    else:
                        s.wait_ge(ostsl[j_ % 2].h, ost_[j_ % 2] + 32 * (MB + j_ // 2))
                    s.activation(obst[j_ % 2][:], ofst[j_ % 2][:],
                                 AF.Identity).then_inc(ctx.sc.h, 1)

                def cast_sq(hf, jj):
                    s.wait_ge(ctx.ev.h, S["ev_a", hf, jj])
                    if jj >= 4:
                        s.wait_ge(ctx.mm.h, S["mm_s", hf, jj - 4])
                    elif hf == 1:
                        s.wait_ge(ctx.mm.h, S["mm_s", 0, 2 * MB - 4 + jj])
                    s.activation(rb16[jj % 4][:], rres[:, rslot(hf, jj), :],
                                 AF.Identity).then_inc(ctx.sc.h, 1)
                    s.activation(rq16[jj % 4][:], rres[:, rslot(hf, jj), :],
                                 AF.Square).then_inc(ctx.sc.h, 1)

                for hf in range(2):
                    for j in range(2 * MB):
                        ob = j // 2
                        s.wait_ge(ctx.mm.h, S["mm_p", hf, j])
                        jy = hf * 2 * MB + j
                        if jy >= 2:
                            hp, jp = divmod(jy - 2, 2 * MB)
                            s.wait_ge(ctx.ev.h, S["ev_a", hp, jp])
                        bank = (ob % 2) * 2 + j % 2
                        if bias_col is not None:
                            s.activation(ysb[j % 2][:], pb[bank][:], AF.Identity,
                                         bias=bias_col[:, ob:ob + 1], scale=1.0).then_inc(ctx.sc.h, 1)
                        else:
                            s.activation(ysb[j % 2][:], pb[bank][:],
                                         AF.Identity).then_inc(ctx.sc.h, 1)
                        if j >= 1:
                            cast_sq(hf, j - 1)
                            if hf == 1:
                                cast_o(0, j - 1)
                                store_pair(0, j - 1, False)
                    cast_sq(hf, 2 * MB - 1)
                    if hf == 1:
                        cast_o(0, 2 * MB - 1)
                        store_pair(0, 2 * MB - 1, False)
                    for tb in range(2):
                        base = 8 * tb
                        s.wait_ge(ctx.ev.h, S["ev_f", hf, tb] - 3)
                        s.activation(srow[:, 3 + base, :], srow[:, 2 + base, :],
                                     AF.Square).then_inc(ctx.sc.h, 1)
                        s.wait_ge(ctx.ev.h, S["ev_f", hf, tb] - 1)
                        s.activation(srow[:, 5 + base, :], srow[:, 4 + base, :],
                                     AF.Sqrt).then_inc(ctx.sc.h, 1)
                    s.wait_ge(ctx.mm.h, S["mm_bc", hf])
                    if hf == 1:
                        s.wait_ge(ctx.ev.h, S["ev_ap", 0, 2 * MB - 1])
                    s.activation(mbc[0][:], pbc[0][:], AF.Identity).then_inc(ctx.sc.h, 1)
                    s.activation(sbc[0][:], pbc[1][:], AF.Identity).then_inc(ctx.sc.h, 1)
                    s.activation(mbc[1][:], pbs[0][:], AF.Identity).then_inc(ctx.sc.h, 1)
                    s.activation(sbc[1][:], pbs[1][:], AF.Identity).then_inc(ctx.sc.h, 1)
                for jj in range(MB):
                    cast_o(1, jj)
                    store_pair(1, jj, False)
                    cast_o(1, MB + jj)
                    store_pair(1, MB + jj, True)
                for s_ in ostsl + ostpl:
                    s.wait_ge(s_.h, s_.n)
                ctx.sc.n = sc_end

            @blk.vector
            def _(v):
                myev = ctx.ev.n

                def vop(ins):
                    nonlocal myev
                    ins.then_inc(ctx.ev.h, 1)
                    myev += 1

                def add_op(hf, j):
                    jx = hf * 2 * MB + j
                    v.wait_ge(ctx.sc.h, S["sc_y", hf, j])
                    v.wait_ge(xsl[jx % 4].h, xb_[jx % 4] + 16 * (jx // 4 + 1))
                    if hf == 1 and j >= 2:
                        v.wait_ge(ctx.ev.h, S["ev_ap", 0, j - 2])
                    vop(v.tensor_add(rres[:, rslot(hf, j), :], ysb[j % 2][:], xsb[jx % 4][:]))

                def fin(hf):
                    for tb in range(2):
                        base = 8 * tb
                        v.wait_ge(ctx.mm.h, S["mm_s", hf, 2 * MB - 1])
                        vop(v.tensor_copy(srow[:, 0 + base, :], pbs[tb][0:1, :]))
                        vop(v.tensor_copy(srow[:, 1 + base, :], pbs[tb][32:33, :]))
                        v.wait_ge(ctx.ev.h, myev)
                        vop(v.tensor_scalar_mul(srow[:, 2 + base, :], srow[:, 0 + base, :], 1.0 / D))
                        v.wait_ge(ctx.sc.h, S["sc_m2", hf, tb])
                        v.wait_ge(ctx.ev.h, myev)
                        vop(v.scalar_tensor_tensor(srow[:, 4 + base, :], srow[:, 1 + base, :],
                                                   1.0 / D, srow[:, 3 + base, :],
                                                   ALU.mult, ALU.subtract))
                        v.wait_ge(ctx.ev.h, myev)
                        vop(v.tensor_scalar_add(srow[:, 4 + base, :], srow[:, 4 + base, :],
                                                float(EPS)))
                        v.wait_ge(ctx.sc.h, S["sc_sd", hf, tb])
                        vop(v.reciprocal(srow[:, 6 + base, :], srow[:, 5 + base, :]))

                def apply_op(hf, j):
                    tb = j % 2
                    if j == 0:
                        v.wait_ge(ctx.sc.h, S["sc_mb", hf])
                    if hf == 0:
                        if j >= 2:
                            v.wait_ge(ostsl[j % 2].h, ost_[j % 2] + 32 * (j // 2))
                    else:
                        v.wait_ge(ostsl[j % 2].h, ost_[j % 2] + 32 * (MB + j // 2))
                    v.wait_ge(ctx.ev.h, myev)
                    vop(v.tensor_sub(tmp[j % 2][:], rres[:, rslot(hf, j), :], mbc[tb][:]))
                    v.wait_ge(ctx.ev.h, myev)
                    vop(v.tensor_mul(ofst[j % 2][:], tmp[j % 2][:], sbc[tb][:]))

                for j in range(2 * MB):
                    add_op(0, j)
                fin(0)
                for j in range(2 * MB):
                    apply_op(0, j)
                    add_op(1, j)
                fin(1)
                for j in range(MB):
                    apply_op(1, j)
                ctx.ev.n = ev_end
                assert myev == ev_end, (myev, ev_end)
    _bar(nc)


def _emit_proj_ln3(nc, ctx, name, *, rhs_sb, K_, w_hbm, bias_col, xold_hbm,
                   rbuf, xnew_f, xnew_b, onescol, onesrow, hf, xb_off=0):
    KC = K_ // 128
    MB = DC
    t0 = hf * 1024
    PW = 8
    NP = KC // PW

    def idx4(j):
        return (j % 2) * 2 + (j // 2) % 2

    with ExitStack() as st:
        wbuf = [st.enter_context(nc.sbuf_tensor(f"{name}w{i}", [128, KC, 128], BF16)) for i in range(3)]
        wstg = [st.enter_context(nc.sbuf_tensor(f"{name}wf{i}", [128, PW, 128], F32)) for i in range(4)]
        xsb = [st.enter_context(nc.sbuf_tensor(f"{name}x{i}", [128, 512], F32)) for i in range(4)]
        ysb = [st.enter_context(nc.sbuf_tensor(f"{name}y{i}", [128, 512], F32)) for i in range(4)]
        rsb = [st.enter_context(nc.sbuf_tensor(f"{name}r{i}", [128, 512], F32)) for i in range(4)]
        rb16 = [st.enter_context(nc.sbuf_tensor(f"{name}rb{i}", [128, 512], BF16)) for i in range(4)]
        rq16 = [st.enter_context(nc.sbuf_tensor(f"{name}rq{i}", [128, 512], BF16)) for i in range(4)]
        rsb2 = [st.enter_context(nc.sbuf_tensor(f"{name}r2{i}", [128, 512], F32)) for i in range(4)]
        tmp = [st.enter_context(nc.sbuf_tensor(f"{name}tp{i}", [128, 512], F32)) for i in range(2)]
        ofst = [st.enter_context(nc.sbuf_tensor(f"{name}of{i}", [128, 512], F32)) for i in range(4)]
        obst = [st.enter_context(nc.sbuf_tensor(f"{name}ob{i}", [128, 512], BF16)) for i in range(4)]
        srow = st.enter_context(nc.sbuf_tensor(f"{name}sr", [1, 16, 512], F32))
        mbc = [st.enter_context(nc.sbuf_tensor(f"{name}mb{i}", [128, 512], F32)) for i in range(2)]
        sbc = [st.enter_context(nc.sbuf_tensor(f"{name}sc{i}", [128, 512], F32)) for i in range(2)]
        pp = [st.enter_context(nc.psum_tensor(f"{name}pp{i}", [128, 512], F32)) for i in range(6)]
        pbs = [st.enter_context(nc.psum_tensor(f"{name}pq{i}", [128, 512], F32)) for i in range(2)]

        wsl = [ctx.slot[0], ctx.slot[1],
               Sem(nc, st, f"{name}w2"), Sem(nc, st, f"{name}w3")]
        rstsl = [ctx.slot[2], ctx.slot[3],
                 Sem(nc, st, f"{name}rs2"), Sem(nc, st, f"{name}rs3")]
        ostsl = [ctx.slot[4], ctx.slot[5],
                 Sem(nc, st, f"{name}os2"), Sem(nc, st, f"{name}os3")]
        xsl = [ctx.slot[6 + i] for i in range(4)]
        rldsl = [ctx.slot[10], ctx.slot[11],
                 Sem(nc, st, f"{name}rl2"), Sem(nc, st, f"{name}rl3")]
        wb_ = [s.n for s in wsl]
        rst_ = [s.n for s in rstsl]
        ost_ = [s.n for s in ostsl]
        xb_ = [s.n for s in xsl]
        rld_ = [s.n for s in rldsl]

        S = {}
        mmv, scv, evv, gpv = ctx.mm.n, ctx.sc.n, ctx.ev.n, ctx.gp.n
        def statjs(ob):
            if ob < 2:
                return []
            return [(ob - 2) * 2, (ob - 2) * 2 + 1]
        for ob in range(MB):
            for tb in range(2):
                mmv += 1; S["mm_p", ob * 2 + tb] = mmv
            for jj in statjs(ob):
                mmv += 2; S["mm_s", jj] = mmv
        for jj in range(2 * MB - 4, 2 * MB):
            mmv += 2; S["mm_s", jj] = mmv
        mmv += 4; S["mm_bc"] = mmv
        for ob in range(MB):
            for p in range(NP):
                i = ob * NP + p
                if i % 2 == 1:
                    scv += 1; S["sc_w", i] = scv
            for j in ([2 * (ob - 1), 2 * (ob - 1) + 1] if ob >= 1 else []):
                scv += 1; S["sc_y", j] = scv
                if j >= 1:
                    scv += 1; S["sc_b", j - 1] = scv
                    scv += 1; S["sc_q", j - 1] = scv
        for j in (2 * MB - 2, 2 * MB - 1):
            scv += 1; S["sc_y", j] = scv
            scv += 1; S["sc_b", j - 1] = scv
            scv += 1; S["sc_q", j - 1] = scv
        scv += 1; S["sc_b", 2 * MB - 1] = scv
        scv += 1; S["sc_q", 2 * MB - 1] = scv
        for tb in range(2):
            scv += 1; S["sc_m2", tb] = scv
            scv += 1; S["sc_sd", tb] = scv
        scv += 4; S["sc_mb"] = scv
        for j in range(2 * MB):
            scv += 1; S["sc_o", j] = scv
        for j in range(2 * MB):
            evv += 1; S["ev_a", j] = evv
        for tb in range(2):
            evv += 6; S["ev_f", tb] = evv
        for j in range(0, 2 * MB, 2):
            evv += 2; S["ev_ap", j] = evv
        for i in range(MB * NP):
            if i % 2 == 0:
                gpv += 1; S["gp_w", i] = gpv
        for j in range(1, 2 * MB, 2):
            gpv += 2; S["gp_ap", j] = gpv
        mm_end, sc_end, ev_end, gp_end = mmv, scv, evv, gpv

        with nc.Block(f"{name}") as blk:
            @blk.sync
            def _(sy):
                for j in range(2 * MB):
                    ob, tb = j // 2, j % 2
                    if j >= 4:
                        sy.wait_ge(ctx.ev.h, S["ev_a", j - 4])
                    sy.dma_start(
                        out=xsb[j % 4][:],
                        in_=xold_hbm[ob * 128:(ob + 1) * 128, t0 + tb * 512:t0 + (tb + 1) * 512],
                    ).then_inc(xsl[j % 4].h, 16)
                    xsl[j % 4].n += 16
                for j in range(2 * MB):
                    ob, tb = j // 2, j % 2
                    u = idx4(j)
                    sy.wait_ge(rstsl[j % 4].h, rst_[j % 4] + 16 * (j // 4 + 1))
                    if j >= 4:
                        if j % 2 == 0:
                            sy.wait_ge(ctx.ev.h, S["ev_ap", j - 4])
                        else:
                            sy.wait_ge(ctx.gp.h, S["gp_ap", j - 4])
                    sy.dma_start(
                        out=rsb2[u][:],
                        in_=rbuf[ob * 128:(ob + 1) * 128, t0 + tb * 512:t0 + (tb + 1) * 512],
                    ).then_inc(rldsl[u].h, 16)
                    rldsl[u].n += 16

            @blk.gpsimd
            def _(g):
                pieces = [(ob, p) for ob in range(MB) for p in range(NP)]

                def rstore(j):
                    ob_, tb_ = j // 2, j % 2
                    g.wait_ge(ctx.ev.h, S["ev_a", j])
                    g.dma_start(
                        out=rbuf[ob_ * 128:(ob_ + 1) * 128, t0 + tb_ * 512:t0 + (tb_ + 1) * 512],
                        in_=rsb[j % 4][:]).then_inc(rstsl[j % 4].h, 16)
                    rstsl[j % 4].n += 16

                def pload(i):
                    ob_, p_ = pieces[i]
                    g.dma_start(out=wstg[i % 4][:],
                                in_=w_hbm[ob_][:, p_ * PW:(p_ + 1) * PW, :]).then_inc(wsl[i % 4].h, 16)
                    wsl[i % 4].n += 16

                for i in range(min(4, len(pieces))):
                    pload(i)
                for i, (ob, p) in enumerate(pieces):
                    if i % 2 == 0:
                        g.wait_ge(wsl[i % 4].h, wb_[i % 4] + 16 * (i // 4 + 1))
                        if p == 0 and ob >= 2:
                            g.wait_ge(ctx.mm.h, S["mm_p", (ob - 2) * 2 + 1])
                        g.tensor_copy(wbuf[ob % 3][:, p * PW:(p + 1) * PW, :],
                                      wstg[i % 4][:]).then_inc(ctx.gp.h, 1)
                    if i + 4 < len(pieces):
                        if i % 2 == 0:
                            g.wait_ge(ctx.gp.h, S["gp_w", i])
                        else:
                            g.wait_ge(ctx.sc.h, S["sc_w", i])
                        pload(i + 4)
                    if p == NP - 1 and ob >= 1:
                        rstore(2 * (ob - 1))
                        rstore(2 * (ob - 1) + 1)
                rstore(2 * MB - 2)
                rstore(2 * MB - 1)
                for j in range(2 * MB):
                    ob, tb = j // 2, j % 2
                    u = idx4(j)
                    c0, c1 = t0 + tb * 512, t0 + (tb + 1) * 512
                    if j % 2 == 1:
                        g.wait_ge(ctx.sc.h, S["sc_mb"])
                        g.wait_ge(rldsl[u].h, rld_[u] + 16 * (j // 4 + 1))
                        if j >= 4:
                            g.wait_ge(ostsl[u].h, ost_[u] + 32 * (j // 4))
                        g.tensor_sub(tmp[1][:], rsb2[u][:], mbc[tb][:]).then_inc(ctx.gp.h, 1)
                        g.wait_ge(ctx.gp.h, S["gp_ap", j] - 1)
                        g.tensor_mul(ofst[u][:], tmp[1][:], sbc[tb][:]).then_inc(ctx.gp.h, 1)
                        g.wait_ge(ctx.gp.h, S["gp_ap", j])
                    else:
                        g.wait_ge(ctx.ev.h, S["ev_ap", j])
                    g.wait_ge(ctx.sc.h, S["sc_o", j])
                    g.dma_start(out=xnew_f[ob * 128:(ob + 1) * 128, c0:c1],
                                in_=ofst[u][:]).then_inc(ostsl[u].h, 16)
                    g.dma_start(out=xnew_b[ob * 128:(ob + 1) * 128,
                                            c0 - xb_off:c1 - xb_off],
                                in_=obst[u][:]).then_inc(ostsl[u].h, 16)
                    ostsl[u].n += 32
                for s_ in ostsl:
                    g.wait_ge(s_.h, s_.n)
                ctx.gp.n = gp_end

            @blk.tensor
            def _(t):
                for ob in range(MB):
                    t.wait_ge(ctx.gp.h, S["gp_w", ob * NP + NP - 2])
                    t.wait_ge(ctx.sc.h, S["sc_w", ob * NP + NP - 1])
                    for tb in range(2):
                        j = ob * 2 + tb
                        bank = (ob % 3) * 2 + tb
                        if ob >= 3:
                            t.wait_ge(ctx.sc.h, S["sc_y", (ob - 3) * 2 + tb])
                        for kc in range(KC):
                            ins = t.matmul(
                                pp[bank][:], wbuf[ob % 3][:, kc, :],
                                rhs_sb[:, kc, tb * 512:(tb + 1) * 512],
                                start=(kc == 0), stop=(kc == KC - 1),
                            )
                        ins.then_inc(ctx.mm.h, 1)
                    for jj in statjs(ob):
                        ob2, tb2 = jj // 2, jj % 2
                        t.wait_ge(ctx.sc.h, S["sc_q", jj])
                        t.matmul(pbs[tb2][0:1, :], onescol[:], rb16[jj % 4][:],
                                 start=(ob2 == 0), stop=(ob2 == MB - 1))
                        t.matmul(pbs[tb2][32:33, :], onescol[:], rq16[jj % 4][:],
                                 start=(ob2 == 0), stop=(ob2 == MB - 1)).then_inc(ctx.mm.h, 2)
                for jj in range(2 * MB - 4, 2 * MB):
                    ob2, tb2 = jj // 2, jj % 2
                    t.wait_ge(ctx.sc.h, S["sc_q", jj])
                    t.matmul(pbs[tb2][0:1, :], onescol[:], rb16[jj % 4][:],
                             start=(ob2 == 0), stop=(ob2 == MB - 1))
                    t.matmul(pbs[tb2][32:33, :], onescol[:], rq16[jj % 4][:],
                             start=(ob2 == 0), stop=(ob2 == MB - 1)).then_inc(ctx.mm.h, 2)
                t.wait_ge(ctx.ev.h, S["ev_f", 1])
                t.wait_ge(ctx.sc.h, S["sc_y", 2 * MB - 1])
                t.matmul(pp[0][:], onesrow[:], srow[:, 2, :], start=True, stop=True)
                t.matmul(pp[2][:], onesrow[:], srow[:, 6, :], start=True, stop=True)
                t.matmul(pp[1][:], onesrow[:], srow[:, 10, :], start=True, stop=True)
                t.matmul(pp[3][:], onesrow[:], srow[:, 14, :],
                         start=True, stop=True).then_inc(ctx.mm.h, 4)
                ctx.mm.n = mm_end

            @blk.scalar
            def _(s):
                def cast_sq(jj):
                    s.wait_ge(ctx.ev.h, S["ev_a", jj])
                    if jj >= 4:
                        s.wait_ge(ctx.mm.h, S["mm_s", jj - 4])
                    s.activation(rb16[jj % 4][:], rsb[jj % 4][:], AF.Identity).then_inc(ctx.sc.h, 1)
                    s.activation(rq16[jj % 4][:], rsb[jj % 4][:], AF.Square).then_inc(ctx.sc.h, 1)

                def evac(j):
                    ob_ = j // 2
                    bank = (ob_ % 3) * 2 + j % 2
                    s.wait_ge(ctx.mm.h, S["mm_p", j])
                    if j >= 4:
                        s.wait_ge(ctx.ev.h, S["ev_a", j - 4])
                    if bias_col is not None:
                        s.activation(ysb[j % 4][:], pp[bank][:], AF.Identity,
                                     bias=bias_col[:, ob_:ob_ + 1], scale=1.0).then_inc(ctx.sc.h, 1)
                    else:
                        s.activation(ysb[j % 4][:], pp[bank][:],
                                     AF.Identity).then_inc(ctx.sc.h, 1)

                for ob in range(MB):
                    for p in range(NP):
                        i = ob * NP + p
                        if i % 2 == 1:
                            s.wait_ge(wsl[i % 4].h, wb_[i % 4] + 16 * (i // 4 + 1))
                            if p == 0 and ob >= 2:
                                s.wait_ge(ctx.mm.h, S["mm_p", (ob - 2) * 2 + 1])
                            s.activation(wbuf[ob % 3][:, p * PW:(p + 1) * PW, :],
                                         wstg[i % 4][:], AF.Identity).then_inc(ctx.sc.h, 1)
                    for j in ([2 * (ob - 1), 2 * (ob - 1) + 1] if ob >= 1 else []):
                        evac(j)
                        if j >= 1:
                            cast_sq(j - 1)
                for j in (2 * MB - 2, 2 * MB - 1):
                    evac(j)
                    cast_sq(j - 1)
                cast_sq(2 * MB - 1)
                for tb in range(2):
                    base = 8 * tb
                    s.wait_ge(ctx.ev.h, S["ev_f", tb] - 3)
                    s.activation(srow[:, 3 + base, :], srow[:, 2 + base, :],
                                 AF.Square).then_inc(ctx.sc.h, 1)
                    s.wait_ge(ctx.ev.h, S["ev_f", tb] - 1)
                    s.activation(srow[:, 5 + base, :], srow[:, 4 + base, :],
                                 AF.Sqrt).then_inc(ctx.sc.h, 1)
                s.wait_ge(ctx.mm.h, S["mm_bc"])
                s.activation(mbc[0][:], pp[0][:], AF.Identity).then_inc(ctx.sc.h, 1)
                s.activation(mbc[1][:], pp[1][:], AF.Identity).then_inc(ctx.sc.h, 1)
                s.activation(sbc[0][:], pp[2][:], AF.Identity).then_inc(ctx.sc.h, 1)
                s.activation(sbc[1][:], pp[3][:], AF.Identity).then_inc(ctx.sc.h, 1)
                for j in range(2 * MB):
                    u = idx4(j)
                    if j % 2 == 1:
                        s.wait_ge(ctx.gp.h, S["gp_ap", j])
                    else:
                        s.wait_ge(ctx.ev.h, S["ev_ap", j])
                    if j >= 4:
                        s.wait_ge(ostsl[u].h, ost_[u] + 32 * (j // 4))
                    s.activation(obst[u][:], ofst[u][:],
                                 AF.Identity).then_inc(ctx.sc.h, 1)
                ctx.sc.n = sc_end

            @blk.vector
            def _(v):
                myev = ctx.ev.n

                def vop(ins):
                    nonlocal myev
                    ins.then_inc(ctx.ev.h, 1)
                    myev += 1

                for j in range(2 * MB):
                    v.wait_ge(ctx.sc.h, S["sc_y", j])
                    v.wait_ge(xsl[j % 4].h, xb_[j % 4] + 16 * (j // 4 + 1))
                    if j >= 4:
                        v.wait_ge(rstsl[j % 4].h, rst_[j % 4] + 16 * (j // 4))
                        v.wait_ge(ctx.sc.h, S["sc_q", j - 4])
                    vop(v.tensor_add(rsb[j % 4][:], ysb[j % 4][:], xsb[j % 4][:]))
                for tb in range(2):
                    base = 8 * tb
                    v.wait_ge(ctx.mm.h, S["mm_s", 2 * MB - 1])
                    vop(v.tensor_copy(srow[:, 0 + base, :], pbs[tb][0:1, :]))
                    vop(v.tensor_copy(srow[:, 1 + base, :], pbs[tb][32:33, :]))
                    v.wait_ge(ctx.ev.h, myev)
                    vop(v.tensor_scalar_mul(srow[:, 2 + base, :], srow[:, 0 + base, :], 1.0 / D))
                    v.wait_ge(ctx.sc.h, S["sc_m2", tb])
                    v.wait_ge(ctx.ev.h, myev)
                    vop(v.scalar_tensor_tensor(srow[:, 4 + base, :], srow[:, 1 + base, :],
                                               1.0 / D, srow[:, 3 + base, :],
                                               ALU.mult, ALU.subtract))
                    v.wait_ge(ctx.ev.h, myev)
                    vop(v.tensor_scalar_add(srow[:, 4 + base, :], srow[:, 4 + base, :],
                                            float(EPS)))
                    v.wait_ge(ctx.sc.h, S["sc_sd", tb])
                    vop(v.reciprocal(srow[:, 6 + base, :], srow[:, 5 + base, :]))
                for j in range(0, 2 * MB, 2):
                    tb = j % 2
                    u = idx4(j)
                    if j == 0:
                        v.wait_ge(ctx.sc.h, S["sc_mb"])
                    v.wait_ge(rldsl[u].h, rld_[u] + 16 * (j // 4 + 1))
                    if j >= 4:
                        v.wait_ge(ostsl[u].h, ost_[u] + 32 * (j // 4))
                    v.wait_ge(ctx.ev.h, myev)
                    vop(v.tensor_sub(tmp[0][:], rsb2[u][:], mbc[tb][:]))
                    v.wait_ge(ctx.ev.h, myev)
                    vop(v.tensor_mul(ofst[u][:], tmp[0][:], sbc[tb][:]))
                ctx.ev.n = ev_end
                assert myev == ev_end, (myev, ev_end)
    _bar(nc)


def _emit_relu_proj(nc, ctx, name, *, rhs_sb, rhs_off, K_, M, w_hbm, bias_col,
                    out_sb, relu=True):
    KC = K_ // 128
    MB = M // 128
    NP = KC // 16
    pieces = [(ob, p) for ob in range(MB) for p in range(NP)]
    with ExitStack() as st:
        wbuf = [st.enter_context(nc.sbuf_tensor(f"{name}w{i}", [128, KC, 128], BF16)) for i in range(2)]
        wstg = [st.enter_context(nc.sbuf_tensor(f"{name}wf{i}", [128, 16, 128], F32)) for i in range(2)]
        pp = [st.enter_context(nc.psum_tensor(f"{name}p{i}", [128, 512], F32)) for i in range(4)]
        wsl = [ctx.slot[0], ctx.slot[1]]
        wb_ = [s.n for s in wsl]
        mm0, sc0, gp0 = ctx.mm.n, ctx.sc.n, ctx.gp.n
        with nc.Block(name) as blk:
            @blk.gpsimd
            def _(g):
                def pload(i):
                    ob_, p_ = pieces[i]
                    g.dma_start(out=wstg[i % 2][:],
                                in_=w_hbm[ob_][:, p_ * 16:(p_ + 1) * 16, :]).then_inc(wsl[i % 2].h, 16)
                    wsl[i % 2].n += 16
                for i in range(min(2, len(pieces))):
                    pload(i)
                for i, (ob, p) in enumerate(pieces):
                    g.wait_ge(wsl[i % 2].h, wb_[i % 2] + 16 * (i // 2 + 1))
                    if p == 0 and ob >= 2:
                        g.wait_ge(ctx.mm.h, mm0 + (ob - 1) * 2)
                    g.tensor_copy(wbuf[ob % 2][:, p * 16:(p + 1) * 16, :],
                                  wstg[i % 2][:]).then_inc(ctx.gp.h, 1)
                    if i + 2 < len(pieces):
                        g.wait_ge(ctx.gp.h, gp0 + i + 1)
                        pload(i + 2)
                ctx.gp.n += len(pieces)

            @blk.tensor
            def _(t):
                for ob in range(MB):
                    t.wait_ge(ctx.gp.h, gp0 + NP * (ob + 1))
                    if ob >= 2:
                        t.wait_ge(ctx.sc.h, sc0 + (ob - 1) * 2)
                    for tb in range(2):
                        bank = (ob % 2) * 2 + tb
                        for kc in range(KC):
                            ins = t.matmul(
                                pp[bank][:], wbuf[ob % 2][:, kc, :],
                                rhs_sb[:, kc, rhs_off + tb * 512:rhs_off + (tb + 1) * 512],
                                start=(kc == 0), stop=(kc == KC - 1),
                            )
                        ins.then_inc(ctx.mm.h, 1)
                ctx.mm.n += MB * 2

            @blk.scalar
            def _(s):
                for ob in range(MB):
                    for tb in range(2):
                        bank = (ob % 2) * 2 + tb
                        s.wait_ge(ctx.mm.h, mm0 + ob * 2 + tb + 1)
                        s.activation(out_sb[:, ob, tb * 512:(tb + 1) * 512], pp[bank][:],
                                     AF.Relu if relu else AF.Identity,
                                     bias=bias_col[:, ob:ob + 1], scale=1.0).then_inc(ctx.sc.h, 1)
                ctx.sc.n += MB * 2
    _bar(nc)


def _emit_ffn(nc, ctx, x1b, x1f, W1T, b1c, W2T, b2c, W3T, b3c, rbuf, x2f, x2b,
              onescol, onesrow, x2b_halves=None, post_half=None):
    for hf in range(2):
        t0 = hf * 1024
        with ExitStack() as st:
            h2 = st.enter_context(nc.sbuf_tensor(f"ffh2_{hf}", [128, 32, 1024], BF16))
            with ExitStack() as st2:
                h1 = st2.enter_context(nc.sbuf_tensor(f"ffh1_{hf}", [128, 32, 1024], BF16))
                xh = st2.enter_context(nc.sbuf_tensor(f"ffx_{hf}", [128, 16, 1024], BF16))
                with nc.Block(f"ffl{hf}") as blk:
                    @blk.sync
                    def _(sy):
                        sy.dma_start(
                            out=xh[:],
                            in_=x1b[:, t0:t0 + 1024].rearrange("(c p) t -> p c t", p=128),
                        ).then_inc(ctx.ld_s.h, 16)
                        ctx.ld_s.n += 16
                        sy.wait_ge(ctx.ld_s.h, ctx.ld_s.n)
                _bar(nc)
                _emit_relu_proj(nc, ctx, f"ff1_{hf}", rhs_sb=xh, rhs_off=0, K_=D,
                                M=4096, w_hbm=W1T, bias_col=b1c, out_sb=h1)
                _emit_relu_proj(nc, ctx, f"ff2_{hf}", rhs_sb=h1, rhs_off=0, K_=4096,
                                M=4096, w_hbm=W2T, bias_col=b2c, out_sb=h2)
            xb = x2b if x2b_halves is None else x2b_halves[hf]
            xoff = 0 if x2b_halves is None else hf * 1024
            _emit_proj_ln3(nc, ctx, f"ff3_{hf}", rhs_sb=h2, K_=4096,
                           w_hbm=W3T, bias_col=b3c, xold_hbm=x1f, rbuf=rbuf,
                           xnew_f=x2f, xnew_b=xb, onescol=onescol,
                           onesrow=onesrow, hf=hf, xb_off=xoff)
        if post_half is not None:
            post_half(hf)


def build_launch_C(dbg=False):
    nc = bass.Bass("TRN2", target_bir_lowering=False, debug=False)
    x2all = nc.dram_tensor("x2all", [4 * D, TPC], BF16, kind="ExternalInput").ap()
    wqT = nc.dram_tensor("wqTt", [4, 128, 16, 128], F32, kind="ExternalInput").ap()
    wkT = nc.dram_tensor("wkTt", [4, 128, 16, 128], F32, kind="ExternalInput").ap()
    wvT = nc.dram_tensor("wvTt1", [1, 128, 16, 512], F32, kind="ExternalInput").ap()

    oT1 = nc.dram_tensor("oT1r", [2048, TPC], BF16, kind="ExternalOutput").ap()
    ik = "ExternalOutput" if dbg else "Internal"
    qT1 = nc.dram_tensor("qT1", [512, N], BF16, kind=ik).ap()
    kT1 = nc.dram_tensor("kT1", [512, N], BF16, kind=ik).ap()
    v1 = nc.dram_tensor("v1h", [4, 128, N // 128, 128], BF16, kind=ik).ap()

    onesrow_t = nc.inline_tensor(ONES_ROW_F32, "c_onesrow").ap()
    onescol_t = nc.inline_tensor(ONES_COL_B16, "c_onescol").ap()
    diagm_t = nc.inline_tensor(L1_DIAG_MASKS, "c_diagm").ap()

    with ExitStack() as stack:
        ctx = Ctx(nc, stack)
        onesrow = ctx.sb("onesrow", [1, 128], F32)
        onescol = ctx.sb("onescol", [128, 1], BF16)
        diagm = ctx.sb("diagm", [128, 4, 512], BF16)

        with nc.Block("C0") as blk:
            @blk.gpsimd
            def _(g):
                for dst, src in [(onesrow[:], onesrow_t), (onescol[:], onescol_t),
                                 (diagm[:], diagm_t.rearrange("j p t -> p j t"))]:
                    g.dma_start(out=dst, in_=src).then_inc(ctx.ld_g.h, 16)
                    ctx.ld_g.n += 16
                g.wait_ge(ctx.ld_g.h, ctx.ld_g.n)
        _bar(nc)

        _emit_c1_all(nc, ctx, x2all, wqT, wkT, wvT, qT1, kT1, v1)

        _emit_l1_attn(nc, ctx, qT1, kT1, v1, oT1, onesrow, onescol, diagm)
        _bar(nc)
    return nc


def _emit_qk_fm_c(nc, ctx, r, xr, wqT, wkT, qT1, kT1):
    NB_T = TPC // 512
    with ExitStack() as st:
        wbuf = [st.enter_context(nc.sbuf_tensor(f"c1w{r}{i}", [128, DC, 128], BF16)) for i in range(2)]
        wstg = [st.enter_context(nc.sbuf_tensor(f"c1wf{r}{i}", [128, DC, 128], F32)) for i in range(2)]
        stage = [st.enter_context(nc.sbuf_tensor(f"c1s{r}{i}", [128, TPC], BF16)) for i in range(2)]
        ps = [st.enter_context(nc.psum_tensor(f"c1p{r}{i}", [128, 512], F32)) for i in range(8)]
        wsl = [ctx.slot[0], ctx.slot[1]]
        ssl = [ctx.slot[6], ctx.slot[7]]
        wbase = [s.n for s in wsl]
        sbase = [s.n for s in ssl]
        mm0, sc0, gp0 = ctx.mm.n, ctx.sc.n, ctx.gp.n
        with nc.Block(f"C1qk{r}") as blk:
            @blk.gpsimd
            def _(g):
                for cb in range(2):
                    w = wqT if cb < 4 else wkT
                    g.dma_start(out=wstg[cb % 2][:], in_=w[cb % 4]).then_inc(wsl[cb % 2].h, 16)
                    wsl[cb % 2].n += 16
                for cb in range(8):
                    g.wait_ge(wsl[cb % 2].h, wbase[cb % 2] + 16 * (cb // 2 + 1))
                    if cb >= 2:
                        g.wait_ge(ctx.mm.h, mm0 + (cb - 1) * NB_T)
                    g.tensor_copy(wbuf[cb % 2][:], wstg[cb % 2][:]).then_inc(ctx.gp.h, 1)
                    if cb + 2 < 8:
                        g.wait_ge(ctx.gp.h, gp0 + cb + 1)
                        w2 = wqT if cb + 2 < 4 else wkT
                        g.dma_start(out=wstg[cb % 2][:], in_=w2[(cb + 2) % 4]).then_inc(wsl[cb % 2].h, 16)
                        wsl[cb % 2].n += 16
                ctx.gp.n += 8

            @blk.tensor
            def _(t):
                for cb in range(8):
                    t.wait_ge(ctx.gp.h, gp0 + cb + 1)
                    if cb >= 2:
                        t.wait_ge(ctx.sc.h, sc0 + (cb - 1) * NB_T)
                    bank = (cb % 2) * 4
                    for dc in range(DC):
                        for tb in range(NB_T):
                            ins = t.matmul(
                                ps[bank + tb][:], wbuf[cb % 2][:, dc, :],
                                xr[:, dc, tb * 512:(tb + 1) * 512],
                                start=(dc == 0), stop=(dc == DC - 1),
                            )
                            if dc == DC - 1:
                                ins.then_inc(ctx.mm.h, 1)
                ctx.mm.n += 8 * NB_T

            @blk.scalar
            def _(s):
                for cb in range(8):
                    bank = (cb % 2) * 4
                    if cb >= 2:
                        s.wait_ge(ssl[cb % 2].h, sbase[cb % 2] + 16 * (cb // 2))
                    for tb in range(NB_T):
                        s.wait_ge(ctx.mm.h, mm0 + cb * NB_T + tb + 1)
                        s.activation(stage[cb % 2][:, tb * 512:(tb + 1) * 512],
                                     ps[bank + tb][:], AF.Identity).then_inc(ctx.sc.h, 1)
                ctx.sc.n += 8 * NB_T

            @blk.sync
            def _(sy):
                for cb in range(8):
                    sy.wait_ge(ctx.sc.h, sc0 + (cb + 1) * NB_T)
                    dst = qT1 if cb < 4 else kT1
                    r0 = (cb % 4) * 128
                    sy.dma_start(out=dst[r0:r0 + 128, r * TPC:(r + 1) * TPC],
                                 in_=stage[cb % 2][:]).then_inc(ssl[cb % 2].h, 16)
                    ssl[cb % 2].n += 16
                sy.wait_ge(ssl[0].h, ssl[0].n)
                sy.wait_ge(ssl[1].h, ssl[1].n)


def _emit_v_tm_c(nc, ctx, r, xr, wvT, v1):
    NTC = TPC // 128
    with ExitStack() as st:
        wbuf = st.enter_context(nc.sbuf_tensor(f"c2w{r}", [128, DC, 512], BF16))
        wstg = st.enter_context(nc.sbuf_tensor(f"c2wf{r}", [128, DC, 512], F32))
        stage = [st.enter_context(nc.sbuf_tensor(f"c2s{r}{i}", [128, 512], BF16)) for i in range(4)]
        ps = [st.enter_context(nc.psum_tensor(f"c2p{r}{i}", [128, 512], F32)) for i in range(4)]
        wsl = ctx.slot[0]
        ssl = [ctx.slot[6], ctx.slot[7], ctx.slot[8], ctx.slot[9]]
        wbase = wsl.n
        sbase = [s.n for s in ssl]
        mm0, ev0, gp0 = ctx.mm.n, ctx.ev.n, ctx.gp.n
        with nc.Block(f"C1v{r}") as blk:
            @blk.gpsimd
            def _(g):
                g.dma_start(out=wstg[:], in_=wvT[0]).then_inc(wsl.h, 16)
                wsl.n += 16
                g.wait_ge(wsl.h, wbase + 16)
                g.tensor_copy(wbuf[:], wstg[:]).then_inc(ctx.gp.h, 1)
                ctx.gp.n += 1

            @blk.tensor
            def _(t):
                t.wait_ge(ctx.gp.h, gp0 + 1)
                for tc in range(NTC):
                    if tc >= 4:
                        t.wait_ge(ctx.ev.h, ev0 + tc - 3)
                    bank = tc % 4
                    for dc in range(DC):
                        ins = t.matmul(ps[bank][:], xr[:, dc, tc * 128:(tc + 1) * 128],
                                       wbuf[:, dc, :], start=(dc == 0), stop=(dc == DC - 1))
                    ins.then_inc(ctx.mm.h, 1)
                ctx.mm.n += NTC

            @blk.vector
            def _(v):
                for tc in range(NTC):
                    v.wait_ge(ctx.mm.h, mm0 + tc + 1)
                    if tc >= 4:
                        v.wait_ge(ssl[tc % 4].h, sbase[tc % 4] + 64 * (tc // 4))
                    v.tensor_copy(stage[tc % 4][:], ps[tc % 4][:]).then_inc(ctx.ev.h, 1)
                ctx.ev.n += NTC

            @blk.sync
            def _(sy):
                for tc in range(NTC):
                    sy.wait_ge(ctx.ev.h, ev0 + tc + 1)
                    for j in range(4):
                        sy.dma_start(out=v1[j, :, r * NTC + tc, :],
                                     in_=stage[tc % 4][:, j * 128:(j + 1) * 128]).then_inc(ssl[tc % 4].h, 16)
                    ssl[tc % 4].n += 64
                for s in ssl:
                    sy.wait_ge(s.h, s.n)


def _emit_c1_all(nc, ctx, x2all, wqT, wkT, wvT, qT1, kT1, v1, cc_gate=None):
    NU = 8

    with ExitStack() as st:
        xrh = [st.enter_context(nc.sbuf_tensor(f"c1x{i}", [128, DC, 1024], BF16)) for i in range(3)]
        wbufr = st.enter_context(nc.sbuf_tensor("c1wqk", [128, 8, DC, 128], BF16))
        wstg = [st.enter_context(nc.sbuf_tensor(f"c1wf{i}", [128, 8, 128], F32)) for i in range(4)]
        wvb = st.enter_context(nc.sbuf_tensor("c1wv", [128, DC, 512], BF16))
        wstgv = [st.enter_context(nc.sbuf_tensor(f"c1wg{i}", [128, 2, 512], F32)) for i in range(4)]
        qkst = [st.enter_context(nc.sbuf_tensor(f"c1s{i}", [128, 1024], BF16)) for i in range(2)]
        vst = [st.enter_context(nc.sbuf_tensor(f"c1t{i}", [128, 512], BF16)) for i in range(4)]
        ps = [st.enter_context(nc.psum_tensor(f"c1p{i}", [128, 512], F32)) for i in range(8)]

        lsl = [Sem(nc, st, f"c1l{i}") for i in range(3)]
        wsl = [ctx.slot[0], ctx.slot[1], ctx.slot[2], ctx.slot[3]]
        qsl = [ctx.slot[6], ctx.slot[7]]
        vsl = [ctx.slot[8], ctx.slot[9], Sem(nc, st, "c1v2"), Sem(nc, st, "c1v3")]
        wb_ = [s.n for s in wsl]
        qs_ = [s.n for s in qsl]
        vs_ = [s.n for s in vsl]

        S = {}
        mmv, scv, evv, gpv = ctx.mm.n, ctx.sc.n, ctx.ev.n, ctx.gp.n
        for r in range(4):
            for cb in range(8):
                for ph in range(2):
                    gpv += 1
                S["gp_qk", r, cb] = gpv
            for p in range(8):
                gpv += 1
            S["gp_v", r] = gpv
        for u in range(NU):
            for cb in range(8):
                for tb in range(2):
                    mmv += 1; S["mm_qk", u, cb, tb] = mmv
            for tci in range(8):
                mmv += 1; S["mm_v", u, tci] = mmv
        for u in range(NU):
            for cb in range(8):
                for tb in range(2):
                    scv += 1; S["sc_e", u, cb, tb] = scv
        for u in range(NU):
            for tci in range(8):
                evv += 1; S["ev_e", u, tci] = evv
        QC = {}; VC = {}
        qcnt = [qs_[0], qs_[1]]
        vcnt = list(vs_)
        for u in range(NU):
            for cb in range(8):
                qcnt[cb % 2] += 16
                QC[u, cb] = qcnt[cb % 2]
            for tci in range(8):
                vcnt[tci % 4] += 64
                VC[u, tci] = vcnt[tci % 4]
        mm_end, sc_end, ev_end, gp_end = mmv, scv, evv, gpv

        with nc.Block("C1") as blk:
            @blk.sync
            def _(sy):
                split = isinstance(x2all, (list, tuple))

                def load(u):
                    r, hh = u // 2, u % 2
                    if cc_gate is not None:
                        if split and u < 2:
                            sy.wait_ge(ctx.cc.h, cc_gate[hh])
                        elif (not split) and u == 0:
                            sy.wait_ge(ctx.cc.h, cc_gate)
                    if u >= 3:
                        sy.wait_ge(ctx.mm.h, S["mm_v", u - 3, 7])
                    if split:
                        srcap = x2all[hh][r * D:(r + 1) * D, :].rearrange(
                            "(c p) t -> p c t", p=128)
                    else:
                        srcap = x2all[r * D:(r + 1) * D,
                                      hh * 1024:(hh + 1) * 1024].rearrange("(c p) t -> p c t", p=128)
                    sy.dma_start(out=xrh[u % 3][:], in_=srcap).then_inc(lsl[u % 3].h, 16)
                    lsl[u % 3].n += 16

                for u in range(3):
                    load(u)
                for u in range(NU):
                    r, hh = u // 2, u % 2
                    for cb in range(8):
                        sy.wait_ge(ctx.sc.h, S["sc_e", u, cb, 1])
                        dst = qT1 if cb < 4 else kT1
                        r0 = (cb % 4) * 128
                        sy.dma_start(
                            out=dst[r0:r0 + 128, r * TPC + hh * 1024:r * TPC + (hh + 1) * 1024],
                            in_=qkst[cb % 2][:]).then_inc(qsl[cb % 2].h, 16)
                        qsl[cb % 2].n += 16
                    for tci in range(8):
                        sy.wait_ge(ctx.ev.h, S["ev_e", u, tci])
                        for j in range(4):
                            sy.dma_start(out=v1[j, :, r * 16 + hh * 8 + tci, :],
                                         in_=vst[tci % 4][:, j * 128:(j + 1) * 128]).then_inc(vsl[tci % 4].h, 16)
                        vsl[tci % 4].n += 64
                    if u + 3 < NU:
                        load(u + 3)
                for s_ in qsl + vsl:
                    sy.wait_ge(s_.h, s_.n)

            @blk.gpsimd
            def _(g):
                pieces = []
                for r in range(4):
                    for cb in range(8):
                        for ph in range(2):
                            pieces.append(("qk", r, cb, ph))
                    for p in range(8):
                        pieces.append(("v", r, p, 0))

                def pload(i):
                    kind = pieces[i]
                    if kind[0] == "qk":
                        _, r_, cb_, ph_ = kind
                        w = wqT if cb_ < 4 else wkT
                        g.dma_start(out=wstg[i % 4][:],
                                    in_=w[cb_ % 4][:, ph_ * 8:(ph_ + 1) * 8, :]).then_inc(wsl[i % 4].h, 16)
                    else:
                        _, r_, p_, _ = kind
                        g.dma_start(out=wstgv[p_ % 4][:],
                                    in_=wvT[0][:, p_ * 2:(p_ + 1) * 2, :]).then_inc(wsl[i % 4].h, 16)
                    wsl[i % 4].n += 16

                for i in range(4):
                    pload(i)
                gpc = ctx.gp.n
                for i, kind in enumerate(pieces):
                    g.wait_ge(wsl[i % 4].h, wb_[i % 4] + 16 * (i // 4 + 1))
                    if kind[0] == "qk":
                        _, r, cb, ph = kind
                        if r >= 1:
                            g.wait_ge(ctx.mm.h, S["mm_qk", 2 * (r - 1) + 1, cb, 1])
                        g.tensor_copy(wbufr[:, cb, ph * 8:(ph + 1) * 8, :],
                                      wstg[i % 4][:]).then_inc(ctx.gp.h, 1)
                    else:
                        _, r, p, _ = kind
                        if r >= 1:
                            g.wait_ge(ctx.mm.h, S["mm_v", 2 * (r - 1) + 1, 7])
                        g.tensor_copy(wvb[:, p * 2:(p + 1) * 2, :],
                                      wstgv[p % 4][:]).then_inc(ctx.gp.h, 1)
                    gpc += 1
                    if i + 4 < len(pieces):
                        g.wait_ge(ctx.gp.h, gpc)
                        pload(i + 4)
                ctx.gp.n = gp_end

            @blk.tensor
            def _(t):
                for u in range(NU):
                    r, hh = u // 2, u % 2
                    t.wait_ge(lsl[u % 3].h, 16 * (u // 3 + 1))
                    for cb in range(8):
                        if hh == 0:
                            t.wait_ge(ctx.gp.h, S["gp_qk", r, cb])
                        bank = (cb % 2) * 2
                        for tb in range(2):
                            if u >= 1 or cb >= 2:
                                pu, pcb = (u, cb - 2) if cb >= 2 else (u - 1, cb + 6)
                                t.wait_ge(ctx.sc.h, S["sc_e", pu, pcb, tb])
                            for dc in range(DC):
                                ins = t.matmul(
                                    ps[bank + tb][:], wbufr[:, cb, dc, :],
                                    xrh[u % 3][:, dc, tb * 512:(tb + 1) * 512],
                                    start=(dc == 0), stop=(dc == DC - 1),
                                )
                            ins.then_inc(ctx.mm.h, 1)
                    if hh == 0:
                        t.wait_ge(ctx.gp.h, S["gp_v", r])
                    for tci in range(8):
                        bank = 4 + tci % 4
                        if u >= 1 or tci >= 4:
                            pu, ptc = (u, tci - 4) if tci >= 4 else (u - 1, tci + 4)
                            t.wait_ge(ctx.ev.h, S["ev_e", pu, ptc])
                        for dc in range(DC):
                            ins = t.matmul(ps[bank][:],
                                           xrh[u % 3][:, dc, tci * 128:(tci + 1) * 128],
                                           wvb[:, dc, :], start=(dc == 0), stop=(dc == DC - 1))
                        ins.then_inc(ctx.mm.h, 1)
                ctx.mm.n = mm_end

            @blk.scalar
            def _(s):
                for u in range(NU):
                    for cb in range(8):
                        for tb in range(2):
                            s.wait_ge(ctx.mm.h, S["mm_qk", u, cb, tb])
                            if u >= 1 or cb >= 2:
                                pu, pcb = (u, cb - 2) if cb >= 2 else (u - 1, cb + 6)
                                s.wait_ge(qsl[cb % 2].h, QC[pu, pcb])
                            s.activation(qkst[cb % 2][:, tb * 512:(tb + 1) * 512],
                                         ps[(cb % 2) * 2 + tb][:], AF.Identity).then_inc(ctx.sc.h, 1)
                ctx.sc.n = sc_end

            @blk.vector
            def _(v):
                for u in range(NU):
                    for tci in range(8):
                        v.wait_ge(ctx.mm.h, S["mm_v", u, tci])
                        if u >= 1 or tci >= 4:
                            pu, ptc = (u, tci - 4) if tci >= 4 else (u - 1, tci + 4)
                            v.wait_ge(vsl[tci % 4].h, VC[pu, ptc])
                        v.tensor_copy(vst[tci % 4][:], ps[4 + tci % 4][:]).then_inc(ctx.ev.h, 1)
                ctx.ev.n = ev_end
    _bar(nc)


def _emit_l1_attn(nc, ctx, qT1, kT1, v1, oT1, onesrow, onescol, diagm):
    NQG = N // 512
    lsl = [ctx.slot[10], ctx.slot[11]]
    osl = [ctx.slot[6], ctx.slot[7]]
    lbase = [s.n for s in lsl]
    obase = [s.n for s in osl]
    with ExitStack() as st:
        qh = [st.enter_context(nc.sbuf_tensor(f"c3q{i}", [128, N], BF16)) for i in range(2)]
        kh = [st.enter_context(nc.sbuf_tensor(f"c3k{i}", [128, N], BF16)) for i in range(2)]
        vh = [st.enter_context(nc.sbuf_tensor(f"c3v{i}", [128, N // 128, 128], BF16)) for i in range(2)]
        sq2 = [st.enter_context(nc.sbuf_tensor(f"c3sq{i}", [128, 512], BF16)) for i in range(2)]
        nrow = st.enter_context(nc.sbuf_tensor("c3nr", [1, 6, 512], F32))
        mcol = st.enter_context(nc.sbuf_tensor("c3mc", [128, 1], F32))
        pt = [st.enter_context(nc.sbuf_tensor(f"c3pt{i}", [128, 512], BF16)) for i in range(4)]
        ptq = [st.enter_context(nc.sbuf_tensor(f"c3pp{i}", [128, 512], BF16)) for i in range(2)]
        pts2 = st.enter_context(nc.sbuf_tensor("c3p2", [128, 512], BF16))
        rrow = [st.enter_context(nc.sbuf_tensor(f"c3rr{i}", [1, 512], F32)) for i in range(2)]
        rbc = [st.enter_context(nc.sbuf_tensor(f"c3rb{i}", [128, 512], F32)) for i in range(2)]
        ohst = [st.enter_context(nc.sbuf_tensor(f"c3o{i}", [128, 512], BF16)) for i in range(2)]
        pssc = [st.enter_context(nc.psum_tensor(f"c3ps{i}", [128, 512], F32)) for i in range(3)]
        psO = [st.enter_context(nc.psum_tensor(f"c3pO{i}", [128, 512], F32)) for i in range(2)]
        pssum = [st.enter_context(nc.psum_tensor(f"c3pS{i}", [1, 512], F32)) for i in range(2)]
        psB = st.enter_context(nc.psum_tensor("c3pB", [128, 512], F32))

        bsteps = [(side, tbb) for side in range(2) for tbb in range(16)]
        seq = []
        for qg in range(NQG):
            kcs = list(range(4 * qg, 4 * qg + 4)) + list(range(4 * qg))
            for pos, kc in enumerate(kcs):
                seq.append((qg, kc, pos))
        nst = len(seq)
        L = {qg: 4 * qg + 4 for qg in range(NQG)}
        lastq = {qg: (qg + 1) - 1 for qg in range(NQG)}

        S = {}

        def plan(emit):
            c = {"mm": ctx.mm.n, "sc": ctx.sc.n, "ev": ctx.ev.n, "gp": ctx.gp.n}

            def bump(kind, key):
                c[kind] += 1
                S[key] = c[kind]

            for h in range(4):
                for side, tbb in bsteps:
                    bump("sc", ("sc_sq", h, side, tbb))
                    bump("mm", ("mm_bs", h, side, tbb))
                    bump("ev", ("ev_rm", h, side, tbb))
                    bump("ev", ("ev_mx", h, side, tbb))
                bump("ev", ("ev_mul", h))
                bump("sc", ("sc_sqrt", h))
                bump("ev", ("ev_neg", h))
                bump("mm", ("mm_mcol", h))
                bump("ev", ("ev_mc", h))
                for i, (qg, kc, pos) in enumerate(seq):
                    bump("mm", ("mm_sc", h, qg, kc))
                    bump("sc", ("sc_exp", h, qg, kc))
                    if pos < 4:
                        bump("ev", ("ev_mk", h, qg, kc))
                    if pos % 2 == 1:
                        if pos % 4 == 1:
                            bump("ev", ("ev_pa", h, qg, pos))
                        else:
                            bump("ev", ("ev_pa", h, qg, pos))
                            bump("ev", ("ev_q", h, qg, pos // 4))
                        kcm1 = seq[i - 1][1]
                        S["ev_rd", h, qg, kc] = c["ev"]
                        S["ev_rd", h, qg, kcm1] = c["ev"]
                    if i >= 2:
                        bump("mm", ("mm_av", h, seq[i - 2][0], seq[i - 2][1]))
                    if i >= 4 and seq[i - 4][2] % 4 == 3:
                        bump("mm", ("mm_sum", h, seq[i - 4][0], seq[i - 4][2] // 4))
                    if i >= 6 and seq[i - 6][2] == L[seq[i - 6][0]] - 1:
                        bump("ev", ("ev_rc1", h, seq[i - 6][0]))
                        bump("ev", ("ev_rec", h, seq[i - 6][0]))
                    if i >= 10 and seq[i - 10][2] == L[seq[i - 10][0]] - 1:
                        bump("mm", ("mm_bc1", h, seq[i - 10][0]))
                        bump("mm", ("mm_bc", h, seq[i - 10][0]))
                    for dep in (11, 12):
                        if i >= dep and seq[i - dep][2] == L[seq[i - dep][0]] - 1 \
                                and (seq[i - dep][0] == 0) == (dep == 11):
                            qq = seq[i - dep][0]
                            bump("ev", ("ev_rbc", h, qq))
                            bump("ev", ("ev_o", h, qq))
                for i in range(nst, nst + 12):
                    if i - 2 < nst:
                        bump("mm", ("mm_av", h, seq[i - 2][0], seq[i - 2][1]))
                    if i - 4 < nst and seq[i - 4][2] % 4 == 3:
                        bump("mm", ("mm_sum", h, seq[i - 4][0], seq[i - 4][2] // 4))
                    if i - 6 < nst and seq[i - 6][2] == L[seq[i - 6][0]] - 1:
                        bump("ev", ("ev_rc1", h, seq[i - 6][0]))
                        bump("ev", ("ev_rec", h, seq[i - 6][0]))
                    if i - 10 < nst and i >= 10 and seq[i - 10][2] == L[seq[i - 10][0]] - 1:
                        bump("mm", ("mm_bc1", h, seq[i - 10][0]))
                        bump("mm", ("mm_bc", h, seq[i - 10][0]))
                    if i - 12 < nst and i >= 12 and seq[i - 12][2] == L[seq[i - 12][0]] - 1:
                        qq = seq[i - 12][0]
                        bump("ev", ("ev_rbc", h, qq))
                        bump("ev", ("ev_o", h, qq))
            return c

        ends = plan(False)
        mm_end, sc_end, ev_end, gp_end = ends["mm"], ends["sc"], ends["ev"], ends["gp"]
        gsteps = [(h,) + s for h in range(4) for s in seq]
        gindex = {(h, qg, kc): gi for gi, (h, qg, kc, pos) in enumerate(gsteps)}
        gquads = [(h, qg, qi) for h in range(4) for qg in range(NQG) for qi in range(qg + 1)]
        gqid = {k: i for i, k in enumerate(gquads)}
        last_step = seq[-1]

        with nc.Block("C3") as blk:
            @blk.sync
            def _(sy):
                for h in range(4):
                    if h >= 2:
                        sy.wait_ge(ctx.ev.h, S["ev_o", h - 2, NQG - 1])
                        sy.wait_ge(ctx.mm.h, S["mm_av", h - 2, last_step[0], last_step[1]])
                    sy.dma_start(out=qh[h % 2][:], in_=qT1[h * 128:(h + 1) * 128, :]).then_inc(lsl[h % 2].h, 16)
                    sy.dma_start(out=kh[h % 2][:], in_=kT1[h * 128:(h + 1) * 128, :]).then_inc(lsl[h % 2].h, 16)
                    sy.dma_start(out=vh[h % 2][:], in_=v1[h]).then_inc(lsl[h % 2].h, 16)
                    lsl[h % 2].n += 48
                    for qg in range(NQG):
                        j = h * NQG + qg
                        sy.wait_ge(ctx.ev.h, S["ev_o", h, qg])
                        rd, tq = qg // 4, qg % 4
                        sy.dma_start(
                            out=oT1[rd * 512 + h * 128:rd * 512 + (h + 1) * 128,
                                    tq * 512:(tq + 1) * 512],
                            in_=ohst[qg % 2][:]).then_inc(osl[j % 2].h, 16)
                        osl[j % 2].n += 16
                for s_ in osl:
                    sy.wait_ge(s_.h, s_.n)

            @blk.scalar
            def _(s):
                for h in range(4):
                    s.wait_ge(lsl[h % 2].h, lbase[h % 2] + 48 * (h // 2 + 1))
                    for bi, (side, tbb) in enumerate(bsteps):
                        if bi >= 2 or h >= 1:
                            pb = bsteps[bi - 2] if bi >= 2 else bsteps[bi - 2 + 32]
                            ph = h if bi >= 2 else h - 1
                            s.wait_ge(ctx.mm.h, S["mm_bs", ph, pb[0], pb[1]])
                        src = qh[h % 2] if side == 0 else kh[h % 2]
                        s.activation(sq2[bi % 2][:], src[:, tbb * 512:(tbb + 1) * 512],
                                     AF.Square).then_inc(ctx.sc.h, 1)
                    s.wait_ge(ctx.ev.h, S["ev_mul", h])
                    s.activation(nrow[:, 3, 0:1], nrow[:, 3, 0:1], AF.Sqrt).then_inc(ctx.sc.h, 1)
                    for i, (qg, kc, pos) in enumerate(seq):
                        s.wait_ge(ctx.mm.h, S["mm_sc", h, qg, kc])
                        gi = h * nst + i
                        if gi >= 4:
                            pk = gsteps[gi - 4]
                            s.wait_ge(ctx.mm.h, S["mm_av", pk[0], pk[1], pk[2]])
                            s.wait_ge(ctx.ev.h, S.get(("ev_rd", pk[0], pk[1], pk[2]), 0))
                        if i == 0:
                            s.wait_ge(ctx.ev.h, S["ev_mc", h])
                        s.activation(pt[kc % 4][:], pssc[i % 3][:], AF.Exp,
                                     bias=mcol[:], scale=float(INV_SQRT_DH)).then_inc(ctx.sc.h, 1)
                ctx.sc.n = sc_end

            @blk.tensor
            def _(t):
                def do_sum(h, qg2, pos2):
                    qi = pos2 // 4
                    t.wait_ge(ctx.ev.h, S["ev_q", h, qg2, qi])
                    if qi == 0:
                        if qg2 >= 2:
                            t.wait_ge(ctx.ev.h, S["ev_rec", h, qg2 - 2])
                        elif h >= 1 and qg2 == 0:
                            t.wait_ge(ctx.ev.h, S["ev_rec", h - 1, NQG - 2])
                        elif h >= 1 and qg2 == 1:
                            t.wait_ge(ctx.ev.h, S["ev_rec", h - 1, NQG - 1])
                    t.matmul(pssum[qg2 % 2][:], onescol[:], ptq[gqid[h, qg2, qi] % 2][:],
                             start=(qi == 0), stop=(qi == lastq[qg2])).then_inc(ctx.mm.h, 1)

                def sums_av(h, qg2, kc2, pos2):
                    if ("ev_mk", h, qg2, kc2) in S:
                        t.wait_ge(ctx.ev.h, S["ev_mk", h, qg2, kc2])
                    else:
                        t.wait_ge(ctx.sc.h, S["sc_exp", h, qg2, kc2])
                    if pos2 == 0:
                        if qg2 >= 2:
                            t.wait_ge(ctx.ev.h, S["ev_o", h, qg2 - 2])
                        elif h >= 1 and qg2 == 0:
                            t.wait_ge(ctx.ev.h, S["ev_o", h - 1, NQG - 2])
                        elif h >= 1 and qg2 == 1:
                            t.wait_ge(ctx.ev.h, S["ev_o", h - 1, NQG - 1])
                    t.matmul(psO[qg2 % 2][:], vh[h % 2][:, kc2, :], pt[kc2 % 4][:],
                             start=(pos2 == 0), stop=(pos2 == L[qg2] - 1)).then_inc(ctx.mm.h, 1)

                def bc(h, qq):
                    t.wait_ge(ctx.ev.h, S["ev_rc1", h, qq])
                    if qq >= 2:
                        t.wait_ge(ctx.ev.h, S["ev_rbc", h, qq - 2])
                    elif h >= 1 and qq == 0:
                        t.wait_ge(ctx.ev.h, S["ev_rbc", h - 1, NQG - 2])
                    elif h >= 1 and qq == 1:
                        t.wait_ge(ctx.ev.h, S["ev_rbc", h - 1, NQG - 1])
                    t.matmul(psB[:, 0:256], onesrow[:], rrow[qq % 2][:, 0:256],
                             start=True, stop=True).then_inc(ctx.mm.h, 1)
                    t.wait_ge(ctx.ev.h, S["ev_rec", h, qq])
                    t.matmul(psB[:, 256:512], onesrow[:], rrow[qq % 2][:, 256:512],
                             start=True, stop=True).then_inc(ctx.mm.h, 1)

                for h in range(4):
                    t.wait_ge(lsl[h % 2].h, lbase[h % 2] + 48 * (h // 2 + 1))
                    if h >= 1:
                        t.wait_ge(ctx.sc.h, S["sc_exp", h - 1, last_step[0], last_step[1]])
                    for bi, (side, tbb) in enumerate(bsteps):
                        t.wait_ge(ctx.sc.h, S["sc_sq", h, side, tbb])
                        if bi >= 3:
                            pb = bsteps[bi - 3]
                            t.wait_ge(ctx.ev.h, S["ev_rm", h, pb[0], pb[1]])
                        t.matmul(pssc[bi % 3][0:1, :], onescol[:], sq2[bi % 2][:],
                                 start=True, stop=True).then_inc(ctx.mm.h, 1)
                    t.wait_ge(ctx.ev.h, S["ev_neg", h])
                    t.matmul(psB[:, 0:1], onesrow[:], nrow[:, 4, 0:1], start=True,
                             stop=True).then_inc(ctx.mm.h, 1)
                    for i, (qg, kc, pos) in enumerate(seq):
                        gi = h * nst + i
                        if gi >= 3:
                            pk = gsteps[gi - 3]
                            t.wait_ge(ctx.sc.h, S["sc_exp", pk[0], pk[1], pk[2]])
                        if i < 3:
                            bb = bsteps[(30, 31, 29)[i]]
                            t.wait_ge(ctx.ev.h, S["ev_rm", h, bb[0], bb[1]])
                        t.matmul(pssc[i % 3][:], kh[h % 2][:, kc * 128:(kc + 1) * 128],
                                 qh[h % 2][:, qg * 512:(qg + 1) * 512], start=True,
                                 stop=True).then_inc(ctx.mm.h, 1)
                        if i >= 2:
                            qg2, kc2, pos2 = seq[i - 2]
                            sums_av(h, qg2, kc2, pos2)
                        if i >= 4 and seq[i - 4][2] % 4 == 3:
                            do_sum(h, seq[i - 4][0], seq[i - 4][2])
                        if i >= 10 and seq[i - 10][2] == L[seq[i - 10][0]] - 1:
                            bc(h, seq[i - 10][0])
                    for i in range(nst, nst + 12):
                        if i - 2 < nst:
                            qg2, kc2, pos2 = seq[i - 2]
                            sums_av(h, qg2, kc2, pos2)
                        if i - 4 < nst and seq[i - 4][2] % 4 == 3:
                            do_sum(h, seq[i - 4][0], seq[i - 4][2])
                        if i - 10 < nst and i >= 10 and seq[i - 10][2] == L[seq[i - 10][0]] - 1:
                            bc(h, seq[i - 10][0])
                ctx.mm.n = mm_end

            @blk.vector
            def _(v):
                myev = ctx.ev.n
                def vop(ins):
                    nonlocal myev
                    ins.then_inc(ctx.ev.h, 1)
                    myev += 1
                for h in range(4):
                    for bi, (side, tbb) in enumerate(bsteps):
                        acc = nrow[:, 1 + side, 0:1]
                        v.wait_ge(ctx.mm.h, S["mm_bs", h, side, tbb])
                        v.wait_ge(ctx.ev.h, myev)
                        vop(v.reduce_max(nrow[:, 5, 0:1], pssc[bi % 3][0:1, :], axis=AX))
                        v.wait_ge(ctx.ev.h, myev)
                        if tbb == 0:
                            vop(v.tensor_copy(acc, nrow[:, 5, 0:1]))
                        else:
                            vop(v.tensor_max(acc, acc, nrow[:, 5, 0:1]))
                    v.wait_ge(ctx.ev.h, myev)
                    vop(v.tensor_mul(nrow[:, 3, 0:1], nrow[:, 1, 0:1], nrow[:, 2, 0:1]))
                    v.wait_ge(ctx.sc.h, S["sc_sqrt", h])
                    vop(v.tensor_scalar_mul(nrow[:, 4, 0:1], nrow[:, 3, 0:1],
                                            -float(INV_SQRT_DH)))
                    v.wait_ge(ctx.mm.h, S["mm_mcol", h])
                    vop(v.tensor_copy(mcol[:], psB[:, 0:1]))
                    for i, (qg, kc, pos) in enumerate(seq):
                        if pos < 4:
                            v.wait_ge(ctx.sc.h, S["sc_exp", h, qg, kc])
                            v.wait_ge(ctx.ev.h, myev)
                            vop(v.tensor_tensor(pt[kc % 4][:], pt[kc % 4][:],
                                                diagm[:, kc - 4 * qg, :], ALU.mult))
                        if pos % 2 == 1:
                            kcm1 = seq[i - 1][1]
                            if pos < 4:
                                v.wait_ge(ctx.ev.h, myev)
                            else:
                                v.wait_ge(ctx.sc.h, S["sc_exp", h, qg, kc])
                            qi = pos // 4
                            gq = gqid[h, qg, qi]
                            if pos % 4 == 1:
                                if gq >= 2:
                                    pq = gquads[gq - 2]
                                    v.wait_ge(ctx.mm.h, S["mm_sum", pq[0], pq[1], pq[2]])
                                v.wait_ge(ctx.ev.h, myev)
                                vop(v.tensor_add(ptq[gq % 2][:], pt[kcm1 % 4][:], pt[kc % 4][:]))
                            else:
                                v.wait_ge(ctx.ev.h, myev)
                                vop(v.tensor_add(pts2[:], pt[kcm1 % 4][:], pt[kc % 4][:]))
                                v.wait_ge(ctx.ev.h, myev)
                                vop(v.tensor_add(ptq[gq % 2][:], ptq[gq % 2][:], pts2[:]))
                        if i >= 6 and seq[i - 6][2] == L[seq[i - 6][0]] - 1:
                            qq = seq[i - 6][0]
                            v.wait_ge(ctx.mm.h, S["mm_sum", h, qq, lastq[qq]])
                            if qq >= 2:
                                v.wait_ge(ctx.mm.h, S["mm_bc", h, qq - 2])
                            elif h >= 1 and qq == 0:
                                v.wait_ge(ctx.mm.h, S["mm_bc", h - 1, NQG - 2])
                            elif h >= 1 and qq == 1:
                                v.wait_ge(ctx.mm.h, S["mm_bc", h - 1, NQG - 1])
                            v.wait_ge(ctx.ev.h, myev)
                            vop(v.reciprocal(rrow[qq % 2][:, 0:256], pssum[qq % 2][:, 0:256]))
                            v.wait_ge(ctx.ev.h, myev)
                            vop(v.reciprocal(rrow[qq % 2][:, 256:512], pssum[qq % 2][:, 256:512]))
                        for dep in (11, 12):
                            if i >= dep and seq[i - dep][2] == L[seq[i - dep][0]] - 1 \
                                    and (seq[i - dep][0] == 0) == (dep == 11):
                                qq = seq[i - dep][0]
                                j = h * NQG + qq
                                v.wait_ge(ctx.mm.h, S["mm_bc", h, qq])
                                v.wait_ge(ctx.ev.h, myev)
                                vop(v.tensor_copy(rbc[qq % 2][:], psB[:]))
                                if j >= 2:
                                    v.wait_ge(osl[j % 2].h, obase[j % 2] + 16 * (j // 2))
                                v.wait_ge(ctx.ev.h, myev)
                                vop(v.tensor_mul(ohst[qq % 2][:], psO[qq % 2][:], rbc[qq % 2][:]))
                    for i in range(nst, nst + 12):
                        if i - 6 < nst and seq[i - 6][2] == L[seq[i - 6][0]] - 1:
                            qq = seq[i - 6][0]
                            v.wait_ge(ctx.mm.h, S["mm_sum", h, qq, lastq[qq]])
                            v.wait_ge(ctx.mm.h, S["mm_bc", h, qq - 2])
                            v.wait_ge(ctx.ev.h, myev)
                            vop(v.reciprocal(rrow[qq % 2][:, 0:256], pssum[qq % 2][:, 0:256]))
                            v.wait_ge(ctx.ev.h, myev)
                            vop(v.reciprocal(rrow[qq % 2][:, 256:512], pssum[qq % 2][:, 256:512]))
                        if i - 12 < nst and i >= 12 and seq[i - 12][2] == L[seq[i - 12][0]] - 1:
                            qq = seq[i - 12][0]
                            j = h * NQG + qq
                            v.wait_ge(ctx.mm.h, S["mm_bc", h, qq])
                            v.wait_ge(ctx.ev.h, myev)
                            vop(v.tensor_copy(rbc[qq % 2][:], psB[:]))
                            if j >= 2:
                                v.wait_ge(osl[j % 2].h, obase[j % 2] + 16 * (j // 2))
                            v.wait_ge(ctx.ev.h, myev)
                            vop(v.tensor_mul(ohst[qq % 2][:], psO[qq % 2][:], rbc[qq % 2][:]))
                ctx.ev.n = ev_end
                assert myev == ev_end, (myev, ev_end)



def build_launch_D(dbg=False):
    nc = bass.Bass("TRN2", target_bir_lowering=False, debug=False)
    oTm = nc.dram_tensor("oTm", [D, TPC], BF16, kind="ExternalInput").ap()
    x2f = nc.dram_tensor("x2f", [D, TPC], F32, kind="ExternalInput").ap()
    woT = nc.dram_tensor("woTt", [16, 128, 16, 128], F32, kind="ExternalInput").ap()
    acat = nc.dram_tensor("acatt", [128, 16, 512], F32, kind="ExternalInput").ap()
    wgT = nc.dram_tensor("wgTt", [128, 16, 8], F32, kind="ExternalInput").ap()
    bwT = nc.dram_tensor("bwTt", [128, 4, 2048], F32, kind="ExternalInput").ap()

    out_c = nc.dram_tensor("out_c", [TPC, D], F32, kind="ExternalOutput").ap()
    ik = "ExternalOutput" if dbg else "Internal"
    x3f = nc.dram_tensor("x3f", [D, TPC], F32, kind=ik).ap()
    x3b = nc.dram_tensor("x3b", [D, TPC], BF16, kind=ik).ap()
    ybuf = nc.dram_tensor("ybuf", [D, TPC], F32, kind=ik).ap()
    rbuf = nc.dram_tensor("rbuf", [D, TPC], F32, kind="Internal").ap()

    onesrow_t = nc.inline_tensor(ONES_ROW_F32, "c_onesrow").ap()
    onescol_t = nc.inline_tensor(ONES_COL_B16, "c_onescol").ap()
    identf_t = nc.inline_tensor(IDENT_F32, "c_identf").ap()
    identb_t = nc.inline_tensor(IDENT_B16, "c_identb").ap()

    with ExitStack() as stack:
        ctx = Ctx(nc, stack)
        onesrow = ctx.sb("onesrow", [1, 128], F32)
        onescol = ctx.sb("onescol", [128, 1], BF16)
        identf = ctx.sb("identf", [128, 128], F32)
        identb = ctx.sb("identb", [128, 128], BF16)

        with nc.Block("D0") as blk:
            @blk.gpsimd
            def _(g):
                for dst, src in [(onesrow[:], onesrow_t), (onescol[:], onescol_t),
                                 (identf[:], identf_t), (identb[:], identb_t)]:
                    g.dma_start(out=dst, in_=src).then_inc(ctx.ld_g.h, 16)
                    ctx.ld_g.n += 16
                g.wait_ge(ctx.ld_g.h, ctx.ld_g.n)
        _bar(nc)

        _emit_proj_ln2(nc, ctx, "D1", rhs_hbm=oTm, K_=D, w_hbm=woT,
                       bias_col=None, xold_hbm=x2f,
                       xnew_f=x3f, xnew_b=x3b, onescol=onescol, onesrow=onesrow)
        _emit_moe(nc, ctx, x3b, acat, wgT, bwT, ybuf, identb)
        _bar(nc)
        _emit_final(nc, ctx, ybuf, x3f, out_c, onescol, onesrow, identf)
        _bar(nc)
    return nc


def _emit_moe(nc, ctx, x3b, acat, wgT, bwT, ybuf, identb):
    NTC = TPC // 128
    YG = {}
    with ExitStack() as st:
        xb3 = st.enter_context(nc.sbuf_tensor("moex", [128, DC, TPC], BF16))
        acat_sb = st.enter_context(nc.sbuf_tensor("moea", [128, DC, 512], BF16))
        wg_sb = st.enter_context(nc.sbuf_tensor("moew", [128, DC, 8], BF16))
        bw_sb = st.enter_context(nc.sbuf_tensor("moeb", [128, 4, D], BF16))
        hmT = st.enter_context(nc.sbuf_tensor("moeh", [128, 4, TPC], BF16))
        hm = [st.enter_context(nc.sbuf_tensor(f"moem{i}", [128, 512], BF16)) for i in range(2)]
        pex = [st.enter_context(nc.sbuf_tensor(f"moep{i}", [128, 8], F32)) for i in range(2)]
        gx = st.enter_context(nc.sbuf_tensor("moegx", [128, 6], F32))
        msk = st.enter_context(nc.sbuf_tensor("moemk", [128, 8], F32))
        ystage = [st.enter_context(nc.sbuf_tensor(f"moey{i}", [128, 512], F32)) for i in range(2)]
        stA = ExitStack()
        psh = [stA.enter_context(nc.psum_tensor(f"moeph{i}", [128, 512], F32)) for i in range(2)]
        psl = [stA.enter_context(nc.psum_tensor(f"moepl{i}", [128, 8], F32)) for i in range(2)]
        pst = [stA.enter_context(nc.psum_tensor(f"moept{i}", [128, 128], BF16)) for i in range(2)]

        S = {}
        mmv, scv, evv = ctx.mm.n, ctx.sc.n, ctx.ev.n
        for tc in range(NTC):
            mmv += 1; S["mm_hl", tc] = mmv
            evv += 2; S["ev_gneg", tc] = evv
            scv += 1; S["sc_exp", tc] = scv
            evv += 5; S["ev_mskg", tc] = evv
            evv += 8; S["ev_app", tc] = evv
            for i in range(4):
                mmv += 1; S["mm_tr", tc, i] = mmv
                evv += 1; S["ev_tr", tc, i] = evv
        mm_end, sc_end, ev_end = mmv, scv, evv

        g0 = ctx.ld_g.n
        with nc.Block("MOE_A") as blk:
            @blk.gpsimd
            def _(g):
                g.dma_start(out=xb3[:], in_=x3b.rearrange("(c p) t -> p c t", p=128)).then_inc(ctx.ld_g.h, 16)
                g.dma_start(out=acat_sb[:], in_=acat).then_inc(ctx.ld_g.h, 16)
                g.dma_start(out=wg_sb[:], in_=wgT).then_inc(ctx.ld_g.h, 16)
                g.dma_start(out=bw_sb[:], in_=bwT).then_inc(ctx.ld_g.h, 16)
                ctx.ld_g.n += 64
                g.wait_ge(ctx.ld_g.h, ctx.ld_g.n)

            @blk.tensor
            def _(t):
                t.wait_ge(ctx.ld_g.h, g0 + 64)
                for tc in range(NTC):
                    if tc >= 2:
                        t.wait_ge(ctx.ev.h, S["ev_app", tc - 2])
                    for dc in range(DC):
                        t.matmul(psh[tc % 2][:], xb3[:, dc, tc * 128:(tc + 1) * 128],
                                 acat_sb[:, dc, :], start=(dc == 0), stop=(dc == DC - 1))
                        ins = t.matmul(psl[tc % 2][:], xb3[:, dc, tc * 128:(tc + 1) * 128],
                                       wg_sb[:, dc, :], start=(dc == 0), stop=(dc == DC - 1))
                    ins.then_inc(ctx.mm.h, 1)
                    for i in range(4):
                        t.wait_ge(ctx.ev.h, S["ev_app", tc])
                        if tc >= 1 or i >= 2:
                            pi = (tc, i - 2) if i >= 2 else (tc - 1, i + 2)
                            t.wait_ge(ctx.ev.h, S["ev_tr", pi[0], pi[1]])
                        t.transpose(pst[i % 2][:], hm[tc % 2][:, i * 128:(i + 1) * 128],
                                    identb[:]).then_inc(ctx.mm.h, 1)
                ctx.mm.n = mm_end

            @blk.scalar
            def _(s):
                for tc in range(NTC):
                    s.wait_ge(ctx.ev.h, S["ev_gneg", tc])
                    if tc >= 2:
                        s.wait_ge(ctx.ev.h, S["ev_mskg", tc - 2])
                    s.activation(pex[tc % 2][:], psl[tc % 2][:], AF.Exp,
                                 bias=gx[:, 1:2], scale=1.0).then_inc(ctx.sc.h, 1)
                ctx.sc.n = sc_end

            @blk.vector
            def _(v):
                myev = ctx.ev.n
                def vop(ins):
                    nonlocal myev
                    ins.then_inc(ctx.ev.h, 1)
                    myev += 1
                for tc in range(NTC):
                    v.wait_ge(ctx.mm.h, S["mm_hl", tc])
                    if tc >= 1:
                        v.wait_ge(ctx.sc.h, S["sc_exp", tc - 1])
                    v.wait_ge(ctx.ev.h, myev)
                    vop(v.reduce_max(gx[:, 0:1], psl[tc % 2][:], axis=AX))
                    v.wait_ge(ctx.ev.h, myev)
                    vop(v.tensor_scalar_mul(gx[:, 1:2], gx[:, 0:1], -1.0))
                    v.wait_ge(ctx.sc.h, S["sc_exp", tc])
                    vop(v.reduce_sum(gx[:, 2:3], pex[tc % 2][:], axis=AX))
                    v.wait_ge(ctx.ev.h, myev)
                    vop(v.reciprocal(gx[:, 3:4], gx[:, 2:3]))
                    v.wait_ge(ctx.ev.h, myev)
                    vop(v.tensor_scalar_mul(gx[:, 4:5], gx[:, 3:4], 4.0))
                    vop(v.tensor_single_scalar(msk[:], psl[tc % 2][:], gx[:, 0:1], ALU.is_ge))
                    v.wait_ge(ctx.ev.h, myev)
                    vop(v.tensor_scalar_mul(msk[:], msk[:], gx[:, 4:5]))
                    v.wait_ge(ctx.ev.h, myev)
                    if tc >= 2:
                        v.wait_ge(ctx.mm.h, S["mm_tr", tc - 2, 3])
                    for e in range(8):
                        vop(v.tensor_scalar_mul(hm[tc % 2][:, e * 64:(e + 1) * 64],
                                                psh[tc % 2][:, e * 64:(e + 1) * 64],
                                                msk[:, e:e + 1]))
                    for i in range(4):
                        v.wait_ge(ctx.mm.h, S["mm_tr", tc, i])
                        vop(v.tensor_copy(hmT[:, i, tc * 128:(tc + 1) * 128], pst[i % 2][:]))
                ctx.ev.n = ev_end
                assert myev == ev_end
        _bar(nc)
        stA.close()

        psy = [st.enter_context(nc.psum_tensor(f"moepy{i}", [128, 512], F32)) for i in range(4)]
        ssl = [ctx.slot[2], ctx.slot[3]]
        sbase = [s.n for s in ssl]
        mm0, sc0 = ctx.mm.n, ctx.sc.n
        with nc.Block("MOE_B") as blk:
            @blk.tensor
            def _(t):
                for ob in range(16):
                    for tb in range(4):
                        j = ob * 4 + tb
                        if j >= 4:
                            t.wait_ge(ctx.sc.h, sc0 + j - 3)
                        for rc in range(4):
                            ins = t.matmul(psy[j % 4][:], bw_sb[:, rc, ob * 128:(ob + 1) * 128],
                                           hmT[:, rc, tb * 512:(tb + 1) * 512],
                                           start=(rc == 0), stop=(rc == 3))
                        ins.then_inc(ctx.mm.h, 1)
                ctx.mm.n += 64

            @blk.scalar
            def _(s):
                for j in range(64):
                    s.wait_ge(ctx.mm.h, mm0 + j + 1)
                    if j >= 2:
                        s.wait_ge(ssl[j % 2].h, sbase[j % 2] + 16 * (j // 2))
                    s.activation(ystage[j % 2][:], psy[j % 4][:], AF.Identity).then_inc(ctx.sc.h, 1)
                ctx.sc.n += 64

            @blk.gpsimd
            def _(g):
                for ob in range(16):
                    for tb in range(4):
                        j = ob * 4 + tb
                        g.wait_ge(ctx.sc.h, sc0 + j + 1)
                        g.dma_start(out=ybuf[ob * 128:(ob + 1) * 128, tb * 512:(tb + 1) * 512],
                                    in_=ystage[j % 2][:]).then_inc(ssl[j % 2].h, 16)
                        ssl[j % 2].n += 16
                        YG[ob, tb] = (ssl[j % 2], ssl[j % 2].n)
                for s_ in ssl:
                    g.wait_ge(s_.h, s_.n)
    return YG


def _emit_final(nc, ctx, ybuf, x3f, out_c, onescol, onesrow, identf,
                ygate=None, pre=None):
    MB = DC
    for hf in range(2):
        t0 = hf * 1024
        with ExitStack() as st:
            if pre is not None:
                rres, ysb, xsb = pre
            else:
                rres = st.enter_context(nc.sbuf_tensor(f"fnr{hf}", [128, 2 * MB, 512], F32))
                ysb = [st.enter_context(nc.sbuf_tensor(f"fny{hf}{i}", [128, 512], F32)) for i in range(2)]
                xsb = [st.enter_context(nc.sbuf_tensor(f"fnx{hf}{i}", [128, 512], F32)) for i in range(2)]
            rb16 = [st.enter_context(nc.sbuf_tensor(f"fnb{hf}{i}", [128, 512], BF16)) for i in range(2)]
            rq16 = [st.enter_context(nc.sbuf_tensor(f"fnq{hf}{i}", [128, 512], BF16)) for i in range(2)]
            tmp = [st.enter_context(nc.sbuf_tensor(f"fnt{hf}{i}", [128, 512], F32)) for i in range(2)]
            x4t = [st.enter_context(nc.sbuf_tensor(f"fn4{hf}{i}", [128, 512], F32)) for i in range(2)]
            outst = [st.enter_context(nc.sbuf_tensor(f"fno{hf}{i}", [128, 4, D], F32)) for i in range(2)]
            srow = st.enter_context(nc.sbuf_tensor(f"fns{hf}", [1, 16, 512], F32))
            pgen = [st.enter_context(nc.psum_tensor(f"fnp{hf}{i}", [128, 4, 128], F32)) for i in range(4)]
            pbc = [st.enter_context(nc.psum_tensor(f"fnc{hf}{i}", [128, 512], F32)) for i in range(4)]

            ysl = [ctx.slot[6], ctx.slot[7]]
            xsl = [ctx.slot[8], ctx.slot[9]]
            ostsl = [ctx.slot[u] for u in range(4)]
            yb_ = [s.n for s in ysl]
            xb_ = [s.n for s in xsl]
            ost_ = [s.n for s in ostsl]
            gp0 = ctx.gp.n

            S = {}
            mmv, scv, evv = ctx.mm.n, ctx.sc.n, ctx.ev.n
            for j in range(2 * MB):
                evv += 1; S["ev_add", j] = evv
                scv += 1; S["sc_sq", j] = scv
                mmv += 2; S["mm_st", j] = mmv
            for tb in range(2):
                evv += 6; S["ev_fin", tb] = evv
                scv += 2
                mmv += 2; S["mm_bc", tb] = mmv
            for tb in range(2):
                for ob in range(MB):
                    evv += 2; S["ev_ap", tb, ob] = evv
                    mmv += 4; S["mm_tr", tb, ob] = mmv
                    scv += 1; S["sc_ev", tb, ob] = scv
            mm_end, sc_end, ev_end = mmv, scv, evv

            def step(i):
                return (i // MB, i % MB)

            with nc.Block(f"FN{hf}") as blk:
                @blk.sync
                def _(sy):
                    for j in range(2 * MB):
                        ob, tb = j // 2, j % 2
                        if ygate is not None and hf == 0:
                            gs, gv = ygate[ob, hf * 2 + tb]
                            sy.wait_ge(gs.h, gv)
                        if j >= 2:
                            sy.wait_ge(ctx.ev.h, S["ev_add", j - 2])
                        c0, c1 = t0 + tb * 512, t0 + (tb + 1) * 512
                        sy.dma_start(out=ysb[j % 2][:], in_=ybuf[ob * 128:(ob + 1) * 128, c0:c1]).then_inc(ysl[j % 2].h, 16)
                        ysl[j % 2].n += 16
                        sy.dma_start(out=xsb[j % 2][:], in_=x3f[ob * 128:(ob + 1) * 128, c0:c1]).then_inc(xsl[j % 2].h, 16)
                        xsl[j % 2].n += 16

                @blk.gpsimd
                def _(g):
                    for j in range(2 * MB):
                        g.wait_ge(ctx.ev.h, S["ev_add", j])
                        if j >= 2:
                            g.wait_ge(ctx.mm.h, S["mm_st", j - 2])
                        g.tensor_copy(rb16[j % 2][:], rres[:, j, :]).then_inc(ctx.gp.h, 1)
                    ctx.gp.n += 2 * MB
                    for tb in range(2):
                        for u in range(4):
                            tc = tb * 4 + u
                            g.wait_ge(ctx.sc.h, S["sc_ev", tb, MB - 1])
                            g.dma_start(out=out_c[t0 + tc * 128:t0 + (tc + 1) * 128, :],
                                        in_=outst[tb][:, u, :]).then_inc(ostsl[u].h, 16)
                            ostsl[u].n += 16
                    for s_ in ostsl:
                        g.wait_ge(s_.h, s_.n)

                @blk.tensor
                def _(t):
                    for j in range(2 * MB):
                        ob, tb = j // 2, j % 2
                        t.wait_ge(ctx.gp.h, gp0 + j + 1)
                        t.wait_ge(ctx.sc.h, S["sc_sq", j])
                        t.matmul(pbc[tb][0:1, :], onescol[:], rb16[j % 2][:],
                                 start=(ob == 0), stop=(ob == MB - 1))
                        t.matmul(pbc[2 + tb][0:1, :], onescol[:], rq16[j % 2][:],
                                 start=(ob == 0), stop=(ob == MB - 1)).then_inc(ctx.mm.h, 2)
                    for tb in range(2):
                        t.wait_ge(ctx.ev.h, S["ev_fin", tb])
                        t.matmul(pbc[tb][:], onesrow[:], srow[:, 2 + 8 * tb, :], start=True, stop=True)
                        t.matmul(pbc[2 + tb][:], onesrow[:], srow[:, 6 + 8 * tb, :],
                                 start=True, stop=True).then_inc(ctx.mm.h, 2)
                    for i in range(2 * MB):
                        tb, ob = step(i)
                        t.wait_ge(ctx.ev.h, S["ev_ap", tb, ob])
                        if i >= 4:
                            t.wait_ge(ctx.sc.h, S["sc_ev", step(i - 4)[0], step(i - 4)[1]])
                        for u in range(4):
                            ins = t.transpose(pgen[i % 4][:, u, :],
                                              x4t[i % 2][:, u * 128:(u + 1) * 128], identf[:])
                        ins.then_inc(ctx.mm.h, 4)
                    ctx.mm.n = mm_end

                @blk.scalar
                def _(s):
                    for j in range(2 * MB):
                        s.wait_ge(ctx.ev.h, S["ev_add", j])
                        if j >= 2:
                            s.wait_ge(ctx.mm.h, S["mm_st", j - 2])
                        s.activation(rq16[j % 2][:], rres[:, j, :], AF.Square).then_inc(ctx.sc.h, 1)
                    for tb in range(2):
                        base = 8 * tb
                        s.wait_ge(ctx.ev.h, S["ev_fin", tb] - 3)
                        s.activation(srow[:, 3 + base, :], srow[:, 2 + base, :],
                                     AF.Square).then_inc(ctx.sc.h, 1)
                        s.wait_ge(ctx.ev.h, S["ev_fin", tb] - 1)
                        s.activation(srow[:, 5 + base, :], srow[:, 4 + base, :],
                                     AF.Sqrt).then_inc(ctx.sc.h, 1)
                    for i in range(2 * MB):
                        tb, ob = step(i)
                        s.wait_ge(ctx.mm.h, S["mm_tr", tb, ob])
                        s.activation(outst[tb][:, :, ob * 128:(ob + 1) * 128],
                                     pgen[i % 4][:], AF.Identity).then_inc(ctx.sc.h, 1)
                    ctx.sc.n = sc_end

                @blk.vector
                def _(v):
                    myev = ctx.ev.n
                    def vop(ins):
                        nonlocal myev
                        ins.then_inc(ctx.ev.h, 1)
                        myev += 1
                    for j in range(2 * MB):
                        v.wait_ge(ysl[j % 2].h, yb_[j % 2] + 16 * (j // 2 + 1))
                        v.wait_ge(xsl[j % 2].h, xb_[j % 2] + 16 * (j // 2 + 1))
                        vop(v.tensor_add(rres[:, j, :], ysb[j % 2][:], xsb[j % 2][:]))
                    for tb in range(2):
                        base = 8 * tb
                        v.wait_ge(ctx.mm.h, S["mm_st", 2 * MB - 1])
                        vop(v.tensor_copy(srow[:, 0 + base, :], pbc[tb][0:1, :]))
                        vop(v.tensor_copy(srow[:, 1 + base, :], pbc[2 + tb][0:1, :]))
                        v.wait_ge(ctx.ev.h, myev)
                        vop(v.tensor_scalar_mul(srow[:, 2 + base, :], srow[:, 0 + base, :], 1.0 / D))
                        v.wait_ge(ctx.sc.h, S["sc_sq", 2 * MB - 1] + 2 * tb + 1)
                        v.wait_ge(ctx.ev.h, myev)
                        vop(v.scalar_tensor_tensor(srow[:, 4 + base, :], srow[:, 1 + base, :],
                                                   1.0 / D, srow[:, 3 + base, :],
                                                   ALU.mult, ALU.subtract))
                        v.wait_ge(ctx.ev.h, myev)
                        vop(v.tensor_scalar_add(srow[:, 4 + base, :], srow[:, 4 + base, :], float(EPS)))
                        v.wait_ge(ctx.sc.h, S["sc_sq", 2 * MB - 1] + 2 * tb + 2)
                        vop(v.reciprocal(srow[:, 6 + base, :], srow[:, 5 + base, :]))
                    for i in range(2 * MB):
                        tb, ob = step(i)
                        v.wait_ge(ctx.mm.h, S["mm_bc", tb])
                        v.wait_ge(ctx.ev.h, myev)
                        vop(v.tensor_sub(tmp[i % 2][:], rres[:, ob * 2 + tb, :], pbc[tb][:]))
                        if i >= 2:
                            v.wait_ge(ctx.mm.h, S["mm_tr", step(i - 2)[0], step(i - 2)[1]])
                        v.wait_ge(ctx.ev.h, myev)
                        vop(v.tensor_mul(x4t[i % 2][:], tmp[i % 2][:], pbc[2 + tb][:]))
                    ctx.ev.n = ev_end
                    assert myev == ev_end, (myev, ev_end)
        _bar(nc)



RG_BATCH = [[0, 1, 2, 3], [4, 5, 6, 7]]


def _emit_ag(nc, ctx, name, src_ap, dst_ap):
    with nc.Block(name) as blk:
        @blk.gpsimd
        def _(g):
            g.collective_compute(
                "AllGather", ALU.bypass, replica_groups=RG_BATCH,
                ins=[src_ap.opt()], outs=[dst_ap.opt()],
            ).then_inc(ctx.cc.h)
            ctx.cc.n += 1
            g.wait_ge(ctx.cc.h, ctx.cc.n)
    _bar(nc)


def build_fused():
    from concourse.ap import AP as _AP
    nc = bass.Bass("TRN2", target_bir_lowering=False, debug=False)
    xT = nc.dram_tensor("xT", [D, TPC], F32, kind="ExternalInput").ap()
    wqkvT = nc.dram_tensor("wqkTt", [32, 128, 16, 128], F32, kind="ExternalInput").ap()
    wvTt = nc.dram_tensor("wvTt", [4, 128, 16, 512], F32, kind="ExternalInput").ap()
    bqkv_pc = nc.dram_tensor("bqkv_pc", [128, 48], F32, kind="ExternalInput").ap()
    bqkv_row = nc.dram_tensor("bqkv_row", [1, 3 * D], F32, kind="ExternalInput").ap()
    woutT = nc.dram_tensor("woutTt", [16, 128, 16, 128], F32, kind="ExternalInput").ap()
    bout_pc = nc.dram_tensor("bout_pc", [128, 16], F32, kind="ExternalInput").ap()
    W1T = nc.dram_tensor("W1Tt", [32, 128, 16, 128], F32, kind="ExternalInput").ap()
    b1_pc = nc.dram_tensor("b1_pc", [128, 32], F32, kind="ExternalInput").ap()
    W2T = nc.dram_tensor("W2Tt", [32, 128, 32, 128], F32, kind="ExternalInput").ap()
    b2_pc = nc.dram_tensor("b2_pc", [128, 32], F32, kind="ExternalInput").ap()
    W3T = nc.dram_tensor("W3Tt", [16, 128, 32, 128], F32, kind="ExternalInput").ap()
    b3_pc = nc.dram_tensor("b3_pc", [128, 16], F32, kind="ExternalInput").ap()
    coeff = nc.dram_tensor("coeff", [128, 64], F32, kind="ExternalInput").ap()
    wqT = nc.dram_tensor("wqTt", [4, 128, 16, 128], F32, kind="ExternalInput").ap()
    wkT = nc.dram_tensor("wkTt", [4, 128, 16, 128], F32, kind="ExternalInput").ap()
    wvT = nc.dram_tensor("wvTt1", [1, 128, 16, 512], F32, kind="ExternalInput").ap()
    woT = nc.dram_tensor("woTt", [16, 128, 16, 128], F32, kind="ExternalInput").ap()
    acat = nc.dram_tensor("acatt", [128, 16, 512], F32, kind="ExternalInput").ap()
    wgT = nc.dram_tensor("wgTt", [128, 16, 8], F32, kind="ExternalInput").ap()
    bwT = nc.dram_tensor("bwTt", [128, 4, 2048], F32, kind="ExternalInput").ap()

    out_c = nc.dram_tensor("out_c", [TPC, D], F32, kind="ExternalOutput").ap()

    qT0 = nc.dram_tensor("qT0", [D, TPC], BF16, kind="Internal").ap()
    kT0 = nc.dram_tensor("kT0", [D, TPC], BF16, kind="Internal").ap()
    v0b = nc.dram_tensor("v0h", [H, 128, NCH, 128], BF16, kind="Internal").ap()
    U0 = nc.dram_tensor("U0", [NCH, H, 128, 128], F32, kind="Internal").ap()
    T0 = nc.dram_tensor("T0", [H * 128, 128], F32, kind="Internal").ap()
    T_all = nc.dram_tensor("T_all", [4 * H * 128, 128], F32, kind="Internal").ap()
    oT0 = nc.dram_tensor("oT0", [D, TPC], BF16, kind="Internal").ap()
    x1f = nc.dram_tensor("x1f", [D, TPC], F32, kind="Internal").ap()
    x1b = nc.dram_tensor("x1b", [D, TPC], BF16, kind="Internal").ap()
    rbuf = nc.dram_tensor("rbuf", [D, TPC], F32, kind="Internal").ap()
    x2f = nc.dram_tensor("x2f", [D, TPC], F32, kind="Internal").ap()
    x2bh = [nc.dram_tensor(f"x2bh{i}", [D, 1024], BF16, kind="Internal").ap()
            for i in range(2)]
    x2allh = [nc.dram_tensor(f"x2allh{i}", [4 * D, 1024], BF16, kind="Internal").ap()
              for i in range(2)]
    qT1 = nc.dram_tensor("qT1", [512, N], BF16, kind="Internal").ap()
    kT1 = nc.dram_tensor("kT1", [512, N], BF16, kind="Internal").ap()
    v1 = nc.dram_tensor("v1h", [4, 128, N // 128, 128], BF16, kind="Internal").ap()
    oT1 = nc.dram_tensor("oT1r", [2048, TPC], BF16, kind="Internal").ap()
    oTall = nc.dram_tensor("oTall", [4 * 2048, TPC], BF16, kind="Internal").ap()
    oTm = nc.dram_tensor("oTm", [D, TPC], BF16, kind="Internal").ap()
    x3f = nc.dram_tensor("x3f", [D, TPC], F32, kind="Internal").ap()
    x3b = nc.dram_tensor("x3b", [D, TPC], BF16, kind="Internal").ap()
    ybuf = nc.dram_tensor("ybuf", [D, TPC], F32, kind="Internal").ap()

    ident_t = nc.inline_tensor(IDENT_B16, "c_ident16").ap()
    identf_t = nc.inline_tensor(IDENT_F32, "c_identf").ap()
    onesrow_t = nc.inline_tensor(ONES_ROW_F32, "c_onesrow").ap()
    onescol_t = nc.inline_tensor(ONES_COL_B16, "c_onescol").ap()
    dkcol_t = nc.inline_tensor(DECAY_K_COL, "c_dkcol").ap()
    dmaskT_t = nc.inline_tensor(DMASKT_T, "c_dmaskT").ap()
    decq_t = nc.inline_tensor(np.ascontiguousarray(DECAY_Q_T), "c_decq").ap()
    diagm_t = nc.inline_tensor(L1_DIAG_MASKS, "c_diagm").ap()

    with ExitStack() as stack:
        ctx = Ctx(nc, stack)
        ctx.cc = Sem(nc, stack, "s_cc")
        onesrow = ctx.sb("onesrow", [1, 128], F32)
        onescol = ctx.sb("onescol", [128, 1], BF16)
        identb = ctx.sb("identb", [128, 128], BF16)
        identf = ctx.sb("identf", [128, 128], F32)
        bias16a = ctx.sb("bias16a", [128, 16], F32)
        bias32a = ctx.sb("bias32a", [128, 32], F32)
        bias32b = ctx.sb("bias32b", [128, 32], F32)
        bias16b = ctx.sb("bias16b", [128, 16], F32)

        with nc.Block("F0") as blk:
            @blk.gpsimd
            def _(g):
                for dst, src in [
                    (onesrow[:], onesrow_t), (onescol[:], onescol_t),
                    (identb[:], ident_t), (identf[:], identf_t),
                    (bias16a[:], bout_pc), (bias32a[:], b1_pc),
                    (bias32b[:], b2_pc), (bias16b[:], b3_pc),
                ]:
                    g.dma_start(out=dst, in_=src).then_inc(ctx.ld_g.h, 16)
                    ctx.ld_g.n += 16
                g.wait_ge(ctx.ld_g.h, ctx.ld_g.n)
        _bar(nc)

        with ExitStack() as stAB:
            dkcol = stAB.enter_context(nc.sbuf_tensor("dkcol", [128, H], F32))
            dmaskT = stAB.enter_context(nc.sbuf_tensor("dmaskT", [128, H, 128], F32))
            decq = stAB.enter_context(nc.sbuf_tensor("decq", [128, H, 128], BF16))
            coeff_sb = stAB.enter_context(nc.sbuf_tensor("coeff_sb", [128, 64], F32))
            sinit = stAB.enter_context(nc.sbuf_tensor("sinit", [128, H, 128], F32))
            with nc.Block("F0b") as blk:
                @blk.gpsimd
                def _(g):
                    for dst, src in [
                        (dkcol[:], dkcol_t),
                        (dmaskT[:], dmaskT_t.rearrange("h p e -> p h e")),
                        (decq[:], decq_t.rearrange("h p e -> p h e")),
                        (coeff_sb[:], coeff),
                    ]:
                        g.dma_start(out=dst, in_=src).then_inc(ctx.ld_g.h, 16)
                        ctx.ld_g.n += 16
                    g.wait_ge(ctx.ld_g.h, ctx.ld_g.n)
            _bar(nc)
            with ExitStack() as stA:
                xb = stA.enter_context(nc.sbuf_tensor("xb", [128, DC, TPC], BF16))
                biasqkv = stA.enter_context(nc.sbuf_tensor("biasqkv", [128, 48], F32))
                biasr = stA.enter_context(nc.sbuf_tensor("biasr", [1, 3 * D], F32))
                with nc.Block("A0") as blk:
                    @blk.gpsimd
                    def _(g):
                        for dst, src in [(xb[:], xT.rearrange("(c p) t -> p c t", p=128)),
                                         (biasqkv[:], bqkv_pc), (biasr[:], bqkv_row)]:
                            g.dma_start(out=dst, in_=src).then_inc(ctx.ld_g.h, 16)
                            ctx.ld_g.n += 16
                        g.wait_ge(ctx.ld_g.h, ctx.ld_g.n)
                _bar(nc)
                _emit_qk_fm(nc, ctx, xb, wqkvT, biasqkv, qT0, kT0)
                _bar(nc)
                _emit_v_tm(nc, ctx, xb, wvTt, biasr, onesrow, v0b)
                _bar(nc)
            _emit_UT(nc, ctx, kT0, v0b, dkcol, identb, U0, T0)
            _bar(nc)

            _emit_ag(nc, ctx, "AG1", T0, T_all)

            _emit_sinit(nc, ctx, T_all, coeff_sb, sinit)
            _bar(nc)
            _emit_scan(nc, ctx, qT0, kT0, v0b, U0, sinit, dmaskT, decq, oT0)
            _bar(nc)
        _emit_proj_ln2(nc, ctx, "B3", rhs_hbm=oT0, K_=D, w_hbm=woutT,
                       bias_col=bias16a, xold_hbm=xT,
                       xnew_f=x1f, xnew_b=x1b, onescol=onescol, onesrow=onesrow)
        cc_ag2h = [None, None]

        def _ag2_half(hf):
            with nc.Block(f"AG2i{hf}") as blk:
                @blk.gpsimd
                def _(g):
                    g.collective_compute(
                        "AllGather", ALU.bypass, replica_groups=RG_BATCH,
                        ins=[x2bh[hf].opt()], outs=[x2allh[hf].opt()],
                    ).then_inc(ctx.cc.h)
                    ctx.cc.n += 1
            cc_ag2h[hf] = ctx.cc.n

        _emit_ffn(nc, ctx, x1b, x1f, W1T, bias32a, W2T, bias32b, W3T, bias16b,
                  rbuf, x2f, None, onescol, onesrow, x2b_halves=x2bh,
                  post_half=_ag2_half)

        diagm = stack.enter_context(nc.sbuf_tensor("diagm", [128, 4, 512], BF16))
        with nc.Block("F0c") as blk:
            @blk.gpsimd
            def _(g):
                g.dma_start(out=diagm[:], in_=diagm_t.rearrange("j p t -> p j t")).then_inc(ctx.ld_g.h, 16)
                ctx.ld_g.n += 16
                g.wait_ge(ctx.ld_g.h, ctx.ld_g.n)
        _emit_c1_all(nc, ctx, x2allh, wqT, wkT, wvT, qT1, kT1, v1,
                     cc_gate=tuple(cc_ag2h))
        _emit_l1_attn(nc, ctx, qT1, kT1, v1, oT1, onesrow, onescol, diagm)
        _bar(nc)

        with nc.Block("AG3i") as blk:
            @blk.gpsimd
            def _(g):
                g.collective_compute(
                    "AllGather", ALU.bypass, replica_groups=RG_BATCH,
                    ins=[oT1.opt()], outs=[oTall.opt()],
                ).then_inc(ctx.cc.h)
                ctx.cc.n += 1
        cc_ag3 = ctx.cc.n
        with nc.Block("RSH") as blk:
            @blk.sync
            def _(sy):
                sy.wait_ge(ctx.cc.h, cc_ag3)
                pid = nc.partition_id()
                myr = pid % 4
                for gg in range(4):
                    src = _AP(oTall.tensor, (gg * 2048 + myr * 512) * TPC,
                              [[TPC, 512], [1, TPC]])
                    sy.dma_start(out=oTm[gg * 512:(gg + 1) * 512, :], in_=src).then_inc(ctx.ld_s.h, 16)
                    ctx.ld_s.n += 16
        rsh_gate = ctx.ld_s.n

        _emit_proj_ln2(nc, ctx, "D1", rhs_hbm=oTm, K_=D, w_hbm=woT,
                       bias_col=None, xold_hbm=x2f,
                       xnew_f=x3f, xnew_b=x3b, onescol=onescol, onesrow=onesrow,
                       rh_gate=rsh_gate)
        fn_rres = stack.enter_context(nc.sbuf_tensor("fnrr", [128, 2 * DC, 512], F32))
        fn_ysb = [stack.enter_context(nc.sbuf_tensor(f"fnyy{i}", [128, 512], F32)) for i in range(2)]
        fn_xsb = [stack.enter_context(nc.sbuf_tensor(f"fnxx{i}", [128, 512], F32)) for i in range(2)]
        yg = _emit_moe(nc, ctx, x3b, acat, wgT, bwT, ybuf, identb)
        _emit_final(nc, ctx, ybuf, x3f, out_c, onescol, onesrow, identf,
                    ygate=yg, pre=(fn_rres, fn_ysb, fn_xsb))
        _bar(nc)
    return nc


def _prep_fused(inputs, core):
    d = _prep_A(inputs, core)
    r = core % 4
    cs = slice(r * 512, (r + 1) * 512)
    d.update({
        "woutTt": _tile_w(inputs["L0_attn_w_out"].T),
        "bout_pc": np.ascontiguousarray(inputs["L0_attn_b_out"].reshape(16, 128).T),
        "W1Tt": _tile_w(inputs["L0_ffn_W1"].T),
        "b1_pc": np.ascontiguousarray(inputs["L0_ffn_b1"].reshape(32, 128).T),
        "W2Tt": _tile_w(inputs["L0_ffn_W2"].T),
        "b2_pc": np.ascontiguousarray(inputs["L0_ffn_b2"].reshape(32, 128).T),
        "W3Tt": _tile_w(inputs["L0_ffn_W3"].T),
        "b3_pc": np.ascontiguousarray(inputs["L0_ffn_b3"].reshape(16, 128).T),
        "coeff": _sinit_coeff_table(core % 4),
        "wqTt": _tile_w(inputs["L1_attn_Wq"].T[:, cs], 128),
        "wkTt": _tile_w(inputs["L1_attn_Wk"].T[:, cs], 128),
        "wvTt1": _tile_w(inputs["L1_attn_Wv"].T[:, cs], 512),
        "woTt": _tile_w(inputs["L1_attn_Wo"].T),
        "acatt": _tile_w(np.concatenate([inputs["L1_ffn_A"][e].T for e in range(8)], axis=1), 512)[0],
        "wgTt": _tile_w(inputs["L1_ffn_Wg"].T, 8)[0],
        "bwTt": _tile_w(np.concatenate([inputs["L1_ffn_Bw"][e].T for e in range(8)], axis=0), 2048)[0],
    })
    return d


def _tile_w(WT, mblk=128):
    K_, M = WT.shape
    return np.ascontiguousarray(
        WT.reshape(K_ // 128, 128, M // mblk, mblk).transpose(2, 1, 0, 3))


def _prep_A(inputs, core):
    xT = np.ascontiguousarray(
        inputs["L0_attn_x"].reshape(NTOK, D).T[:, core * TPC:(core + 1) * TPC]
    )
    wqkvT = inputs["L0_attn_w_qkv"].T
    b = inputs["L0_attn_b_qkv"]
    return {
        "xT": xT,
        "wqkTt": _tile_w(wqkvT[:, :2 * D], 128),
        "wvTt": _tile_w(wqkvT[:, 2 * D:], 512),
        "bqkv_pc": np.ascontiguousarray(b.reshape(48, 128).T),
        "bqkv_row": np.ascontiguousarray(b.reshape(1, 3 * D)),
    }


def _prep_B(inputs, core, aouts):
    grp = range(0, 4) if core < 4 else range(4, 8)
    T_all = np.concatenate([np.asarray(aouts[r]["T0"]) for r in grp], axis=0)
    d = {
        "xT": np.ascontiguousarray(
            inputs["L0_attn_x"].reshape(NTOK, D).T[:, core * TPC:(core + 1) * TPC]),
        "T_all": T_all,
        "qT0": np.asarray(aouts[core]["qT0"]),
        "kT0": np.asarray(aouts[core]["kT0"]),
        "v0h": np.asarray(aouts[core]["v0h"]),
        "U0": np.asarray(aouts[core]["U0"]),
        "woutTt": _tile_w(inputs["L0_attn_w_out"].T),
        "bout_pc": np.ascontiguousarray(inputs["L0_attn_b_out"].reshape(16, 128).T),
        "W1Tt": _tile_w(inputs["L0_ffn_W1"].T),
        "b1_pc": np.ascontiguousarray(inputs["L0_ffn_b1"].reshape(32, 128).T),
        "W2Tt": _tile_w(inputs["L0_ffn_W2"].T),
        "b2_pc": np.ascontiguousarray(inputs["L0_ffn_b2"].reshape(32, 128).T),
        "W3Tt": _tile_w(inputs["L0_ffn_W3"].T),
        "b3_pc": np.ascontiguousarray(inputs["L0_ffn_b3"].reshape(16, 128).T),
        "coeff": _sinit_coeff_table(core % 4),
    }
    return d


def _prep_C(inputs, core, bouts):
    grp = range(0, 4) if core < 4 else range(4, 8)
    r = core % 4
    cs = slice(r * 512, (r + 1) * 512)
    return {
        "x2all": np.concatenate([np.asarray(bouts[g]["x2b"]) for g in grp], axis=0),
        "wqTt": _tile_w(inputs["L1_attn_Wq"].T[:, cs], 128),
        "wkTt": _tile_w(inputs["L1_attn_Wk"].T[:, cs], 128),
        "wvTt1": _tile_w(inputs["L1_attn_Wv"].T[:, cs], 512),
    }


def _prep_D(inputs, core, bouts, couts):
    grp = list(range(0, 4)) if core < 4 else list(range(4, 8))
    r = core % 4
    oTm = np.concatenate(
        [np.asarray(couts[g]["oT1r"])[r * 512:(r + 1) * 512, :] for g in grp], axis=0)
    return {
        "oTm": np.ascontiguousarray(oTm),
        "x2f": np.asarray(bouts[core]["x2f"]),
        "woTt": _tile_w(inputs["L1_attn_Wo"].T),
        "acatt": _tile_w(np.concatenate([inputs["L1_ffn_A"][e].T for e in range(8)], axis=1), 512)[0],
        "wgTt": _tile_w(inputs["L1_ffn_Wg"].T, 8)[0],
        "bwTt": _tile_w(np.concatenate([inputs["L1_ffn_Bw"][e].T for e in range(8)], axis=0), 2048)[0],
    }


_NC_CACHE = {}


def _get_nc(name, builder):
    if name not in _NC_CACHE:
        _NC_CACHE[name] = builder()
    return _NC_CACHE[name]


def kernel(**inputs):
    import os
    if os.environ.get("MK_UNFUSED", "0") != "1":
        inputs = {k: np.asarray(v) for k, v in inputs.items()}
        cores = list(range(8))
        ncF = _get_nc("F", build_fused)
        res = run_bass_kernel_spmd(ncF, [_prep_fused(inputs, c) for c in cores], core_ids=cores)
        out = np.empty((NTOK, D), dtype=np.float32)
        for c in cores:
            out[c * TPC:(c + 1) * TPC] = np.asarray(res.results[c]["out_c"])
        return out.reshape(B, N, D)
    return _kernel_unfused(**inputs)


def _kernel_unfused(**inputs):
    inputs = {k: np.asarray(v) for k, v in inputs.items()}
    cores = list(range(8))

    ncA = _get_nc("A", build_launch_A)
    resA = run_bass_kernel_spmd(ncA, [_prep_A(inputs, c) for c in cores], core_ids=cores)
    aouts = resA.results

    ncB = _get_nc("B", build_launch_B)
    resB = run_bass_kernel_spmd(ncB, [_prep_B(inputs, c, aouts) for c in cores], core_ids=cores)
    bouts = resB.results

    ncC = _get_nc("C", build_launch_C)
    resC = run_bass_kernel_spmd(ncC, [_prep_C(inputs, c, bouts) for c in cores], core_ids=cores)
    couts = resC.results

    ncD = _get_nc("D", build_launch_D)
    resD = run_bass_kernel_spmd(ncD, [_prep_D(inputs, c, bouts, couts) for c in cores], core_ids=cores)

    out = np.empty((NTOK, D), dtype=np.float32)
    for c in cores:
        out[c * TPC:(c + 1) * TPC] = np.asarray(resD.results[c]["out_c"])
    return out.reshape(B, N, D)
